# version 13
# baseline (speedup 1.0000x reference)
import numpy as np
import ml_dtypes
import concourse.bass as bass
import concourse.mybir as mybir
from concourse.bass_utils import run_bass_kernel_spmd
from contextlib import ExitStack

F32 = mybir.dt.float32
BF16 = mybir.dt.bfloat16
FP16 = mybir.dt.float16
FP8 = mybir.dt.float8e4
AF = mybir.ActivationFunctionType
ALU = mybir.AluOpType
AX = mybir.AxisListType

T = 2048
D = 1024
TC = T // 128
DI = 2048
E4 = 4096
I1 = 2816
IC = I1 // 128

C_EPS, C_I1024, C_HALF, C_QUART, C_ONE = 0, 1, 2, 3, 4


class Sem:
    def __init__(self, nc, es, name):
        self.h = es.enter_context(nc.semaphore(name))
        self.n = 0

    def inc(self, instr, k=1):
        instr.then_inc(self.h, k)
        self.n += k
        return instr


class Builder:
    def __init__(self, n_stages=6, wq_force=False):
        self.n_stages = n_stages
        self.wq_on = False
        self.wq_force = wq_force
        self.uid = 0

    def name(self, tag):
        self.uid += 1
        return f"{tag}_{self.uid}"

    def build(self):
        nc = bass.Bass("TRN2", target_bir_lowering=False)
        self.nc = nc
        dram = lambda name, shape, dt, kind: nc.dram_tensor(name, shape, dt, kind=kind).ap()
        IN, OUT, SC = "ExternalInput", "ExternalOutput", "Internal"

        self.x_in = dram("x", [T, D], F32, IN)
        self.winT = dram("winT", [D, E4], F32, IN)
        self.inb = dram("inb", [128, 32], F32, IN)
        self.convw = dram("convw", [128, 64], F32, IN)
        self.convb = dram("convb", [128, 16], F32, IN)
        self.dtwT = dram("dtwT", [DI, DI], F32, IN)
        self.bwT = dram("bwT", [DI, DI], F32, IN)
        self.cwT = dram("cwT", [DI, DI], F32, IN)
        self.dtb = dram("dtb", [128, 16], F32, IN)
        self.alog = dram("alog", [128, 16], F32, IN)
        self.dpar = dram("dpar", [128, 16], F32, IN)
        self.outwT = dram("outwT", [DI, D], F32, IN)
        self.outb_rep = dram("outb_rep", [128, D], F32, IN)
        self.kern_in = dram("kern", [D, D], BF16, IN)
        self.sin_c = dram("sin_c", [128, 1], F32, IN)
        self.skl_c = dram("skl_c", [128, 1], F32, IN)
        self.fbias_rep = dram("fbias_rep", [128, D], BF16, IN)
        self.qT_in = dram("qT", [D, D], F32, IN)
        self.wg_in = dram("wg", [I1, D], F32, IN)
        self.wu_in = dram("wu", [I1, D], F32, IN)
        self.wd_in = dram("wd", [D, I1], F32, IN)
        self.consts = dram("consts", [128, 8], F32, IN)
        self.ident_in = dram("ident", [128, 128], FP16, IN)

        self.out = dram("out", [T, D], F32, OUT)

        self.xd = dram("xd", [T, D], F32, SC)
        self.stg_td = dram("stg_td", [T, D], BF16, SC)
        self.stg_td16 = dram("stg_td16", [T, D], FP16, SC)
        self.stg_ti = dram("stg_ti", [T, I1], BF16, SC)
        self.stg_y2 = dram("stg_y2", [DI, T], FP16, SC)
        self.stg_tz = dram("stg_tz", [DI, T], FP16, SC)
        self.stg_b = dram("stg_b", [DI, T], FP16, SC)
        self.stg_c = dram("stg_c", [DI, T], FP16, SC)
        self.wgq_d = dram("wgq_d", [I1, D], FP16, SC)
        self.wuq_d = dram("wuq_d", [I1, D], FP16, SC)
        self.wdq_d = dram("wdq_d", [D, I1], FP16, SC)
        self.hq_d = dram("hq_d", [T, I1], FP16, SC)

        es = ExitStack()
        self.es = es
        with es:
            self._alloc_persistent()
            self.block = es.enter_context(nc.Block())
            with nc.named_scope("p0"):
                self._emit_p0()
            if self.n_stages >= 1:
                with nc.named_scope("ffn8a"):
                    self._emit_ffn8()
            if self.n_stages >= 2:
                self.wq_on = self.n_stages >= 4 or self.wq_force
                with nc.named_scope("mixer"):
                    self._emit_mixer()
            if self.n_stages >= 3:
                with nc.named_scope("ffn8b"):
                    self._emit_ffn8()
            if self.n_stages >= 4:
                with nc.named_scope("ffn4a"):
                    self._emit_ffn4()
            if self.n_stages >= 5:
                with nc.named_scope("attn"):
                    self._emit_attn()
            if self.n_stages >= 6:
                with nc.named_scope("ffn4b"):
                    self._emit_ffn4()
            with nc.named_scope("fin"):
                self._emit_fin()
        return nc

    def _alloc_persistent(self):
        nc, es = self.nc, self.es
        sb = lambda nm, shape, dt: es.enter_context(nc.sbuf_tensor(nm, shape, dt))
        self.cst = sb("cst", [128, 8], F32)
        self.inb_t = sb("inb_t", [128, 32], F32)
        self.convw_t = sb("convw_t", [128, 64], F32)
        self.convb_t = sb("convb_t", [128, 16], F32)
        self.dtb_t = sb("dtb_t", [128, 16], F32)
        self.alog_t = sb("alog_t", [128, 16], F32)
        self.negA = sb("negA", [128, 16], F32)
        self.nb2 = sb("nb2", [128, 16], F32)
        self.hbz = sb("hbz", [128, 16], F32)
        self.dpar_t = sb("dpar_t", [128, 16], F32)
        self.outb_t = sb("outb_t", [128, D], F32)
        self.fbias_t = sb("fbias_t", [128, D], BF16)
        self.sin_t = sb("sin_t", [128, 1], F32)
        self.skl_t = sb("skl_t", [128, 1], F32)
        self.rs_in = sb("rs_in", [128, 1], F32)
        self.rs_k = sb("rs_k", [128, 1], F32)
        self.sprod = sb("sprod", [128, 1], F32)
        self.stat = sb("stat", [128, 96], F32)
        self.tmp16 = sb("tmp16", [128, 16], F32)
        self.ident = sb("ident_sb", [128, 128], FP16)
        self.pb = [es.enter_context(nc.psum_tensor(f"pb{i}", [128, 512], F32))
                   for i in range(8)]
        self.sems = {n: Sem(nc, es, n) for n in
                     ["ld", "v", "s", "t", "st", "ph", "fin", "g"]}

    def barrier(self, extra=()):
        sph = self.sems["ph"]
        waits = [(self.sems[k].h, self.sems[k].n) for k in ["ld", "v", "s", "t", "st", "g"]]
        waits += [(s.h, s.n) for s in extra]
        for eng in ["vector", "scalar", "tensor", "sync", "gpsimd"]:
            def body(e):
                for h, n in waits:
                    e.wait_ge(h, n)
                sph.inc(e.nop())
            getattr(self.block, eng)(body)
        n = sph.n
        for eng in ["vector", "scalar", "tensor", "sync", "gpsimd"]:
            def body2(e):
                e.wait_ge(sph.h, n)
            getattr(self.block, eng)(body2)

    def _emit_p0(self):
        nc, block = self.nc, self.block
        ld, sv, ss = self.sems["ld"], self.sems["v"], self.sems["s"]

        def p0_sync(sync):
            for dst, src in [(self.cst, self.consts), (self.inb_t, self.inb),
                             (self.convw_t, self.convw), (self.convb_t, self.convb),
                             (self.dtb_t, self.dtb), (self.alog_t, self.alog),
                             (self.dpar_t, self.dpar), (self.outb_t, self.outb_rep),
                             (self.fbias_t, self.fbias_rep), (self.sin_t, self.sin_c),
                             (self.skl_t, self.skl_c)]:
                ld.inc(sync.dma_start(out=dst[:], in_=src), 16)
            ld.inc(sync.dma_start(out=self.ident[:], in_=self.ident_in), 16)
            ld.inc(sync.dma_start(out=self.xd, in_=self.x_in), 16)
            sync.wait_ge(ld.h, ld.n)
        block.sync(p0_sync)
        n_ld = ld.n

        def p0_scalar(s):
            s.wait_ge(ld.h, n_ld)
            ss.inc(s.activation(self.tmp16[:], self.alog_t[:], AF.Exp))
        block.scalar(p0_scalar)
        n_s = ss.n

        def p0_vector(v):
            v.wait_ge(ld.h, n_ld)
            v.reciprocal(self.rs_in[:], self.sin_t[:])
            v.reciprocal(self.rs_k[:], self.skl_t[:])
            v.tensor_tensor(self.sprod[:], self.sin_t[:], self.skl_t[:], ALU.mult)
            v.tensor_scalar(self.hbz[:], self.inb_t[:, 16:32], 0.5, None, ALU.mult)
            v.wait_ge(ss.h, n_s)
            v.tensor_scalar(self.nb2[:], self.tmp16[:], -1e-4, None, ALU.mult)
            sv.inc(v.tensor_scalar(self.negA[:], self.tmp16[:], -1.0, None, ALU.mult))
            v.wait_ge(sv.h, sv.n)
        block.vector(p0_vector)
        self.n_consts_ready = sv.n
        self.barrier()

    def _emit_ln(self, xsb, xn_out):
        block = self.block
        sv, ss = self.sems["v"], self.sems["s"]
        stat, cst = self.stat, self.cst

        def vec1(v):
            for c in range(TC):
                xc = xsb[:, c * D:(c + 1) * D]
                v.tensor_reduce(stat[:, c:c + 1], xc, AX.X, ALU.add)
                v.scalar_tensor_tensor(xn_out[:, c * D:(c + 1) * D], xc, 1.0, xc,
                                       ALU.mult, ALU.mult,
                                       accum_out=stat[:, 16 + c:17 + c])
            v.drain()
            v.tensor_scalar(stat[:, 32:48], stat[:, 0:16], 1.0 / D, None, ALU.mult)
            v.drain()
            v.tensor_tensor(stat[:, 64:80], stat[:, 32:48], stat[:, 32:48], ALU.mult)
            v.drain()
            sv.inc(v.scalar_tensor_tensor(stat[:, 48:64], stat[:, 16:32], 1.0 / D,
                                          stat[:, 64:80], ALU.mult, ALU.subtract))
        block.vector(vec1)
        n_v1 = sv.n

        def sc1(s):
            s.wait_ge(sv.h, n_v1)
            ss.inc(s.activation(stat[:, 80:96], stat[:, 48:64], AF.Sqrt,
                                bias=cst[:, C_EPS:C_EPS + 1]))
        block.scalar(sc1)
        n_s1 = ss.n

        def vec2(v):
            v.wait_ge(ss.h, n_s1)
            sv.inc(v.reciprocal(stat[:, 48:64], stat[:, 80:96]))
            v.wait_ge(sv.h, sv.n)
            for c in range(TC):
                v.tensor_scalar(xn_out[:, c * D:(c + 1) * D], xsb[:, c * D:(c + 1) * D],
                                stat[:, 32 + c:33 + c], stat[:, 48 + c:49 + c],
                                ALU.subtract, ALU.mult)
            v.drain()
            sv.inc(v.nop())
        block.vector(vec2)
        return sv.n

    def _load_x(self, xsb):
        ld = self.sems["ld"]

        sst = self.sems["st"]

        n0 = ld.n

        def ldx(sync):
            sync.wait_ge(sst.h, sst.n)
            sync.wait_ge(ld.h, n0)
            ld.inc(sync.dma_start(
                out=xsb[:].rearrange("p (c d) -> p c d", c=TC),
                in_=self.xd.rearrange("(c p) d -> p c d", p=128)), 16)
        self.block.sync(ldx)
        n = ld.n

        def vwait(v):
            v.wait_ge(ld.h, n)
        self.block.vector(vwait)

    def _store_x(self, xsb, n_ready, to_out=False):
        sst = self.sems["st"]
        sv = self.sems["v"]
        dst = self.out if to_out else self.xd

        def stx(sync):
            sync.wait_ge(sv.h, n_ready)
            sst.inc(sync.dma_start(
                out=dst.rearrange("(c p) d -> p c d", p=128),
                in_=xsb[:].rearrange("p (c d) -> p c d", c=TC)), 16)
        self.block.sync(stx)
        return sst.n

    def _emit_ffn8(self):
        nc, block = self.nc, self.block
        ld, sv, ss, st_, sst = (self.sems[k] for k in ["ld", "v", "s", "t", "st"])
        ph = ExitStack()
        sbl = lambda nm, shape, dt: ph.enter_context(nc.sbuf_tensor(self.name(nm), shape, dt))
        xsb = sbl("x", [128, TC * D], F32)
        xnb = sbl("xn", [128, TC * D], BF16)
        q8 = sbl("q8", [128, TC * D], FP8)
        xq8T = sbl("xq8T", [128, 8 * T], BF16)
        ybuf = sbl("y", [128, 2 * D], BF16)
        y2buf = sbl("y2", [128, 2 * D], BF16)
        k8b = sbl("k8b", [128, 8 * D], BF16)
        kf8 = sbl("kf8", [128, 8 * D], FP8)
        pb = self.pb

        ldk0 = ld.n

        def k8_ld(sync):
            sync.wait_ge(ld.h, ldk0)
            ld.inc(sync.dma_start(
                out=k8b[:].rearrange("p (k d) -> p k d", k=8),
                in_=self.kern_in.rearrange("(k p) d -> p k d", p=128)), 16)
        block.sync(k8_ld)
        n_k8ld = ld.n

        self._load_x(xsb)
        self._emit_ln(xsb, xnb)

        def vq(v):
            v.drain()
            sv.inc(v.tensor_scalar(q8[:], xnb[:], self.rs_in[:, 0:1], None, ALU.mult))
        block.vector(vq)
        n_vq = sv.n

        def k8_v(v):
            v.wait_ge(ld.h, n_k8ld)
            v.drain()
            sv.inc(v.tensor_scalar(kf8[:], k8b[:], self.rs_k[:, 0:1], None, ALU.mult))
        block.vector(k8_v)
        n_k8v = sv.n

        def sq_(s):
            s.wait_ge(sv.h, n_vq)
            ss.inc(s.activation(xnb[:], q8[:], AF.Copy))
            s.wait_ge(sv.h, n_k8v)
            s.drain()
            ss.inc(s.activation(k8b[:], kf8[:], AF.Copy))
        block.scalar(sq_)
        n_sq = ss.n
        n_k8 = ss.n

        def stg(sync):
            sync.wait_ge(ss.h, n_sq)
            sst.inc(sync.dma_start(
                out=self.stg_td.rearrange("(c p) d -> p c d", p=128),
                in_=xnb[:].rearrange("p (c d) -> p c d", c=TC)), 16)
            sync.wait_ge(sst.h, sst.n)
            for k in range(8):
                sst.inc(sync.dma_start_transpose(
                    xq8T[:, k * T:(k + 1) * T], self.stg_td[:, k * 128:(k + 1) * 128]), 16)
        block.sync(stg)
        n_stg = sst.n

        evac_base = ss.n
        mm_base = st_.n


        def mm(t):
            t.wait_ge(sst.h, n_stg)
            t.wait_ge(ss.h, n_k8)
            for c in range(TC):
                if c >= 2:
                    t.wait_ge(ss.h, evac_base + (c - 2) * 2 + 2)
                pa, pbk = pb[(c % 2) * 2], pb[(c % 2) * 2 + 1]
                for k in range(8):
                    i1 = t.matmul(pa[:], xq8T[:, k * T + c * 128: k * T + (c + 1) * 128],
                                  k8b[:, k * D:k * D + 512],
                                  start=(k == 0), stop=(k == 7))
                st_.inc(i1)
                for k in range(8):
                    i2 = t.matmul(pbk[:], xq8T[:, k * T + c * 128: k * T + (c + 1) * 128],
                                  k8b[:, k * D + 512:(k + 1) * D],
                                  start=(k == 0), stop=(k == 7))
                st_.inc(i2)
        block.tensor(mm)

        vres_base = sv.n

        def ev(s):
            for c in range(TC):
                s.wait_ge(st_.h, mm_base + (c + 1) * 2)
                if c >= 2:
                    s.wait_ge(sv.h, vres_base + (c - 1))
                pa, pbk = pb[(c % 2) * 2], pb[(c % 2) * 2 + 1]
                yb = ybuf[:, (c % 2) * D:(c % 2 + 1) * D]
                s.activation(yb[:, 0:512], pa[:], AF.Copy, scale=self.sprod[:, 0:1])
                ss.inc(s.activation(yb[:, 512:1024], pbk[:], AF.Copy,
                                    scale=self.sprod[:, 0:1]))
                ss.inc(s.nop())
        block.scalar(ev)

        def vres(v):
            for c in range(TC):
                v.wait_ge(ss.h, evac_base + c * 2 + 2)
                yb = ybuf[:, (c % 2) * D:(c % 2 + 1) * D]
                y2 = y2buf[:, (c % 2) * D:(c % 2 + 1) * D]
                v.tensor_tensor(y2, yb, self.fbias_t[:], ALU.add)
                v.drain()
                sv.inc(v.scalar_tensor_tensor(
                    xsb[:, c * D:(c + 1) * D], y2, 0.5,
                    xsb[:, c * D:(c + 1) * D], ALU.mult, ALU.add))
        block.vector(vres)

        self._store_x(xsb, sv.n)
        self.barrier()
        ph.close()


    def _emit_mixer(self):
        nc, block = self.nc, self.block
        ld, sv, ss, st_, sst, sg = (self.sems[k] for k in ["ld", "v", "s", "t", "st", "g"])
        pb = self.pb

        sc_m = self.nc.enter_named_scope("m0", False)
        ph = ExitStack()
        sbl = lambda nm, shape, dt: ph.enter_context(nc.sbuf_tensor(self.name(nm), shape, dt))
        xsb = sbl("mx", [128, TC * D], F32)
        xnb = sbl("mxn", [128, TC * D], FP16)
        self._load_x(xsb)
        self._emit_ln(xsb, xnb)
        n_ln = self.sems["v"].n

        def stg0(sync):
            sync.wait_ge(sv.h, n_ln)
            sst.inc(sync.dma_start(
                out=self.stg_td16.rearrange("(c p) d -> p c d", p=128),
                in_=xnb[:].rearrange("p (c d) -> p c d", c=TC)), 16)
        block.sync(stg0)
        self.barrier()
        ph.close()
        self.nc.leave_named_scope("m0", sc_m[0], False)
        sc_m = self.nc.enter_named_scope("m1", False)

        ph_u = ExitStack()
        ubf = ph_u.enter_context(nc.sbuf_tensor(self.name("ubf"), [128, 16 * T], FP16))
        if self.wq_on:
            self._wq_init()
        ph = ExitStack()
        sbl = lambda nm, shape, dt: ph.enter_context(nc.sbuf_tensor(self.name(nm), shape, dt))
        xnT = sbl("xnT", [128, 8 * T], FP16)
        wsl = [sbl("wsl", [128, 8 * 128], FP16) for _ in range(2)]
        xzpad = sbl("xzpad", [128, 3 + T], FP16)
        cacc = [sbl("cacc", [128, T], F32) for _ in range(2)]
        thb = [sbl("thb", [128, T], FP16) for _ in range(2)]
        tzc = [sbl("tzc", [128, T], FP16) for _ in range(2)]
        sem_evac = Sem(nc, ph, self.name("m1ev"))
        sem_th = Sem(nc, ph, self.name("m1th"))
        sem_tz = Sem(nc, ph, self.name("m1tz"))
        sem_conv = Sem(nc, ph, self.name("m1cv"))
        sem_u = Sem(nc, ph, self.name("m1u"))

        def m1_tload(sync):
            sync.wait_ge(sst.h, sst.n)
            for k in range(8):
                sst.inc(sync.dma_start_transpose(
                    xnT[:, k * T:(k + 1) * T], self.stg_td16[:, k * 128:(k + 1) * 128]), 16)
        block.sync(m1_tload)
        n_xnT = sst.n

        g_base = sg.n
        t_base = st_.n
        tzst_base = sst.n

        def m1_g(g, e):
            g.wait_ge(sg.h, g_base + e * 16)
            if e >= 2:
                g.wait_ge(st_.h, t_base + (e - 1) * 4)
            sg.inc(g.dma_start(
                out=wsl[e % 2][:].rearrange("p (k e) -> p k e", k=8),
                in_=self.winT.rearrange("(k p) e -> p k e", p=128)[:, :, e * 128:(e + 1) * 128]), 16)

        def m1_t(t):
            t.wait_ge(sst.h, n_xnT)
            for e in range(32):
                t.wait_ge(sg.h, g_base + (e + 1) * 16)
                if e >= 2:
                    if e < 16 + 2:
                        pass
                    if (e - 2) < 16:
                        t.wait_ge(sem_evac.h, (e - 2) + 1)
                    else:
                        t.wait_ge(sem_tz.h, ((e - 2) - 16 + 1) * 4)
                w = wsl[e % 2]
                quad = [pb[(e % 2) * 4 + i] for i in range(4)]
                for tt in range(4):
                    for k in range(8):
                        i = t.matmul(quad[tt][:],
                                     w[:, k * 128:(k + 1) * 128],
                                     xnT[:, k * T + tt * 512:(k * T) + (tt + 1) * 512],
                                     start=(k == 0), stop=(k == 7))
                    st_.inc(i)
        block.tensor(m1_t)

        def v_init(v):
            v.memset(xzpad[:, 0:3], 0.0)
            v.drain()
        block.vector(v_init)

        for e in range(16):
            block.gpsimd(lambda g, e=e: m1_g(g, e))

            def s_x(s, e=e):
                s.wait_ge(st_.h, t_base + (e + 1) * 4)
                if e >= 1:
                    s.wait_ge(sem_conv.h, e)
                quad = [pb[(e % 2) * 4 + i] for i in range(4)]
                for tt in range(4):
                    s.activation(xzpad[:, 3 + tt * 512:3 + (tt + 1) * 512],
                                 quad[tt][:], AF.Identity, bias=self.inb_t[:, e:e + 1])
                s.drain()
                sem_evac.inc(s.nop())
                if e >= 1:
                    ee = e - 1
                    if ee >= 2:
                        s.wait_ge(sem_u.h, ee - 1)
                    s.drain()
                    sem_th.inc(s.activation(thb[ee % 2][:], cacc[ee % 2][:], AF.Tanh,
                                            scale=self.cst[:, C_HALF:C_HALF + 1]))
            block.scalar(s_x)

            def v_conv(v, e=e):
                v.wait_ge(sem_evac.h, e + 1)
                if e >= 2:
                    v.wait_ge(sem_th.h, e - 1)
                ca = cacc[e % 2]
                v.drain()
                v.tensor_scalar(ca[:], xzpad[:, 0:T], self.convw_t[:, 4 * e:4 * e + 1],
                                self.convb_t[:, e:e + 1], ALU.mult, ALU.add)
                for tau in range(1, 4):
                    v.drain()
                    i = v.scalar_tensor_tensor(ca[:], xzpad[:, tau:tau + T],
                                               self.convw_t[:, 4 * e + tau:4 * e + tau + 1],
                                               ca[:], ALU.mult, ALU.add)
                sem_conv.inc(i)
                if e >= 1:
                    ee = e - 1
                    v.wait_ge(sem_th.h, ee + 1)
                    sem_u.inc(v.scalar_tensor_tensor(ubf[:, ee * T:(ee + 1) * T], thb[ee % 2][:],
                                                     1.0, cacc[ee % 2][:], ALU.add, ALU.mult))
            block.vector(v_conv)

        def s_thf(s):
            s.wait_ge(sem_conv.h, 16)
            s.drain()
            sem_th.inc(s.activation(thb[15 % 2][:], cacc[15 % 2][:], AF.Tanh,
                                    scale=self.cst[:, C_HALF:C_HALF + 1]))
        block.scalar(s_thf)

        def v_uf(v):
            v.wait_ge(sem_th.h, 16)
            sem_u.inc(v.scalar_tensor_tensor(ubf[:, 15 * T:16 * T], thb[15 % 2][:],
                                             1.0, cacc[15 % 2][:], ALU.add, ALU.mult))
            v.drain()
            sv.inc(v.nop())
        block.vector(v_uf)

        for e in range(16, 32):
            block.gpsimd(lambda g, e=e: m1_g(g, e))

            def s_z(s, e=e):
                s.wait_ge(st_.h, t_base + (e + 1) * 4)
                if e >= 18:
                    s.wait_ge(sst.h, tzst_base + (e - 16 - 1) * 16)
                quad = [pb[(e % 2) * 4 + i] for i in range(4)]
                for tt in range(4):
                    sem_tz.inc(s.activation(tzc[e % 2][:, tt * 512:(tt + 1) * 512], quad[tt][:],
                                            AF.Tanh, scale=self.cst[:, C_HALF:C_HALF + 1],
                                            bias=self.hbz[:, e - 16:e - 15]))
            block.scalar(s_z)

            def m1_tzstore(sync, e=e):
                sync.wait_ge(sst.h, tzst_base + (e - 16) * 16)
                sync.wait_ge(sem_tz.h, (e - 16 + 1) * 4)
                sst.inc(sync.dma_start(
                    out=self.stg_tz[(e - 16) * 128:(e - 15) * 128, :],
                    in_=tzc[e % 2][:]), 16)
            block.sync(m1_tzstore)
            if self.wq_on:
                self._wq_step()

        self.barrier()
        ph.close()
        self.nc.leave_named_scope("m1", sc_m[0], False)
        sc_m = self.nc.enter_named_scope("m2", False)

        ph2 = ExitStack()
        sbl2 = lambda nm, shape, dt: ph2.enter_context(nc.sbuf_tensor(self.name(nm), shape, dt))
        bsl = [sbl2("bsl", [128, 16 * 128], FP16) for _ in range(2)]
        bcc = [sbl2("bcc", [128, T], FP16) for _ in range(2)]

        for wsrc, dsts in [(self.bwT, self.stg_b), (self.cwT, self.stg_c)]:
            g2 = sg.n
            t2 = st_.n
            s2 = ss.n
            d2 = sst.n

            def m2_t(t, g2=g2, s2=s2, t2=t2):
                for e in range(16):
                    t.wait_ge(sg.h, g2 + (e + 1) * 16)
                    t.wait_ge(ss.h, s2 + max(0, e - 1) * 4)
                    w = bsl[e % 2]
                    quad = [pb[(e % 2) * 4 + i] for i in range(4)]
                    for tt in range(4):
                        for k in range(16):
                            i = t.matmul(quad[tt][:], w[:, k * 128:(k + 1) * 128],
                                         ubf[:, k * T + tt * 512: k * T + (tt + 1) * 512],
                                         start=(k == 0), stop=(k == 15))
                        st_.inc(i)
            block.tensor(m2_t)

            for e in range(16):
                def m2_g(g, wsrc=wsrc, t2=t2, g2=g2, e=e):
                    g.wait_ge(sg.h, g2 + e * 16)
                    g.wait_ge(st_.h, t2 + max(0, e - 1) * 4)
                    sg.inc(g.dma_start(
                        out=bsl[e % 2][:].rearrange("p (k e) -> p k e", k=16),
                        in_=wsrc.rearrange("(k p) e -> p k e", p=128)[:, :, e * 128:(e + 1) * 128]), 16)
                block.gpsimd(m2_g)
                if self.wq_on:
                    self._wq_step()
                    self._wq_step()

                def m2_s_e(s, t2=t2, d2=d2, e=e):
                    s.wait_ge(st_.h, t2 + (e + 1) * 4)
                    s.wait_ge(sst.h, d2 + max(0, e - 1) * 16)
                    quad = [pb[(e % 2) * 4 + i] for i in range(4)]
                    for tt in range(4):
                        ss.inc(s.activation(bcc[e % 2][:, tt * 512:(tt + 1) * 512],
                                            quad[tt][:], AF.Tanh,
                                            scale=self.cst[:, C_HALF:C_HALF + 1]))
                block.scalar(m2_s_e)

                def m2_st(sync, s2=s2, dsts=dsts, d2=d2, e=e):
                    sync.wait_ge(sst.h, d2 + e * 16)
                    sync.wait_ge(ss.h, s2 + (e + 1) * 4)
                    sst.inc(sync.dma_start(out=dsts[e * 128:(e + 1) * 128, :],
                                           in_=bcc[e % 2][:]), 16)
                block.sync(m2_st)
        if self.wq_on:
            self._wq_drain()
            self.barrier(extra=self.wq_extras)
            ph2.close()
            self.wq_ph.close()
        else:
            self.barrier()
            ph2.close()
        self.nc.leave_named_scope("m2", sc_m[0], False)
        sc_m = self.nc.enter_named_scope("m3", False)

        ph3 = ExitStack()
        sbl3 = lambda nm, shape, dt: ph3.enter_context(nc.sbuf_tensor(self.name(nm), shape, dt))
        dsl = [sbl3("dsl", [128, 16 * 128], FP16) for _ in range(2)]
        ebuf = sbl3("ebuf", [128, T], F32)
        spbuf = sbl3("spbuf", [128, T], F32)
        dbuf = [sbl3("dbuf", [128, T], F32) for _ in range(2)]
        ombuf = sbl3("ombuf", [128, T], F32)
        t1buf = sbl3("t1buf", [128, T], F32)
        incb = sbl3("incb", [128, T], F32)
        stsb = sbl3("stsb", [128, T], F32)
        t2buf = sbl3("t2buf", [128, T], F32)
        ypb = sbl3("ypb", [128, T], F32)
        y2c = [sbl3("y2c", [128, T], FP16) for _ in range(2)]
        bld = [sbl3("bld", [128, T], FP16) for _ in range(2)]
        cld = [sbl3("cld", [128, T], FP16) for _ in range(2)]
        tzld = [sbl3("tzld", [128, T], FP16) for _ in range(2)]

        g3 = sg.n
        t3 = st_.n
        s3 = ss.n
        v3 = sv.n
        d3 = sst.n
        ld3 = ld.n

        for e in range(16):
            def m3_g(g, e=e):
                g.wait_ge(sg.h, g3 + e * 16)
                if e >= 2:
                    g.wait_ge(st_.h, t3 + (e - 1) * 4)
                sg.inc(g.dma_start(
                    out=dsl[e % 2][:].rearrange("p (k e) -> p k e", k=16),
                    in_=self.dtwT.rearrange("(k p) e -> p k e", p=128)[:, :, e * 128:(e + 1) * 128]), 16)
            block.gpsimd(m3_g)

        def m3_ld_body(sync, e):
            sync.wait_ge(ld.h, ld3 + e * 48)
            if e >= 2:
                sync.wait_ge(sv.h, v3 + (e - 1) * 7)
            ld.inc(sync.dma_start(out=bld[e % 2][:], in_=self.stg_b[e * 128:(e + 1) * 128, :]), 16)
            ld.inc(sync.dma_start(out=cld[e % 2][:], in_=self.stg_c[e * 128:(e + 1) * 128, :]), 16)
            ld.inc(sync.dma_start(out=tzld[e % 2][:], in_=self.stg_tz[e * 128:(e + 1) * 128, :]), 16)

        def m3_st_body(sync, e):
            sync.wait_ge(sst.h, d3 + e * 16)
            sync.wait_ge(sv.h, v3 + (e + 1) * 7)
            sst.inc(sync.dma_start(out=self.stg_y2[e * 128:(e + 1) * 128, :],
                                   in_=y2c[e % 2][:]), 16)

        for e in range(16):
            block.sync(lambda sync, e=e: m3_ld_body(sync, e))
            if e >= 2:
                block.sync(lambda sync, e=e: m3_st_body(sync, e - 2))
        block.sync(lambda sync: m3_st_body(sync, 14))
        block.sync(lambda sync: m3_st_body(sync, 15))

        def m3_t(t):
            for e in range(16):
                t.wait_ge(sg.h, g3 + (e + 1) * 16)
                if e >= 2:
                    t.wait_ge(ss.h, s3 + (e - 1) * 6)
                w = dsl[e % 2]
                quad = [pb[(e % 2) * 4 + i] for i in range(4)]
                for tt in range(4):
                    for k in range(16):
                        i = t.matmul(quad[tt][:], w[:, k * 128:(k + 1) * 128],
                                     ubf[:, k * T + tt * 512: k * T + (tt + 1) * 512],
                                     start=(k == 0), stop=(k == 15))
                    st_.inc(i)
        block.tensor(m3_t)

        def m3_s(s):
            for e in range(16):
                s.wait_ge(st_.h, t3 + (e + 1) * 4)
                if e >= 2:
                    s.wait_ge(sv.h, v3 + (e - 1) * 7)
                quad = [pb[(e % 2) * 4 + i] for i in range(4)]
                for tt in range(4):
                    i = s.activation(ebuf[:, tt * 512:(tt + 1) * 512], quad[tt][:], AF.Exp,
                                     scale=self.cst[:, C_HALF:C_HALF + 1],
                                     bias=self.dtb[:, e:e + 1]) if False else                         s.activation(ebuf[:, tt * 512:(tt + 1) * 512], quad[tt][:], AF.Exp,
                                     scale=self.cst[:, C_HALF:C_HALF + 1],
                                     bias=self.dtb_t[:, e:e + 1])
                    if tt == 3:
                        ss.inc(i)
                s.drain()
                s.activation(spbuf[:], ebuf[:], AF.Ln, bias=self.cst[:, C_ONE:C_ONE + 1])
                s.drain()
                i = s.activation(dbuf[e % 2][:], spbuf[:], AF.Exp,
                                 scale=self.negA[:, e:e + 1], bias=self.nb2[:, e:e + 1])
                ss.inc(i)
                ss.inc(s.nop()); ss.inc(s.nop()); ss.inc(s.nop()); ss.inc(s.nop())
        block.scalar(m3_s)

        def m3_v(v):
            for e in range(16):
                v.wait_ge(ss.h, s3 + (e + 1) * 6)
                v.wait_ge(ld.h, ld3 + (e + 1) * 48)
                if e >= 2:
                    v.wait_ge(sst.h, d3 + (e - 1) * 16)
                v.tensor_scalar(ombuf[:], dbuf[e % 2][:], -1.0, 1.0, ALU.mult, ALU.add)
                v.tensor_tensor(t1buf[:], bld[e % 2][:], ubf[:, e * T:(e + 1) * T], ALU.mult)
                v.drain()
                v.tensor_tensor(incb[:], ombuf[:], t1buf[:], ALU.mult)
                v.drain()
                v.tensor_tensor_scan(stsb[:], dbuf[e % 2][:], incb[:], 0.0, ALU.mult, ALU.add)
                v.drain()
                v.tensor_tensor(t2buf[:], cld[e % 2][:], stsb[:], ALU.mult)
                v.drain()
                v.scalar_tensor_tensor(ypb[:], ubf[:, e * T:(e + 1) * T],
                                       self.dpar_t[:, e:e + 1], t2buf[:], ALU.mult, ALU.add)
                v.drain()
                i = v.scalar_tensor_tensor(y2c[e % 2][:], tzld[e % 2][:], 1.0,
                                           ypb[:], ALU.add, ALU.mult)
                for _ in range(7):
                    pass
                sv.inc(i)
                sv.inc(v.nop()); sv.inc(v.nop()); sv.inc(v.nop())
                sv.inc(v.nop()); sv.inc(v.nop()); sv.inc(v.nop())
        block.vector(m3_v)



        self.barrier()
        ph3.close()
        ph_u.close()

        self.nc.leave_named_scope("m3", sc_m[0], False)
        sc_m = self.nc.enter_named_scope("m4", False)
        ph4 = ExitStack()
        sbl4 = lambda nm, shape, dt: ph4.enter_context(nc.sbuf_tensor(self.name(nm), shape, dt))
        owb = sbl4("owb", [128, 16 * D], FP16)
        xsb4 = sbl4("mx4", [128, TC * D], F32)
        y2g = [sbl4("y2g", [128, 16 * 128], FP16) for _ in range(2)]
        obuf = [sbl4("obuf", [128, D], F32) for _ in range(2)]
        r1b = [sbl4("r1b", [128, D], F32) for _ in range(2)]

        g4 = sg.n

        def m4_g(g):
            sg.inc(g.dma_start(
                out=owb[:].rearrange("p (k d) -> p k d", k=16),
                in_=self.outwT.rearrange("(k p) d -> p k d", p=128)), 16)
        block.gpsimd(m4_g)
        self._load_x(xsb4)

        ld4 = ld.n
        t4 = st_.n
        s4 = ss.n
        v4 = sv.n
        d4 = sst.n

        for c in range(TC):
            def m4_ld(sync, c=c):
                sync.wait_ge(ld.h, ld4 + c * 16)
                if c >= 2:
                    sync.wait_ge(st_.h, t4 + (c - 1) * 2)
                ld.inc(sync.dma_start(
                    out=y2g[c % 2][:].rearrange("p (e q) -> p e q", e=16),
                    in_=self.stg_y2.rearrange("(e p) t -> p e t", p=128)[:, :, c * 128:(c + 1) * 128]), 16)
            block.sync(m4_ld)

        def m4_t(t):
            t.wait_ge(sg.h, g4 + 16)
            for c in range(TC):
                t.wait_ge(ld.h, ld4 + (c + 1) * 16)
                if c >= 2:
                    t.wait_ge(ss.h, s4 + (c - 1) * 2)
                pa, pbk = pb[(c % 2) * 2], pb[(c % 2) * 2 + 1]
                for e in range(16):
                    i1 = t.matmul(pa[:], y2g[c % 2][:, e * 128:(e + 1) * 128],
                                  owb[:, e * D:e * D + 512], start=(e == 0), stop=(e == 15))
                st_.inc(i1)
                for e in range(16):
                    i2 = t.matmul(pbk[:], y2g[c % 2][:, e * 128:(e + 1) * 128],
                                  owb[:, e * D + 512:(e + 1) * D], start=(e == 0), stop=(e == 15))
                st_.inc(i2)
        block.tensor(m4_t)

        def m4_s(s):
            for c in range(TC):
                s.wait_ge(st_.h, t4 + (c + 1) * 2)
                if c >= 2:
                    s.wait_ge(sv.h, v4 + (c - 1) * 2)
                pa, pbk = pb[(c % 2) * 2], pb[(c % 2) * 2 + 1]
                s.activation(obuf[c % 2][:, 0:512], pa[:], AF.Copy,
                             scale=self.cst[:, C_QUART:C_QUART + 1])
                i = s.activation(obuf[c % 2][:, 512:1024], pbk[:], AF.Copy,
                                 scale=self.cst[:, C_QUART:C_QUART + 1])
                ss.inc(i)
                ss.inc(s.nop())
        block.scalar(m4_s)

        def m4_v(v):
            for c in range(TC):
                v.wait_ge(ss.h, s4 + (c + 1) * 2)
                v.tensor_tensor(r1b[c % 2][:], obuf[c % 2][:], self.outb_t[:], ALU.add)
                v.drain()
                i = v.tensor_tensor(xsb4[:, c * D:(c + 1) * D], xsb4[:, c * D:(c + 1) * D],
                                    r1b[c % 2][:], ALU.add)
                sv.inc(i)
                sv.inc(v.nop())
        block.vector(m4_v)

        self._store_x(xsb4, sv.n)
        self.barrier()
        ph4.close()
        self.nc.leave_named_scope("m4", sc_m[0], False)

    class _Quant:

        def __init__(self, B, ph, F, n_par=2):
            nc = B.nc
            self.B = B
            self.F = F
            self.n_par = n_par
            nb = F // 16
            sbl = lambda nm, shape, dt: ph.enter_context(
                nc.sbuf_tensor(B.name(nm), shape, dt))
            self.xn = [sbl("q_xn", [128, F], FP16) for _ in range(n_par)]
            self.an = [sbl("q_an", [128, F], FP16) for _ in range(n_par)]
            self.sg = [sbl("q_sg", [128, F], FP16) for _ in range(n_par)]
            self.sm = [sbl("q_sm", [128, 3 * nb], F32) for _ in range(n_par)]
            self.sc16 = [sbl("q_sc16", [128, nb], FP16) for _ in range(n_par)]
            self.sv_q = Sem(nc, ph, B.name("qv"))
            self.ss_q = Sem(nc, ph, B.name("qs"))
            self.ss_a = Sem(nc, ph, B.name("qa"))
            self.sv_t = Sem(nc, ph, B.name("qt"))
            self.sv_qb = Sem(nc, ph, B.name("qb"))
            self.nA = 0
            self.nS = 0
            self.n = 0
            self.nC = 0

        def emit_A(self, v, src_ap, W=None):
            F = W if W is not None else self.F
            nb, nbm = F // 16, self.F // 16
            i = self.nA
            self.nA += 1
            p = i % self.n_par
            sm = self.sm[p]
            if i >= self.n_par:
                v.wait_ge(self.sv_qb.h, i - self.n_par + 1)
            v.drain()
            v.tensor_reduce(sm[:, 0:nb], src_ap.rearrange("p (b w) -> p b w", w=16),
                            AX.X, ALU.max, apply_absolute_value=True)
            v.drain()
            v.tensor_scalar(sm[:, 2 * nbm:2 * nbm + nb], sm[:, 0:nb], 1.0 / 6.0, 1e-12,
                            ALU.mult, ALU.max)
            v.drain()
            v.reciprocal(sm[:, nbm:nbm + nb], sm[:, 2 * nbm:2 * nbm + nb])
            v.tensor_scalar(self.sc16[p][:, 0:nb], sm[:, 2 * nbm:2 * nbm + nb], 1.0, None,
                            ALU.mult)
            v.drain()
            self.sv_q.inc(v.tensor_tensor(
                self.xn[p][:, 0:F], src_ap,
                sm[:, nbm:nbm + nb].rearrange("p b -> p b ()").broadcast_to([128, nb, 16]),
                ALU.mult))

        def emit_S(self, s, W=None):
            F = W if W is not None else self.F
            i = self.nS
            self.nS += 1
            p = i % self.n_par
            s.wait_ge(self.sv_q.h, i + 1)
            if i >= self.n_par:
                s.wait_ge(self.sv_qb.h, i - self.n_par + 1)
            s.drain()
            self.ss_a.inc(s.activation(self.an[p][:, 0:F], self.xn[p][:, 0:F], AF.Abs))
            self.ss_q.inc(s.activation(self.sg[p][:, 0:F], self.xn[p][:, 0:F], AF.Sign))

        def emit_B(self, v, W=None):
            F = W if W is not None else self.F
            p = self.n % self.n_par
            an = self.an[p][:, 0:F]
            tA, tB, tC, tD = (t[:, 0:F] for t in self.B.qtb)
            v.wait_ge(self.ss_a.h, self.n + 1)
            v.drain()
            v.tensor_scalar(tA, an, 0.25, 0.5, ALU.is_gt, ALU.mult)
            v.tensor_scalar(tB, an, 0.75, 0.5, ALU.is_gt, ALU.mult)
            v.tensor_scalar(tC, an, 1.25, 0.5, ALU.is_gt, ALU.mult)
            v.tensor_scalar(tD, an, 1.75, 0.5, ALU.is_gt, ALU.mult)
            v.drain()
            v.tensor_tensor(tA, tA, tB, ALU.add)
            v.tensor_tensor(tC, tC, tD, ALU.add)
            v.drain()
            v.tensor_scalar(tB, an, 2.5, 1.0, ALU.is_gt, ALU.mult)
            v.tensor_scalar(tD, an, 3.5, 1.0, ALU.is_gt, ALU.mult)
            v.tensor_tensor(tA, tA, tC, ALU.add)
            v.drain()
            v.tensor_scalar(tC, an, 5.0, 2.0, ALU.is_gt, ALU.mult)
            v.tensor_tensor(tB, tB, tD, ALU.add)
            v.drain()
            v.tensor_tensor(tA, tA, tB, ALU.add)
            v.drain()
            i = v.tensor_tensor(an, tA, tC, ALU.add)
            self.sv_t.inc(i)
            self.n += 1
            return i

        def emit_C(self, g, out_ap, W=None):
            F = W if W is not None else self.F
            nb = F // 16
            i = self.nC
            self.nC += 1
            p = i % self.n_par
            g.wait_ge(self.sv_t.h, i + 1)
            g.wait_ge(self.ss_q.h, i + 1)
            g.drain()
            g.tensor_tensor(self.xn[p][:, 0:F], self.an[p][:, 0:F],
                            self.sg[p][:, 0:F], ALU.mult)
            g.drain()
            i2 = g.tensor_tensor(
                out_ap, self.xn[p][:, 0:F].rearrange("p (b w) -> p b w", w=16),
                self.sc16[p][:, 0:nb].rearrange("p b -> p b ()").broadcast_to([128, nb, 16]),
                ALU.mult)
            self.sv_qb.inc(i2)
            return i2

    def _wq_init(self):
        nc = self.nc
        ph = ExitStack()
        sbl = lambda nm, shape, dt: ph.enter_context(nc.sbuf_tensor(self.name(nm), shape, dt))
        self.wq_ph = ph
        self.wq_cin = [sbl("wq_in", [128, 1408], F32) for _ in range(2)]
        self.wq_cout = [sbl("wq_out", [128, 1408], FP16) for _ in range(2)]
        self.qtb = [sbl("qtb", [128, 1408], FP16) for _ in range(4)]
        self.wq_q = self._Quant(self, ph, 1408)
        self.wq_ld = Sem(nc, ph, self.name("wqld"))
        self.wq_st = Sem(nc, ph, self.name("wqst"))
        pieces = []
        for c in range(IC):
            pieces.append((self.wg_in, self.wgq_d, c * 128, 0, D))
        for c in range(IC):
            pieces.append((self.wu_in, self.wuq_d, c * 128, 0, D))
        for c in range(8):
            pieces.append((self.wd_in, self.wdq_d, c * 128, 0, 1408))
            pieces.append((self.wd_in, self.wdq_d, c * 128, 1408, 1408))
        self.wq_pieces = pieces
        self.wq_j = 0
        q = self.wq_q
        self.wq_extras = [q.sv_q, q.ss_q, q.ss_a, q.sv_t, q.sv_qb, self.wq_ld, self.wq_st]

    def _wq_ldb(self, g, j):
        srcd, dstd, r0, c0, W = self.wq_pieces[j]
        g.wait_ge(self.wq_ld.h, j * 16)
        if j >= 2:
            g.wait_ge(self.wq_q.sv_q.h, j - 1)
        self.wq_ld.inc(g.dma_start(out=self.wq_cin[j % 2][:, 0:W],
                                   in_=srcd[r0:r0 + 128, c0:c0 + W]), 16)

    def _wq_stb(self, sync, j):
        srcd, dstd, r0, c0, W = self.wq_pieces[j]
        sync.wait_ge(self.wq_st.h, j * 16)
        sync.wait_ge(self.wq_q.sv_qb.h, j + 1)
        self.wq_st.inc(sync.dma_start(out=dstd[r0:r0 + 128, c0:c0 + W],
                                      in_=self.wq_cout[j % 2][:, 0:W]), 16)

    def _wq_step(self):
        NP = len(self.wq_pieces)
        j = self.wq_j
        if j > NP:
            return
        block, q = self.block, self.wq_q
        if j < NP:
            srcd, dstd, r0, c0, W = self.wq_pieces[j]
            if j == 0:
                block.sync(lambda sync: self._wq_ldb(sync, 0))
                block.sync(lambda sync: self._wq_ldb(sync, 1))
            elif j + 1 < NP:
                block.sync(lambda sync, k=j + 1: self._wq_ldb(sync, k))

            def va(v, j=j, W=W):
                v.wait_ge(self.wq_ld.h, (j + 1) * 16)
                q.emit_A(v, self.wq_cin[j % 2][:, 0:W], W)
            block.vector(va)

            def sa(s, W=W):
                q.emit_S(s, W)
            block.scalar(sa)
        if j >= 1:
            srcd, dstd, r0, c0, W = self.wq_pieces[j - 1]

            def vb(v, W=W):
                q.emit_B(v, W)
            block.vector(vb)

            def gc(g, jj=j - 1, W=W):
                if jj >= 2:
                    g.wait_ge(self.wq_st.h, (jj - 1) * 16)
                q.emit_C(g, self.wq_cout[jj % 2][:, 0:W]
                         .rearrange("p (b w) -> p b w", w=16), W)
            block.gpsimd(gc)
            if j - 2 >= 0:
                block.sync(lambda sync, k=j - 2: self._wq_stb(sync, k))
        self.wq_j += 1

    def _wq_drain(self):
        NP = len(self.wq_pieces)
        while self.wq_j <= NP:
            self._wq_step()
        self.block.sync(lambda sync: self._wq_stb(sync, NP - 1))

    def _emit_ffn4(self):
        nc, block = self.nc, self.block
        ld, sv, ss, st_, sst, sg = (self.sems[k] for k in ["ld", "v", "s", "t", "st", "g"])
        pb = self.pb
        stat, cst = self.stat, self.cst

        scope_f41 = nc.enter_named_scope("f41", False)
        ph1 = ExitStack()
        sbl1 = lambda nm, shape, dt: ph1.enter_context(nc.sbuf_tensor(self.name(nm), shape, dt))
        wgT = sbl1("wgT", [128, 8 * I1], FP16)
        wuT = sbl1("wuT", [128, 8 * I1], FP16)
        xcb = [sbl1("f4x", [128, D], F32) for _ in range(2)]
        xnf = sbl1("f4xn", [128, D], F32)
        xq4c = [sbl1("f4xq", [128, D], FP16) for _ in range(2)]
        xq4Tc = [sbl1("f4xqT", [128, 8 * 128], FP16) for _ in range(2)]
        sgb = [sbl1("f4sg", [128, 512], FP16) for _ in range(2)]
        upe = [sbl1("f4up", [128, 512], FP16) for _ in range(2)]
        hb = [sbl1("f4h", [128, I1], F32) for _ in range(2)]
        hqp = [sbl1("f4hq", [128, 1408], FP16) for _ in range(2)]
        self.qtb = [sbl1("qtb", [128, 1408], FP16) for _ in range(4)]
        qx = self._Quant(self, ph1, D)
        qh = self._Quant(self, ph1, 1408)
        se_x = Sem(nc, ph1, self.name("f4ex"))
        s_hst = Sem(nc, ph1, self.name("f4hs"))
        sgh = Sem(nc, ph1, self.name("f4gh"))
        NT = [512, 512, 512, 512, 512, 256]
        NPH = 2 * TC

        ldw0 = ld.n

        def f41_wsync(sync):
            sync.wait_ge(ld.h, ldw0)
            for k in range(8):
                ld.inc(sync.dma_start_transpose(
                    wgT[:, k * I1:(k + 1) * I1], self.wgq_d[:, k * 128:(k + 1) * 128]), 16)
        def f41_wsync2(sync):
            for k in range(8):
                ld.inc(sync.dma_start_transpose(
                    wuT[:, k * I1:(k + 1) * I1], self.wuq_d[:, k * 128:(k + 1) * 128]), 16)
        block.sync(f41_wsync)
        block.sync(f41_wsync2)
        n_wld = ld.n
        n_wsg = sg.n

        lp1 = ld.n
        v0 = sv.n
        for c in range(TC):
            def ld1(sync, c=c):
                sync.wait_ge(ld.h, lp1 + c * 16)
                if c >= 2:
                    sync.wait_ge(sv.h, v0 + (c - 1))
                ld.inc(sync.dma_start(
                    out=xcb[c % 2][:],
                    in_=self.xd.rearrange("(c p) d -> p c d", p=128)[:, c, :]), 16)
            block.sync(ld1)

            def v1(v, c=c):
                v.wait_ge(ld.h, lp1 + (c + 1) * 16)
                v.drain()
                v.tensor_reduce(stat[:, c:c + 1], xcb[c % 2][:], AX.X, ALU.add)
                v.scalar_tensor_tensor(xnf[:], xcb[c % 2][:], 1.0, xcb[c % 2][:],
                                       ALU.mult, ALU.mult,
                                       accum_out=stat[:, 16 + c:17 + c])
                v.drain()
                sv.inc(v.nop())
            block.vector(v1)
        v_p1 = sv.n

        def v_stats(v):
            v.drain()
            v.tensor_scalar(stat[:, 32:48], stat[:, 0:16], 1.0 / D, None, ALU.mult)
            v.drain()
            v.tensor_tensor(stat[:, 64:80], stat[:, 32:48], stat[:, 32:48], ALU.mult)
            v.drain()
            sv.inc(v.scalar_tensor_tensor(stat[:, 48:64], stat[:, 16:32], 1.0 / D,
                                          stat[:, 64:80], ALU.mult, ALU.subtract))
        block.vector(v_stats)
        n_v1 = sv.n

        def sc1(s):
            s.wait_ge(sv.h, n_v1)
            ss.inc(s.activation(stat[:, 80:96], stat[:, 48:64], AF.Sqrt,
                                bias=cst[:, C_EPS:C_EPS + 1]))
        block.scalar(sc1)
        n_s1p = ss.n

        def v_rstd(v):
            v.wait_ge(ss.h, n_s1p)
            sv.inc(v.reciprocal(stat[:, 48:64], stat[:, 80:96]))
            v.wait_ge(sv.h, sv.n)
        block.vector(v_rstd)
        v1b = sv.n
        s1b = ss.n
        lp2 = sg.n

        marks_trans = {}
        marks_gu = {}

        def f4_t(t):
            t.wait_ge(ld.h, n_wld)

            def trans(c):
                t.wait_ge(qx.sv_qb.h, c + 1)
                if c >= 2:
                    t.wait_ge(se_x.h, c - 1)
                b0 = pb[4 + 2 * (c % 2)]
                b1 = pb[5 + 2 * (c % 2)]
                for k in range(8):
                    bank = b0 if k < 4 else b1
                    i = t.matmul(bank[:, (k % 4) * 128:(k % 4) * 128 + 128],
                                 xq4c[c % 2][:, k * 128:(k + 1) * 128],
                                 self.ident[:], start=True, stop=True)
                    if k % 4 == 3:
                        st_.inc(i)
                marks_trans[c] = st_.n

            def gu(c):
                t.wait_ge(se_x.h, c + 1)
                off = 0
                for n, w in enumerate(NT):
                    pg, pu = pb[(n % 2) * 2], pb[(n % 2) * 2 + 1]
                    cnt = c * 6 + n - 1
                    if cnt >= 0:
                        t.wait_ge(ss.h, s1b + 2 * cnt)
                    for k in range(8):
                        ig = t.matmul(pg[:, 0:w], xq4Tc[c % 2][:, k * 128:(k + 1) * 128],
                                      wgT[:, k * I1 + off:k * I1 + off + w],
                                      start=(k == 0), stop=(k == 7))
                    st_.inc(ig)
                    for k in range(8):
                        iu = t.matmul(pu[:, 0:w], xq4Tc[c % 2][:, k * 128:(k + 1) * 128],
                                      wuT[:, k * I1 + off:k * I1 + off + w],
                                      start=(k == 0), stop=(k == 7))
                    st_.inc(iu)
                    marks_gu[(c, n)] = st_.n
                    off += w

            trans(0)
            trans(1)
            for c in range(TC):
                gu(c)
                if c + 2 < TC:
                    trans(c + 2)
        block.tensor(f4_t)

        for c in range(TC + 4):
            if c < TC:
                def s_qxS(s):
                    qx.emit_S(s)
                block.scalar(s_qxS)
            if 3 <= c < TC + 3:
                def s_qhS(s, cc=c - 3):
                    for q in range(2):
                        qh.emit_S(s, 1408)
                block.scalar(s_qhS)
            if 1 <= c <= TC:
                def s_evacx(s, cc=c - 1):
                    s.wait_ge(st_.h, marks_trans[cc])
                    b0 = pb[4 + 2 * (cc % 2)]
                    b1 = pb[5 + 2 * (cc % 2)]
                    s.activation(xq4Tc[cc % 2][:, 0:512], b0[:], AF.Copy)
                    se_x.inc(s.activation(xq4Tc[cc % 2][:, 512:1024], b1[:], AF.Copy))
                block.scalar(s_evacx)
            if 2 <= c < TC + 2:
                def s_silu(s, cc=c - 2):
                    for n, w in enumerate(NT):
                        s.wait_ge(st_.h, marks_gu[(cc, n)])
                        pg, pu = pb[(n % 2) * 2], pb[(n % 2) * 2 + 1]
                        prev = cc * 6 + n - 1
                        if prev >= 0:
                            s.wait_ge(sgh.h, prev)
                        ss.inc(s.activation(sgb[n % 2][:, 0:w], pg[:, 0:w], AF.Silu))
                        ss.inc(s.activation(upe[n % 2][:, 0:w], pu[:, 0:w], AF.Copy))
                block.scalar(s_silu)

        def emit_qh_A(cc, q):
            def vA(v):
                if q == 0:
                    v.wait_ge(sgh.h, (cc + 1) * 6)
                qh.emit_A(v, hb[cc % 2][:, q * 1408:(q + 1) * 1408], 1408)
            block.vector(vA)

        def emit_qh_B(j):
            def vB(v):
                qh.emit_B(v, 1408)
            block.vector(vB)

        for c in range(TC + 4):
            if c < TC:
                def v_norm_qxA(v, c=c):
                    v.wait_ge(sg.h, lp2 + (c + 1) * 16)
                    v.drain()
                    v.tensor_scalar(xnf[:], xcb[c % 2][:],
                                    stat[:, 32 + c:33 + c], stat[:, 48 + c:49 + c],
                                    ALU.subtract, ALU.mult)
                    qx.emit_A(v, xnf[:])
                block.vector(v_norm_qxA)
            if 1 <= c <= TC:
                def v_qxB(v):
                    qx.emit_B(v, D)
                block.vector(v_qxB)
            if 3 <= c < TC + 3:
                cc = c - 3
                for q in range(2):
                    j = cc * 2 + q
                    emit_qh_A(cc, q)
                    if j >= 1:
                        emit_qh_B(j - 1)
        emit_qh_B(NPH - 1)

        def ld2(g, c):
            g.wait_ge(sg.h, lp2 + c * 16)
            if c < 2:
                g.wait_ge(sv.h, v0 + 15 + c)
            else:
                g.wait_ge(qx.sv_q.h, c - 1)
            sg.inc(g.dma_start(
                out=xcb[c % 2][:],
                in_=self.xd.rearrange("(c p) d -> p c d", p=128)[:, c, :]), 16)

        def sth(sync, j):
            cc, q = j // 2, j % 2
            sync.wait_ge(s_hst.h, j * 16)
            sync.wait_ge(qh.sv_qb.h, j + 1)
            s_hst.inc(sync.dma_start(
                out=self.hq_d[cc * 128:(cc + 1) * 128, q * 1408:(q + 1) * 1408],
                in_=hqp[j % 2][:]), 16)

        def g_Vh(g, cc):
            g.drain()
            off = 0
            for n, w in enumerate(NT):
                g.wait_ge(ss.h, s1b + 2 * (cc * 6 + n + 1))
                if n == 0 and cc >= 2:
                    g.wait_ge(qh.sv_q.h, 2 * (cc - 1))
                sgh.inc(g.tensor_tensor(hb[cc % 2][:, off:off + w],
                                        sgb[n % 2][:, 0:w], upe[n % 2][:, 0:w], ALU.mult))
                off += w

        def qx_C(g, cc):
            if cc >= 2:
                g.wait_ge(st_.h, marks_trans[cc - 2])
            qx.emit_C(g, xq4c[cc % 2][:].rearrange("p (b w) -> p b w", w=16), D)

        def qh_C(g, j):
            if j >= 2:
                g.wait_ge(s_hst.h, (j - 1) * 16)
            qh.emit_C(g, hqp[j % 2][:].rearrange("p (b w) -> p b w", w=16), 1408)

        for c in range(TC + 4):
            if c < TC:
                block.gpsimd(lambda g, c=c: ld2(g, c))
            if 1 <= c <= TC:
                block.gpsimd(lambda g, cc=c - 1: qx_C(g, cc))
            if 3 <= c < TC + 3:
                cc = c - 3
                block.gpsimd(lambda g, cc=cc: g_Vh(g, cc))
                if 2 * cc - 1 >= 0:
                    block.gpsimd(lambda g, j=2 * cc - 1: qh_C(g, j))
                block.gpsimd(lambda g, j=2 * cc: qh_C(g, j))
        block.gpsimd(lambda g: qh_C(g, NPH - 1))
        for j in range(NPH):
            block.sync(lambda sync, j=j: sth(sync, j))

        self.barrier(extra=[qx.sv_q, qx.ss_q, qx.ss_a, qx.sv_t, qx.sv_qb,
                            qh.sv_q, qh.ss_q, qh.ss_a, qh.sv_t, qh.sv_qb, se_x, s_hst, sgh])
        ph1.close()
        nc.leave_named_scope("f41", scope_f41[0] if isinstance(scope_f41, tuple) else scope_f41, False)

        scope_f42 = nc.enter_named_scope("f42", False)
        ph2 = ExitStack()
        sbl2 = lambda nm, shape, dt: ph2.enter_context(nc.sbuf_tensor(self.name(nm), shape, dt))
        hqT = sbl2("hqT", [128, IC * T // 1], BF16) if False else             sbl2("hqT", [128, 22 * T], FP16)
        wdT = sbl2("wdT", [128, 22 * D], FP16)
        obuf = [sbl2("f4o", [128, D], F32) for _ in range(2)]
        xc = [sbl2("f4xc", [128, D], F32) for _ in range(2)]

        ld2 = ld.n

        def f42_w(sync):
            sync.wait_ge(ld.h, ld2)
            for i in range(22):
                ld.inc(sync.dma_start_transpose(
                    hqT[:, i * T:(i + 1) * T], self.hq_d[:, i * 128:(i + 1) * 128]), 16)
            for i in range(22):
                ld.inc(sync.dma_start_transpose(
                    wdT[:, i * D:(i + 1) * D], self.wdq_d[:, i * 128:(i + 1) * 128]), 16)
        block.sync(f42_w)
        n_w2 = ld.n

        t2 = st_.n
        s2 = ss.n
        v2 = sv.n
        ldx2 = ld.n
        stx2 = sst.n

        def f42_xld(sync, c):
            sync.wait_ge(ld.h, ldx2 + c * 16)
            if c >= 2:
                sync.wait_ge(sst.h, stx2 + (c - 1) * 16)
            ld.inc(sync.dma_start(
                out=xc[c % 2][:],
                in_=self.xd.rearrange("(c p) d -> p c d", p=128)[:, c, :]), 16)

        def f42_xst(sync, c):
            sync.wait_ge(sst.h, stx2 + c * 16)
            sync.wait_ge(sv.h, v2 + c + 1)
            sst.inc(sync.dma_start(
                out=self.xd.rearrange("(c p) d -> p c d", p=128)[:, c, :],
                in_=xc[c % 2][:]), 16)

        for c in range(TC):
            if c >= 2:
                block.sync(lambda sync, c=c: f42_xst(sync, c - 2))
            block.sync(lambda sync, c=c: f42_xld(sync, c))
        block.sync(lambda sync: f42_xst(sync, TC - 2))
        block.sync(lambda sync: f42_xst(sync, TC - 1))

        def f42_t(t):
            t.wait_ge(ld.h, n_w2)
            for c in range(TC):
                if c >= 2:
                    t.wait_ge(ss.h, s2 + (c - 1) * 2)
                pa, pbk = pb[(c % 2) * 2], pb[(c % 2) * 2 + 1]
                for i in range(22):
                    i1 = t.matmul(pa[:], hqT[:, i * T + c * 128:i * T + (c + 1) * 128],
                                  wdT[:, i * D:i * D + 512], start=(i == 0), stop=(i == 21))
                st_.inc(i1)
                for i in range(22):
                    i2 = t.matmul(pbk[:], hqT[:, i * T + c * 128:i * T + (c + 1) * 128],
                                  wdT[:, i * D + 512:(i + 1) * D], start=(i == 0), stop=(i == 21))
                st_.inc(i2)
        block.tensor(f42_t)

        def f42_s(s):
            for c in range(TC):
                s.wait_ge(st_.h, t2 + (c + 1) * 2)
                if c >= 2:
                    s.wait_ge(sv.h, v2 + (c - 1))
                pa, pbk = pb[(c % 2) * 2], pb[(c % 2) * 2 + 1]
                s.activation(obuf[c % 2][:, 0:512], pa[:], AF.Copy)
                ss.inc(s.activation(obuf[c % 2][:, 512:1024], pbk[:], AF.Copy))
                ss.inc(s.nop())
        block.scalar(f42_s)

        def f42_v(v):
            for c in range(TC):
                v.wait_ge(ss.h, s2 + (c + 1) * 2)
                v.wait_ge(ld.h, ldx2 + (c + 1) * 16)
                v.drain()
                sv.inc(v.scalar_tensor_tensor(xc[c % 2][:], obuf[c % 2][:], 0.5,
                                              xc[c % 2][:], ALU.mult, ALU.add))
        block.vector(f42_v)

        self.barrier()
        ph2.close()
        nc.leave_named_scope("f42", scope_f42[0] if isinstance(scope_f42, tuple) else scope_f42, False)

    def _emit_attn(self):
        nc, block = self.nc, self.block
        ld, sv, ss, st_, sst, sg = (self.sems[k] for k in ["ld", "v", "s", "t", "st", "g"])
        pb = self.pb

        phh = ExitStack()
        h_tm = phh.enter_context(nc.sbuf_tensor(self.name("h_tm"), [128, TC * D], FP16))
        hT = phh.enter_context(nc.sbuf_tensor(self.name("hT"), [128, 8 * T], FP16))
        ph0 = ExitStack()
        xsb = ph0.enter_context(nc.sbuf_tensor(self.name("atx"), [128, TC * D], F32))
        self._load_x(xsb)
        self._emit_ln(xsb, h_tm)
        n_ln = sv.n

        def a0_stg(sync):
            sync.wait_ge(sv.h, n_ln)
            sst.inc(sync.dma_start(
                out=self.stg_td16.rearrange("(c p) d -> p c d", p=128),
                in_=h_tm[:].rearrange("p (c d) -> p c d", c=TC)), 16)
            sync.wait_ge(sst.h, sst.n)
            for k in range(8):
                sst.inc(sync.dma_start_transpose(
                    hT[:, k * T:(k + 1) * T], self.stg_td16[:, k * 128:(k + 1) * 128]), 16)
        block.sync(a0_stg)
        n_hT = sst.n
        self.barrier()
        ph0.close()

        ph1 = ExitStack()
        sbl = lambda nm, shape, dt: ph1.enter_context(nc.sbuf_tensor(self.name(nm), shape, dt))
        Gb = sbl("Gb", [128, 8 * D], FP16)
        qTb = sbl("qTb", [128, 8 * D], FP16)
        qfm = sbl("qfm", [128, 8 * T], FP16)
        obuf = [sbl("ato", [128, D], F32) for _ in range(2)]
        xc = [sbl("atxc", [128, D], F32) for _ in range(2)]

        g1 = sg.n

        def a_qw(g):
            sg.inc(g.dma_start(
                out=qTb[:].rearrange("p (k d) -> p k d", k=8),
                in_=self.qT_in.rearrange("(k p) d -> p k d", p=128)), 16)
        block.gpsimd(a_qw)

        t1 = st_.n
        s1 = ss.n

        def a1_t(t):
            for d1 in range(8):
                for n2 in range(2):
                    bk = pb[(d1 * 2 + n2) % 8]
                    if d1 * 2 + n2 >= 8:
                        t.wait_ge(ss.h, s1 + (d1 * 2 + n2) - 8 + 1)
                    for tk in range(TC):
                        i = t.matmul(bk[:], h_tm[:, tk * D + d1 * 128:tk * D + (d1 + 1) * 128],
                                     h_tm[:, tk * D + n2 * 512:tk * D + (n2 + 1) * 512],
                                     start=(tk == 0), stop=(tk == TC - 1))
                    st_.inc(i)
            t.wait_ge(sg.h, g1 + 16)
            t.wait_ge(sst.h, n_hT)
            t.wait_ge(ss.h, s1 + 16)
            for e in range(8):
                for tt in range(4):
                    bk = pb[(e % 2) * 4 + tt]
                    if e >= 2:
                        t.wait_ge(ss.h, s1 + 16 + (e - 1) * 4)
                    for k in range(8):
                        i = t.matmul(bk[:], qTb[:, k * D + e * 128:k * D + (e + 1) * 128],
                                     hT[:, k * T + tt * 512:k * T + (tt + 1) * 512],
                                     start=(k == 0), stop=(k == 7))
                    st_.inc(i)
        block.tensor(a1_t)

        def a1_s(s):
            for d1 in range(8):
                for n2 in range(2):
                    bk = pb[(d1 * 2 + n2) % 8]
                    s.wait_ge(st_.h, t1 + d1 * 2 + n2 + 1)
                    ss.inc(s.activation(Gb[:, d1 * D + n2 * 512:d1 * D + (n2 + 1) * 512],
                                        bk[:], AF.Copy))
            for e in range(8):
                for tt in range(4):
                    bk = pb[(e % 2) * 4 + tt]
                    s.wait_ge(st_.h, t1 + 16 + e * 4 + tt + 1)
                    ss.inc(s.activation(qfm[:, e * T + tt * 512:e * T + (tt + 1) * 512],
                                        bk[:], AF.Copy))
        block.scalar(a1_s)
        n_g_q = ss.n
        self.barrier()

        t3 = st_.n
        s3 = ss.n
        v3 = sv.n
        ldx3 = ld.n
        stx3 = sst.n

        def a3_xld(sync, c):
            sync.wait_ge(ld.h, ldx3 + c * 16)
            if c >= 2:
                sync.wait_ge(sst.h, stx3 + (c - 1) * 16)
            ld.inc(sync.dma_start(
                out=xc[c % 2][:],
                in_=self.xd.rearrange("(c p) d -> p c d", p=128)[:, c, :]), 16)

        def a3_xst(sync, c):
            sync.wait_ge(sst.h, stx3 + c * 16)
            sync.wait_ge(sv.h, v3 + c + 1)
            sst.inc(sync.dma_start(
                out=self.xd.rearrange("(c p) d -> p c d", p=128)[:, c, :],
                in_=xc[c % 2][:]), 16)

        for c in range(TC):
            if c >= 2:
                block.sync(lambda sync, c=c: a3_xst(sync, c - 2))
            block.sync(lambda sync, c=c: a3_xld(sync, c))
        block.sync(lambda sync: a3_xst(sync, TC - 2))
        block.sync(lambda sync: a3_xst(sync, TC - 1))

        def a3_t(t):
            for c in range(TC):
                if c >= 2:
                    t.wait_ge(ss.h, s3 + (c - 1) * 2)
                pa, pbk = pb[(c % 2) * 2], pb[(c % 2) * 2 + 1]
                for e in range(8):
                    i1 = t.matmul(pa[:], qfm[:, e * T + c * 128:e * T + (c + 1) * 128],
                                  Gb[:, e * D:e * D + 512], start=(e == 0), stop=(e == 7))
                st_.inc(i1)
                for e in range(8):
                    i2 = t.matmul(pbk[:], qfm[:, e * T + c * 128:e * T + (c + 1) * 128],
                                  Gb[:, e * D + 512:(e + 1) * D], start=(e == 0), stop=(e == 7))
                st_.inc(i2)
        block.tensor(a3_t)

        def a3_s(s):
            for c in range(TC):
                s.wait_ge(st_.h, t3 + (c + 1) * 2)
                if c >= 2:
                    s.wait_ge(sv.h, v3 + (c - 1))
                pa, pbk = pb[(c % 2) * 2], pb[(c % 2) * 2 + 1]
                s.activation(obuf[c % 2][:, 0:512], pa[:], AF.Copy)
                ss.inc(s.activation(obuf[c % 2][:, 512:1024], pbk[:], AF.Copy))
                ss.inc(s.nop())
        block.scalar(a3_s)

        def a3_v(v):
            for c in range(TC):
                v.wait_ge(ss.h, s3 + (c + 1) * 2)
                v.wait_ge(ld.h, ldx3 + (c + 1) * 16)
                v.drain()
                sv.inc(v.tensor_tensor(xc[c % 2][:], xc[c % 2][:], obuf[c % 2][:], ALU.add))
        block.vector(a3_v)

        self.barrier()
        ph1.close()
        phh.close()

    def _emit_fin(self):
        nc, block = self.nc, self.block
        ld, sfin, sst = self.sems["ld"], self.sems["fin"], self.sems["st"]
        ph = ExitStack()
        fbuf = ph.enter_context(nc.sbuf_tensor("fbuf", [128, TC * D], F32))

        def fin_sync(sync):
            ld.inc(sync.dma_start(
                out=fbuf[:].rearrange("p (c d) -> p c d", c=TC),
                in_=self.xd.rearrange("(c p) d -> p c d", p=128)), 16)
            sync.wait_ge(ld.h, ld.n)
            sfin.inc(sync.dma_start(
                out=self.out.rearrange("(c p) d -> p c d", p=128),
                in_=fbuf[:].rearrange("p (c d) -> p c d", c=TC)), 16)
            sync.wait_ge(sfin.h, sfin.n)
        block.sync(fin_sync)
        ph.close()


def prep_core_inputs(xb, W):
    f = np.ascontiguousarray
    cvec = lambda a, nch: f(np.asarray(a, np.float32).reshape(nch, 128).T)
    consts = np.zeros((128, 8), np.float32)
    consts[:, C_EPS] = 1e-5
    consts[:, C_I1024] = 1.0 / 1024
    consts[:, C_HALF] = 0.5
    consts[:, C_QUART] = 0.25
    consts[:, C_ONE] = 1.0
    return dict(
        x=f(np.asarray(xb, np.float32)),
        winT=f(np.asarray(W["L0_attn_in_proj_w"], np.float32).T),
        inb=cvec(W["L0_attn_in_proj_b"], 32),
        convw=f(np.asarray(W["L0_attn_conv_w"], np.float32).reshape(16, 128, 4)
                .transpose(1, 0, 2).reshape(128, 64)),
        convb=cvec(W["L0_attn_conv_b"], 16),
        dtwT=f(np.asarray(W["L0_attn_dt_w"], np.float32).T),
        bwT=f(np.asarray(W["L0_attn_b_w"], np.float32).T),
        cwT=f(np.asarray(W["L0_attn_c_w"], np.float32).T),
        dtb=cvec(W["L0_attn_dt_b"], 16),
        alog=cvec(W["L0_attn_A_log"], 16),
        dpar=cvec(W["L0_attn_D_param"], 16),
        outwT=f(np.asarray(W["L0_attn_out_w"], np.float32).T),
        outb_rep=f(np.broadcast_to(np.asarray(W["L0_attn_out_b"], np.float32), (128, D))),
        kern=f(np.asarray(W["L0_ffn_kernel"])),
        sin_c=f(np.broadcast_to(np.asarray(W["L0_ffn_input_scale"], np.float32).reshape(1, 1),
                                (128, 1))),
        skl_c=f(np.broadcast_to(np.asarray(W["L0_ffn_kernel_scale"], np.float32).reshape(1, 1),
                                (128, 1))),
        fbias_rep=f(np.broadcast_to(np.asarray(W["L0_ffn_bias"]), (128, D))),
        qT=f(np.asarray(W["L1_attn_queries"], np.float32).T),
        wg=f(np.asarray(W["L1_ffn_w_gate"], np.float32)),
        wu=f(np.asarray(W["L1_ffn_w_up"], np.float32)),
        wd=f(np.asarray(W["L1_ffn_w_down"], np.float32)),
        consts=consts,
        ident=np.eye(128, dtype=np.float16),
    )


def run(inputs, n_stages=6, n_cores=8):
    nc = Builder(n_stages).build()
    x = np.asarray(inputs["L0_attn_x"])
    in_maps = [prep_core_inputs(x[b], inputs) for b in range(n_cores)]
    res = run_bass_kernel_spmd(nc, in_maps, core_ids=list(range(n_cores)))
    return np.stack([r["out"] for r in res.results], axis=0)


def kernel(**inputs):
    nc = Builder(6).build()
    x = np.asarray(inputs["L0_attn_x"])
    in_maps = [prep_core_inputs(x[b], inputs) for b in range(8)]
    res = run_bass_kernel_spmd(nc, in_maps, core_ids=list(range(8)))
    return np.stack([np.asarray(r["out"]) for r in res.results], axis=0)


# version 14
# speedup vs baseline: 1.0323x; 1.0323x over previous
import numpy as np
import ml_dtypes
import concourse.bass as bass
import concourse.mybir as mybir
from concourse.bass_utils import run_bass_kernel_spmd
from contextlib import ExitStack

F32 = mybir.dt.float32
BF16 = mybir.dt.bfloat16
FP16 = mybir.dt.float16
FP8 = mybir.dt.float8e4
AF = mybir.ActivationFunctionType
ALU = mybir.AluOpType
AX = mybir.AxisListType

T = 2048
D = 1024
TC = T // 128
DI = 2048
E4 = 4096
I1 = 2816
IC = I1 // 128

C_EPS, C_I1024, C_HALF, C_QUART, C_ONE = 0, 1, 2, 3, 4


class Sem:
    def __init__(self, nc, es, name):
        self.h = es.enter_context(nc.semaphore(name))
        self.n = 0

    def inc(self, instr, k=1):
        instr.then_inc(self.h, k)
        self.n += k
        return instr


class Builder:
    def __init__(self, n_stages=6, wq_force=False):
        self.n_stages = n_stages
        self.wq_on = False
        self.wq_force = wq_force
        self.uid = 0

    def name(self, tag):
        self.uid += 1
        return f"{tag}_{self.uid}"

    def build(self):
        nc = bass.Bass("TRN2", target_bir_lowering=False)
        self.nc = nc
        dram = lambda name, shape, dt, kind: nc.dram_tensor(name, shape, dt, kind=kind).ap()
        IN, OUT, SC = "ExternalInput", "ExternalOutput", "Internal"

        self.x_in = dram("x", [T, D], F32, IN)
        self.winT = dram("winT", [D, E4], F32, IN)
        self.inb = dram("inb", [128, 32], F32, IN)
        self.convw = dram("convw", [128, 64], F32, IN)
        self.convb = dram("convb", [128, 16], F32, IN)
        self.dtwT = dram("dtwT", [DI, DI], F32, IN)
        self.bwT = dram("bwT", [DI, DI], F32, IN)
        self.cwT = dram("cwT", [DI, DI], F32, IN)
        self.dtb = dram("dtb", [128, 16], F32, IN)
        self.alog = dram("alog", [128, 16], F32, IN)
        self.dpar = dram("dpar", [128, 16], F32, IN)
        self.outwT = dram("outwT", [DI, D], F32, IN)
        self.outb_rep = dram("outb_rep", [128, D], F32, IN)
        self.kern_in = dram("kern", [D, D], BF16, IN)
        self.sin_c = dram("sin_c", [128, 1], F32, IN)
        self.skl_c = dram("skl_c", [128, 1], F32, IN)
        self.fbias_rep = dram("fbias_rep", [128, D], BF16, IN)
        self.qT_in = dram("qT", [D, D], F32, IN)
        self.wg_in = dram("wg", [I1, D], F32, IN)
        self.wu_in = dram("wu", [I1, D], F32, IN)
        self.wd_in = dram("wd", [D, I1], F32, IN)
        self.consts = dram("consts", [128, 8], F32, IN)
        self.ident_in = dram("ident", [128, 128], FP16, IN)

        self.out = dram("out", [T, D], F32, OUT)

        self.xd = dram("xd", [T, D], F32, SC)
        self.stg_td = dram("stg_td", [T, D], BF16, SC)
        self.stg_td16 = dram("stg_td16", [T, D], FP16, SC)
        self.stg_ti = dram("stg_ti", [T, I1], BF16, SC)
        self.stg_y2 = dram("stg_y2", [DI, T], FP16, SC)
        self.stg_tz = dram("stg_tz", [DI, T], FP16, SC)
        self.stg_b = dram("stg_b", [DI, T], FP16, SC)
        self.stg_c = dram("stg_c", [DI, T], FP16, SC)
        self.wgq_d = dram("wgq_d", [I1, D], FP16, SC)
        self.wuq_d = dram("wuq_d", [I1, D], FP16, SC)
        self.wdq_d = dram("wdq_d", [D, I1], FP16, SC)
        self.hq_d = dram("hq_d", [T, I1], FP16, SC)

        es = ExitStack()
        self.es = es
        with es:
            self._alloc_persistent()
            self.block = es.enter_context(nc.Block())
            with nc.named_scope("p0"):
                self._emit_p0()
            if self.n_stages >= 1:
                with nc.named_scope("ffn8a"):
                    self._emit_ffn8()
            if self.n_stages >= 2:
                self.wq_on = self.n_stages >= 4 or self.wq_force
                with nc.named_scope("mixer"):
                    self._emit_mixer()
            if self.n_stages >= 3:
                with nc.named_scope("ffn8b"):
                    self._emit_ffn8()
            if self.n_stages >= 4:
                with nc.named_scope("ffn4a"):
                    self._emit_ffn4()
            if self.n_stages >= 5:
                with nc.named_scope("attn"):
                    self._emit_attn()
            if self.n_stages >= 6:
                with nc.named_scope("ffn4b"):
                    self._emit_ffn4()
            with nc.named_scope("fin"):
                self._emit_fin()
        return nc

    def _alloc_persistent(self):
        nc, es = self.nc, self.es
        sb = lambda nm, shape, dt: es.enter_context(nc.sbuf_tensor(nm, shape, dt))
        self.cst = sb("cst", [128, 8], F32)
        self.inb_t = sb("inb_t", [128, 32], F32)
        self.convw_t = sb("convw_t", [128, 64], F32)
        self.convb_t = sb("convb_t", [128, 16], F32)
        self.dtb_t = sb("dtb_t", [128, 16], F32)
        self.alog_t = sb("alog_t", [128, 16], F32)
        self.negA = sb("negA", [128, 16], F32)
        self.nb2 = sb("nb2", [128, 16], F32)
        self.hbz = sb("hbz", [128, 16], F32)
        self.dpar_t = sb("dpar_t", [128, 16], F32)
        self.outb_t = sb("outb_t", [128, D], F32)
        self.fbias_t = sb("fbias_t", [128, D], BF16)
        self.sin_t = sb("sin_t", [128, 1], F32)
        self.skl_t = sb("skl_t", [128, 1], F32)
        self.rs_in = sb("rs_in", [128, 1], F32)
        self.rs_k = sb("rs_k", [128, 1], F32)
        self.sprod = sb("sprod", [128, 1], F32)
        self.stat = sb("stat", [128, 96], F32)
        self.tmp16 = sb("tmp16", [128, 16], F32)
        self.ident = sb("ident_sb", [128, 128], FP16)
        self.pb = [es.enter_context(nc.psum_tensor(f"pb{i}", [128, 512], F32))
                   for i in range(8)]
        self.sems = {n: Sem(nc, es, n) for n in
                     ["ld", "v", "s", "t", "st", "ph", "fin", "g"]}

    def barrier(self, extra=()):
        sph = self.sems["ph"]
        waits = [(self.sems[k].h, self.sems[k].n) for k in ["ld", "v", "s", "t", "st", "g"]]
        waits += [(s.h, s.n) for s in extra]
        for eng in ["vector", "scalar", "tensor", "sync", "gpsimd"]:
            def body(e):
                for h, n in waits:
                    e.wait_ge(h, n)
                sph.inc(e.nop())
            getattr(self.block, eng)(body)
        n = sph.n
        for eng in ["vector", "scalar", "tensor", "sync", "gpsimd"]:
            def body2(e):
                e.wait_ge(sph.h, n)
            getattr(self.block, eng)(body2)

    def _emit_p0(self):
        nc, block = self.nc, self.block
        ld, sv, ss = self.sems["ld"], self.sems["v"], self.sems["s"]

        def p0_sync(sync):
            for dst, src in [(self.cst, self.consts), (self.inb_t, self.inb),
                             (self.convw_t, self.convw), (self.convb_t, self.convb),
                             (self.dtb_t, self.dtb), (self.alog_t, self.alog),
                             (self.dpar_t, self.dpar), (self.outb_t, self.outb_rep),
                             (self.fbias_t, self.fbias_rep), (self.sin_t, self.sin_c),
                             (self.skl_t, self.skl_c)]:
                ld.inc(sync.dma_start(out=dst[:], in_=src), 16)
            ld.inc(sync.dma_start(out=self.ident[:], in_=self.ident_in), 16)
            ld.inc(sync.dma_start(out=self.xd, in_=self.x_in), 16)
            sync.wait_ge(ld.h, ld.n)
        block.sync(p0_sync)
        n_ld = ld.n

        def p0_scalar(s):
            s.wait_ge(ld.h, n_ld)
            ss.inc(s.activation(self.tmp16[:], self.alog_t[:], AF.Exp))
        block.scalar(p0_scalar)
        n_s = ss.n

        def p0_vector(v):
            v.wait_ge(ld.h, n_ld)
            v.reciprocal(self.rs_in[:], self.sin_t[:])
            v.reciprocal(self.rs_k[:], self.skl_t[:])
            v.tensor_tensor(self.sprod[:], self.sin_t[:], self.skl_t[:], ALU.mult)
            v.tensor_scalar(self.hbz[:], self.inb_t[:, 16:32], 0.5, None, ALU.mult)
            v.wait_ge(ss.h, n_s)
            v.tensor_scalar(self.nb2[:], self.tmp16[:], -1e-4, None, ALU.mult)
            sv.inc(v.tensor_scalar(self.negA[:], self.tmp16[:], -1.0, None, ALU.mult))
            v.wait_ge(sv.h, sv.n)
        block.vector(p0_vector)
        self.n_consts_ready = sv.n
        self.barrier()

    def _emit_ln(self, xsb, xn_out):
        block = self.block
        sv, ss = self.sems["v"], self.sems["s"]
        stat, cst = self.stat, self.cst

        def vec1(v):
            for c in range(TC):
                xc = xsb[:, c * D:(c + 1) * D]
                v.tensor_reduce(stat[:, c:c + 1], xc, AX.X, ALU.add)
                v.scalar_tensor_tensor(xn_out[:, c * D:(c + 1) * D], xc, 1.0, xc,
                                       ALU.mult, ALU.mult,
                                       accum_out=stat[:, 16 + c:17 + c])
            v.drain()
            v.tensor_scalar(stat[:, 32:48], stat[:, 0:16], 1.0 / D, None, ALU.mult)
            v.drain()
            v.tensor_tensor(stat[:, 64:80], stat[:, 32:48], stat[:, 32:48], ALU.mult)
            v.drain()
            sv.inc(v.scalar_tensor_tensor(stat[:, 48:64], stat[:, 16:32], 1.0 / D,
                                          stat[:, 64:80], ALU.mult, ALU.subtract))
        block.vector(vec1)
        n_v1 = sv.n

        def sc1(s):
            s.wait_ge(sv.h, n_v1)
            ss.inc(s.activation(stat[:, 80:96], stat[:, 48:64], AF.Sqrt,
                                bias=cst[:, C_EPS:C_EPS + 1]))
        block.scalar(sc1)
        n_s1 = ss.n

        def vec2(v):
            v.wait_ge(ss.h, n_s1)
            sv.inc(v.reciprocal(stat[:, 48:64], stat[:, 80:96]))
            v.wait_ge(sv.h, sv.n)
            for c in range(TC):
                v.tensor_scalar(xn_out[:, c * D:(c + 1) * D], xsb[:, c * D:(c + 1) * D],
                                stat[:, 32 + c:33 + c], stat[:, 48 + c:49 + c],
                                ALU.subtract, ALU.mult)
            v.drain()
            sv.inc(v.nop())
        block.vector(vec2)
        return sv.n

    def _load_x(self, xsb):
        ld = self.sems["ld"]

        sst = self.sems["st"]

        n0 = ld.n

        def ldx(sync):
            sync.wait_ge(sst.h, sst.n)
            sync.wait_ge(ld.h, n0)
            ld.inc(sync.dma_start(
                out=xsb[:].rearrange("p (c d) -> p c d", c=TC),
                in_=self.xd.rearrange("(c p) d -> p c d", p=128)), 16)
        self.block.sync(ldx)
        n = ld.n

        def vwait(v):
            v.wait_ge(ld.h, n)
        self.block.vector(vwait)

    def _store_x(self, xsb, n_ready, to_out=False):
        sst = self.sems["st"]
        sv = self.sems["v"]
        dst = self.out if to_out else self.xd

        def stx(sync):
            sync.wait_ge(sv.h, n_ready)
            sst.inc(sync.dma_start(
                out=dst.rearrange("(c p) d -> p c d", p=128),
                in_=xsb[:].rearrange("p (c d) -> p c d", c=TC)), 16)
        self.block.sync(stx)
        return sst.n

    def _emit_ffn8(self):
        nc, block = self.nc, self.block
        ld, sv, ss, st_, sst = (self.sems[k] for k in ["ld", "v", "s", "t", "st"])
        ph = ExitStack()
        sbl = lambda nm, shape, dt: ph.enter_context(nc.sbuf_tensor(self.name(nm), shape, dt))
        xsb = sbl("x", [128, TC * D], F32)
        xnb = sbl("xn", [128, TC * D], BF16)
        q8 = sbl("q8", [128, TC * D], FP8)
        xq8T = sbl("xq8T", [128, 8 * T], BF16)
        ybuf = sbl("y", [128, 2 * D], BF16)
        y2buf = sbl("y2", [128, 2 * D], BF16)
        k8b = sbl("k8b", [128, 8 * D], BF16)
        kf8 = sbl("kf8", [128, 8 * D], FP8)
        pb = self.pb

        ldk0 = ld.n

        def k8_ld(sync):
            sync.wait_ge(ld.h, ldk0)
            ld.inc(sync.dma_start(
                out=k8b[:].rearrange("p (k d) -> p k d", k=8),
                in_=self.kern_in.rearrange("(k p) d -> p k d", p=128)), 16)
        block.sync(k8_ld)
        n_k8ld = ld.n

        self._load_x(xsb)
        self._emit_ln(xsb, xnb)

        def vq(v):
            v.drain()
            sv.inc(v.tensor_scalar(q8[:], xnb[:], self.rs_in[:, 0:1], None, ALU.mult))
        block.vector(vq)
        n_vq = sv.n

        def k8_v(v):
            v.wait_ge(ld.h, n_k8ld)
            v.drain()
            sv.inc(v.tensor_scalar(kf8[:], k8b[:], self.rs_k[:, 0:1], None, ALU.mult))
        block.vector(k8_v)
        n_k8v = sv.n

        def sq_(s):
            s.wait_ge(sv.h, n_vq)
            ss.inc(s.activation(xnb[:], q8[:], AF.Copy))
            s.wait_ge(sv.h, n_k8v)
            s.drain()
            ss.inc(s.activation(k8b[:], kf8[:], AF.Copy))
        block.scalar(sq_)
        n_sq = ss.n
        n_k8 = ss.n

        def stg(sync):
            sync.wait_ge(ss.h, n_sq)
            sst.inc(sync.dma_start(
                out=self.stg_td.rearrange("(c p) d -> p c d", p=128),
                in_=xnb[:].rearrange("p (c d) -> p c d", c=TC)), 16)
            sync.wait_ge(sst.h, sst.n)
            for k in range(8):
                sst.inc(sync.dma_start_transpose(
                    xq8T[:, k * T:(k + 1) * T], self.stg_td[:, k * 128:(k + 1) * 128]), 16)
        block.sync(stg)
        n_stg = sst.n

        evac_base = ss.n
        mm_base = st_.n


        def mm(t):
            t.wait_ge(sst.h, n_stg)
            t.wait_ge(ss.h, n_k8)
            for c in range(TC):
                if c >= 2:
                    t.wait_ge(ss.h, evac_base + (c - 2) * 2 + 2)
                pa, pbk = pb[(c % 2) * 2], pb[(c % 2) * 2 + 1]
                for k in range(8):
                    i1 = t.matmul(pa[:], xq8T[:, k * T + c * 128: k * T + (c + 1) * 128],
                                  k8b[:, k * D:k * D + 512],
                                  start=(k == 0), stop=(k == 7))
                st_.inc(i1)
                for k in range(8):
                    i2 = t.matmul(pbk[:], xq8T[:, k * T + c * 128: k * T + (c + 1) * 128],
                                  k8b[:, k * D + 512:(k + 1) * D],
                                  start=(k == 0), stop=(k == 7))
                st_.inc(i2)
        block.tensor(mm)

        vres_base = sv.n

        def ev(s):
            for c in range(TC):
                s.wait_ge(st_.h, mm_base + (c + 1) * 2)
                if c >= 2:
                    s.wait_ge(sv.h, vres_base + (c - 1))
                pa, pbk = pb[(c % 2) * 2], pb[(c % 2) * 2 + 1]
                yb = ybuf[:, (c % 2) * D:(c % 2 + 1) * D]
                s.activation(yb[:, 0:512], pa[:], AF.Copy, scale=self.sprod[:, 0:1])
                ss.inc(s.activation(yb[:, 512:1024], pbk[:], AF.Copy,
                                    scale=self.sprod[:, 0:1]))
                ss.inc(s.nop())
        block.scalar(ev)

        def vres(v):
            for c in range(TC):
                v.wait_ge(ss.h, evac_base + c * 2 + 2)
                yb = ybuf[:, (c % 2) * D:(c % 2 + 1) * D]
                y2 = y2buf[:, (c % 2) * D:(c % 2 + 1) * D]
                v.tensor_tensor(y2, yb, self.fbias_t[:], ALU.add)
                v.drain()
                sv.inc(v.scalar_tensor_tensor(
                    xsb[:, c * D:(c + 1) * D], y2, 0.5,
                    xsb[:, c * D:(c + 1) * D], ALU.mult, ALU.add))
        block.vector(vres)

        self._store_x(xsb, sv.n)
        self.barrier()
        ph.close()


    def _emit_mixer(self):
        nc, block = self.nc, self.block
        ld, sv, ss, st_, sst, sg = (self.sems[k] for k in ["ld", "v", "s", "t", "st", "g"])
        pb = self.pb

        sc_m = self.nc.enter_named_scope("m0", False)
        ph = ExitStack()
        sbl = lambda nm, shape, dt: ph.enter_context(nc.sbuf_tensor(self.name(nm), shape, dt))
        xsb = sbl("mx", [128, TC * D], F32)
        xnb = sbl("mxn", [128, TC * D], FP16)
        self._load_x(xsb)
        self._emit_ln(xsb, xnb)
        n_ln = self.sems["v"].n

        def stg0(sync):
            sync.wait_ge(sv.h, n_ln)
            sst.inc(sync.dma_start(
                out=self.stg_td16.rearrange("(c p) d -> p c d", p=128),
                in_=xnb[:].rearrange("p (c d) -> p c d", c=TC)), 16)
        block.sync(stg0)
        self.barrier()
        ph.close()
        self.nc.leave_named_scope("m0", sc_m[0], False)
        sc_m = self.nc.enter_named_scope("m1", False)

        ph_u = ExitStack()
        ubf = ph_u.enter_context(nc.sbuf_tensor(self.name("ubf"), [128, 16 * T], FP16))
        if self.wq_on:
            self._wq_init()
        ph = ExitStack()
        sbl = lambda nm, shape, dt: ph.enter_context(nc.sbuf_tensor(self.name(nm), shape, dt))
        xnT = sbl("xnT", [128, 8 * T], FP16)
        wsl = [sbl("wsl", [128, 8 * 128], FP16) for _ in range(2)]
        xzpad = sbl("xzpad", [128, 3 + T], FP16)
        cacc = [sbl("cacc", [128, T], F32) for _ in range(2)]
        thb = [sbl("thb", [128, T], FP16) for _ in range(2)]
        tzc = [sbl("tzc", [128, T], FP16) for _ in range(2)]
        sem_evac = Sem(nc, ph, self.name("m1ev"))
        sem_th = Sem(nc, ph, self.name("m1th"))
        sem_tz = Sem(nc, ph, self.name("m1tz"))
        sem_conv = Sem(nc, ph, self.name("m1cv"))
        sem_u = Sem(nc, ph, self.name("m1u"))

        def m1_tload(sync):
            sync.wait_ge(sst.h, sst.n)
            for k in range(8):
                sst.inc(sync.dma_start_transpose(
                    xnT[:, k * T:(k + 1) * T], self.stg_td16[:, k * 128:(k + 1) * 128]), 16)
        block.sync(m1_tload)
        n_xnT = sst.n

        g_base = sg.n
        t_base = st_.n
        tzst_base = sst.n

        for e in range(32):
            def m1_g(g, e=e):
                g.wait_ge(sg.h, g_base + e * 16)
                if e >= 2:
                    g.wait_ge(st_.h, t_base + (e - 1) * 4)
                sg.inc(g.dma_start(
                    out=wsl[e % 2][:].rearrange("p (k e) -> p k e", k=8),
                    in_=self.winT.rearrange("(k p) e -> p k e", p=128)[:, :, e * 128:(e + 1) * 128]), 16)
            block.gpsimd(m1_g)

        def m1_t(t):
            t.wait_ge(sst.h, n_xnT)
            for e in range(32):
                t.wait_ge(sg.h, g_base + (e + 1) * 16)
                if e >= 2:
                    if e < 16 + 2:
                        pass
                    if (e - 2) < 16:
                        t.wait_ge(sem_evac.h, (e - 2) + 1)
                    else:
                        t.wait_ge(sem_tz.h, ((e - 2) - 16 + 1) * 4)
                w = wsl[e % 2]
                quad = [pb[(e % 2) * 4 + i] for i in range(4)]
                for tt in range(4):
                    for k in range(8):
                        i = t.matmul(quad[tt][:],
                                     w[:, k * 128:(k + 1) * 128],
                                     xnT[:, k * T + tt * 512:(k * T) + (tt + 1) * 512],
                                     start=(k == 0), stop=(k == 7))
                    st_.inc(i)
        block.tensor(m1_t)

        def v_init(v):
            v.memset(xzpad[:, 0:3], 0.0)
            v.drain()
        block.vector(v_init)

        for e in range(16):
            def s_x(s, e=e):
                s.wait_ge(st_.h, t_base + (e + 1) * 4)
                if e >= 1:
                    s.wait_ge(sem_conv.h, e)
                quad = [pb[(e % 2) * 4 + i] for i in range(4)]
                for tt in range(4):
                    s.activation(xzpad[:, 3 + tt * 512:3 + (tt + 1) * 512],
                                 quad[tt][:], AF.Identity, bias=self.inb_t[:, e:e + 1])
                s.drain()
                sem_evac.inc(s.nop())
                if e >= 1:
                    ee = e - 1
                    if ee >= 2:
                        s.wait_ge(sem_u.h, ee - 1)
                    s.drain()
                    sem_th.inc(s.activation(thb[ee % 2][:], cacc[ee % 2][:], AF.Tanh,
                                            scale=self.cst[:, C_HALF:C_HALF + 1]))
            block.scalar(s_x)

            def v_conv(v, e=e):
                v.wait_ge(sem_evac.h, e + 1)
                if e >= 2:
                    v.wait_ge(sem_th.h, e - 1)
                ca = cacc[e % 2]
                v.drain()
                v.tensor_scalar(ca[:], xzpad[:, 0:T], self.convw_t[:, 4 * e:4 * e + 1],
                                self.convb_t[:, e:e + 1], ALU.mult, ALU.add)
                for tau in range(1, 4):
                    v.drain()
                    i = v.scalar_tensor_tensor(ca[:], xzpad[:, tau:tau + T],
                                               self.convw_t[:, 4 * e + tau:4 * e + tau + 1],
                                               ca[:], ALU.mult, ALU.add)
                sem_conv.inc(i)
                if e >= 1:
                    ee = e - 1
                    v.wait_ge(sem_th.h, ee + 1)
                    sem_u.inc(v.scalar_tensor_tensor(ubf[:, ee * T:(ee + 1) * T], thb[ee % 2][:],
                                                     1.0, cacc[ee % 2][:], ALU.add, ALU.mult))
            block.vector(v_conv)

        def s_thf(s):
            s.wait_ge(sem_conv.h, 16)
            s.drain()
            sem_th.inc(s.activation(thb[15 % 2][:], cacc[15 % 2][:], AF.Tanh,
                                    scale=self.cst[:, C_HALF:C_HALF + 1]))
        block.scalar(s_thf)

        def v_uf(v):
            v.wait_ge(sem_th.h, 16)
            sem_u.inc(v.scalar_tensor_tensor(ubf[:, 15 * T:16 * T], thb[15 % 2][:],
                                             1.0, cacc[15 % 2][:], ALU.add, ALU.mult))
            v.drain()
            sv.inc(v.nop())
        block.vector(v_uf)

        for e in range(16, 32):
            def s_z(s, e=e):
                s.wait_ge(st_.h, t_base + (e + 1) * 4)
                if e >= 18:
                    s.wait_ge(sst.h, tzst_base + (e - 16 - 1) * 16)
                quad = [pb[(e % 2) * 4 + i] for i in range(4)]
                for tt in range(4):
                    sem_tz.inc(s.activation(tzc[e % 2][:, tt * 512:(tt + 1) * 512], quad[tt][:],
                                            AF.Tanh, scale=self.cst[:, C_HALF:C_HALF + 1],
                                            bias=self.hbz[:, e - 16:e - 15]))
            block.scalar(s_z)

            def m1_tzstore(sync, e=e):
                sync.wait_ge(sst.h, tzst_base + (e - 16) * 16)
                sync.wait_ge(sem_tz.h, (e - 16 + 1) * 4)
                sst.inc(sync.dma_start(
                    out=self.stg_tz[(e - 16) * 128:(e - 15) * 128, :],
                    in_=tzc[e % 2][:]), 16)
            block.sync(m1_tzstore)
            if self.wq_on:
                self._wq_step()

        self.barrier()
        ph.close()
        self.nc.leave_named_scope("m1", sc_m[0], False)
        sc_m = self.nc.enter_named_scope("m2", False)

        ph2 = ExitStack()
        sbl2 = lambda nm, shape, dt: ph2.enter_context(nc.sbuf_tensor(self.name(nm), shape, dt))
        bsl = [sbl2("bsl", [128, 16 * 128], FP16) for _ in range(2)]
        bcc = [sbl2("bcc", [128, T], FP16) for _ in range(2)]

        for wsrc, dsts in [(self.bwT, self.stg_b), (self.cwT, self.stg_c)]:
            g2 = sg.n
            t2 = st_.n
            s2 = ss.n
            d2 = sst.n

            def m2_t(t, g2=g2, s2=s2, t2=t2):
                for e in range(16):
                    t.wait_ge(sg.h, g2 + (e + 1) * 16)
                    t.wait_ge(ss.h, s2 + max(0, e - 1) * 4)
                    w = bsl[e % 2]
                    quad = [pb[(e % 2) * 4 + i] for i in range(4)]
                    for tt in range(4):
                        for k in range(16):
                            i = t.matmul(quad[tt][:], w[:, k * 128:(k + 1) * 128],
                                         ubf[:, k * T + tt * 512: k * T + (tt + 1) * 512],
                                         start=(k == 0), stop=(k == 15))
                        st_.inc(i)
            block.tensor(m2_t)

            for e in range(16):
                def m2_g(g, wsrc=wsrc, t2=t2, g2=g2, e=e):
                    g.wait_ge(sg.h, g2 + e * 16)
                    g.wait_ge(st_.h, t2 + max(0, e - 1) * 4)
                    sg.inc(g.dma_start(
                        out=bsl[e % 2][:].rearrange("p (k e) -> p k e", k=16),
                        in_=wsrc.rearrange("(k p) e -> p k e", p=128)[:, :, e * 128:(e + 1) * 128]), 16)
                block.gpsimd(m2_g)

                def m2_s_e(s, t2=t2, d2=d2, e=e):
                    s.wait_ge(st_.h, t2 + (e + 1) * 4)
                    s.wait_ge(sst.h, d2 + max(0, e - 1) * 16)
                    quad = [pb[(e % 2) * 4 + i] for i in range(4)]
                    for tt in range(4):
                        ss.inc(s.activation(bcc[e % 2][:, tt * 512:(tt + 1) * 512],
                                            quad[tt][:], AF.Tanh,
                                            scale=self.cst[:, C_HALF:C_HALF + 1]))
                block.scalar(m2_s_e)

                def m2_st(sync, s2=s2, dsts=dsts, d2=d2, e=e):
                    sync.wait_ge(sst.h, d2 + e * 16)
                    sync.wait_ge(ss.h, s2 + (e + 1) * 4)
                    sst.inc(sync.dma_start(out=dsts[e * 128:(e + 1) * 128, :],
                                           in_=bcc[e % 2][:]), 16)
                block.sync(m2_st)
                if self.wq_on:
                    self._wq_step()
        if self.wq_on:
            self._wq_drain()
            self.barrier(extra=self.wq_extras)
            ph2.close()
            self.wq_ph.close()
        else:
            self.barrier()
            ph2.close()
        self.nc.leave_named_scope("m2", sc_m[0], False)
        sc_m = self.nc.enter_named_scope("m3", False)

        ph3 = ExitStack()
        sbl3 = lambda nm, shape, dt: ph3.enter_context(nc.sbuf_tensor(self.name(nm), shape, dt))
        dsl = [sbl3("dsl", [128, 16 * 128], FP16) for _ in range(2)]
        ebuf = sbl3("ebuf", [128, T], F32)
        spbuf = sbl3("spbuf", [128, T], F32)
        dbuf = [sbl3("dbuf", [128, T], F32) for _ in range(2)]
        ombuf = sbl3("ombuf", [128, T], F32)
        t1buf = sbl3("t1buf", [128, T], F32)
        incb = sbl3("incb", [128, T], F32)
        stsb = sbl3("stsb", [128, T], F32)
        t2buf = sbl3("t2buf", [128, T], F32)
        ypb = sbl3("ypb", [128, T], F32)
        y2c = [sbl3("y2c", [128, T], FP16) for _ in range(2)]
        bld = [sbl3("bld", [128, T], FP16) for _ in range(2)]
        cld = [sbl3("cld", [128, T], FP16) for _ in range(2)]
        tzld = [sbl3("tzld", [128, T], FP16) for _ in range(2)]

        g3 = sg.n
        t3 = st_.n
        s3 = ss.n
        v3 = sv.n
        d3 = sst.n
        ld3 = ld.n

        for e in range(16):
            def m3_g(g, e=e):
                g.wait_ge(sg.h, g3 + e * 16)
                if e >= 2:
                    g.wait_ge(st_.h, t3 + (e - 1) * 4)
                sg.inc(g.dma_start(
                    out=dsl[e % 2][:].rearrange("p (k e) -> p k e", k=16),
                    in_=self.dtwT.rearrange("(k p) e -> p k e", p=128)[:, :, e * 128:(e + 1) * 128]), 16)
            block.gpsimd(m3_g)

        def m3_ld_body(sync, e):
            sync.wait_ge(ld.h, ld3 + e * 48)
            if e >= 2:
                sync.wait_ge(sv.h, v3 + (e - 1) * 7)
            ld.inc(sync.dma_start(out=bld[e % 2][:], in_=self.stg_b[e * 128:(e + 1) * 128, :]), 16)
            ld.inc(sync.dma_start(out=cld[e % 2][:], in_=self.stg_c[e * 128:(e + 1) * 128, :]), 16)
            ld.inc(sync.dma_start(out=tzld[e % 2][:], in_=self.stg_tz[e * 128:(e + 1) * 128, :]), 16)

        def m3_st_body(sync, e):
            sync.wait_ge(sst.h, d3 + e * 16)
            sync.wait_ge(sv.h, v3 + (e + 1) * 7)
            sst.inc(sync.dma_start(out=self.stg_y2[e * 128:(e + 1) * 128, :],
                                   in_=y2c[e % 2][:]), 16)

        for e in range(16):
            block.sync(lambda sync, e=e: m3_ld_body(sync, e))
            if e >= 2:
                block.sync(lambda sync, e=e: m3_st_body(sync, e - 2))
        block.sync(lambda sync: m3_st_body(sync, 14))
        block.sync(lambda sync: m3_st_body(sync, 15))

        def m3_t(t):
            for e in range(16):
                t.wait_ge(sg.h, g3 + (e + 1) * 16)
                if e >= 2:
                    t.wait_ge(ss.h, s3 + (e - 1) * 6)
                w = dsl[e % 2]
                quad = [pb[(e % 2) * 4 + i] for i in range(4)]
                for tt in range(4):
                    for k in range(16):
                        i = t.matmul(quad[tt][:], w[:, k * 128:(k + 1) * 128],
                                     ubf[:, k * T + tt * 512: k * T + (tt + 1) * 512],
                                     start=(k == 0), stop=(k == 15))
                    st_.inc(i)
        block.tensor(m3_t)

        def m3_s(s):
            for e in range(16):
                s.wait_ge(st_.h, t3 + (e + 1) * 4)
                if e >= 2:
                    s.wait_ge(sv.h, v3 + (e - 1) * 7)
                quad = [pb[(e % 2) * 4 + i] for i in range(4)]
                for tt in range(4):
                    i = s.activation(ebuf[:, tt * 512:(tt + 1) * 512], quad[tt][:], AF.Exp,
                                     scale=self.cst[:, C_HALF:C_HALF + 1],
                                     bias=self.dtb[:, e:e + 1]) if False else                         s.activation(ebuf[:, tt * 512:(tt + 1) * 512], quad[tt][:], AF.Exp,
                                     scale=self.cst[:, C_HALF:C_HALF + 1],
                                     bias=self.dtb_t[:, e:e + 1])
                    if tt == 3:
                        ss.inc(i)
                s.drain()
                s.activation(spbuf[:], ebuf[:], AF.Ln, bias=self.cst[:, C_ONE:C_ONE + 1])
                s.drain()
                i = s.activation(dbuf[e % 2][:], spbuf[:], AF.Exp,
                                 scale=self.negA[:, e:e + 1], bias=self.nb2[:, e:e + 1])
                ss.inc(i)
                ss.inc(s.nop()); ss.inc(s.nop()); ss.inc(s.nop()); ss.inc(s.nop())
        block.scalar(m3_s)

        def m3_v(v):
            for e in range(16):
                v.wait_ge(ss.h, s3 + (e + 1) * 6)
                v.wait_ge(ld.h, ld3 + (e + 1) * 48)
                if e >= 2:
                    v.wait_ge(sst.h, d3 + (e - 1) * 16)
                v.tensor_scalar(ombuf[:], dbuf[e % 2][:], -1.0, 1.0, ALU.mult, ALU.add)
                v.tensor_tensor(t1buf[:], bld[e % 2][:], ubf[:, e * T:(e + 1) * T], ALU.mult)
                v.drain()
                v.tensor_tensor(incb[:], ombuf[:], t1buf[:], ALU.mult)
                v.drain()
                v.tensor_tensor_scan(stsb[:], dbuf[e % 2][:], incb[:], 0.0, ALU.mult, ALU.add)
                v.drain()
                v.tensor_tensor(t2buf[:], cld[e % 2][:], stsb[:], ALU.mult)
                v.drain()
                v.scalar_tensor_tensor(ypb[:], ubf[:, e * T:(e + 1) * T],
                                       self.dpar_t[:, e:e + 1], t2buf[:], ALU.mult, ALU.add)
                v.drain()
                i = v.scalar_tensor_tensor(y2c[e % 2][:], tzld[e % 2][:], 1.0,
                                           ypb[:], ALU.add, ALU.mult)
                for _ in range(7):
                    pass
                sv.inc(i)
                sv.inc(v.nop()); sv.inc(v.nop()); sv.inc(v.nop())
                sv.inc(v.nop()); sv.inc(v.nop()); sv.inc(v.nop())
        block.vector(m3_v)



        self.barrier()
        ph3.close()
        ph_u.close()

        self.nc.leave_named_scope("m3", sc_m[0], False)
        sc_m = self.nc.enter_named_scope("m4", False)
        ph4 = ExitStack()
        sbl4 = lambda nm, shape, dt: ph4.enter_context(nc.sbuf_tensor(self.name(nm), shape, dt))
        owb = sbl4("owb", [128, 16 * D], FP16)
        xsb4 = sbl4("mx4", [128, TC * D], F32)
        y2g = [sbl4("y2g", [128, 16 * 128], FP16) for _ in range(2)]
        obuf = [sbl4("obuf", [128, D], F32) for _ in range(2)]
        r1b = [sbl4("r1b", [128, D], F32) for _ in range(2)]

        g4 = sg.n

        def m4_g(g):
            sg.inc(g.dma_start(
                out=owb[:].rearrange("p (k d) -> p k d", k=16),
                in_=self.outwT.rearrange("(k p) d -> p k d", p=128)), 16)
        block.gpsimd(m4_g)
        self._load_x(xsb4)

        ld4 = ld.n
        t4 = st_.n
        s4 = ss.n
        v4 = sv.n
        d4 = sst.n

        for c in range(TC):
            def m4_ld(sync, c=c):
                sync.wait_ge(ld.h, ld4 + c * 16)
                if c >= 2:
                    sync.wait_ge(st_.h, t4 + (c - 1) * 2)
                ld.inc(sync.dma_start(
                    out=y2g[c % 2][:].rearrange("p (e q) -> p e q", e=16),
                    in_=self.stg_y2.rearrange("(e p) t -> p e t", p=128)[:, :, c * 128:(c + 1) * 128]), 16)
            block.sync(m4_ld)

        def m4_t(t):
            t.wait_ge(sg.h, g4 + 16)
            for c in range(TC):
                t.wait_ge(ld.h, ld4 + (c + 1) * 16)
                if c >= 2:
                    t.wait_ge(ss.h, s4 + (c - 1) * 2)
                pa, pbk = pb[(c % 2) * 2], pb[(c % 2) * 2 + 1]
                for e in range(16):
                    i1 = t.matmul(pa[:], y2g[c % 2][:, e * 128:(e + 1) * 128],
                                  owb[:, e * D:e * D + 512], start=(e == 0), stop=(e == 15))
                st_.inc(i1)
                for e in range(16):
                    i2 = t.matmul(pbk[:], y2g[c % 2][:, e * 128:(e + 1) * 128],
                                  owb[:, e * D + 512:(e + 1) * D], start=(e == 0), stop=(e == 15))
                st_.inc(i2)
        block.tensor(m4_t)

        def m4_s(s):
            for c in range(TC):
                s.wait_ge(st_.h, t4 + (c + 1) * 2)
                if c >= 2:
                    s.wait_ge(sv.h, v4 + (c - 1) * 2)
                pa, pbk = pb[(c % 2) * 2], pb[(c % 2) * 2 + 1]
                s.activation(obuf[c % 2][:, 0:512], pa[:], AF.Copy,
                             scale=self.cst[:, C_QUART:C_QUART + 1])
                i = s.activation(obuf[c % 2][:, 512:1024], pbk[:], AF.Copy,
                                 scale=self.cst[:, C_QUART:C_QUART + 1])
                ss.inc(i)
                ss.inc(s.nop())
        block.scalar(m4_s)

        def m4_v(v):
            for c in range(TC):
                v.wait_ge(ss.h, s4 + (c + 1) * 2)
                v.tensor_tensor(r1b[c % 2][:], obuf[c % 2][:], self.outb_t[:], ALU.add)
                v.drain()
                i = v.tensor_tensor(xsb4[:, c * D:(c + 1) * D], xsb4[:, c * D:(c + 1) * D],
                                    r1b[c % 2][:], ALU.add)
                sv.inc(i)
                sv.inc(v.nop())
        block.vector(m4_v)

        self._store_x(xsb4, sv.n)
        self.barrier()
        ph4.close()
        self.nc.leave_named_scope("m4", sc_m[0], False)

    class _Quant:

        def __init__(self, B, ph, F, n_par=2):
            nc = B.nc
            self.B = B
            self.F = F
            self.n_par = n_par
            nb = F // 16
            sbl = lambda nm, shape, dt: ph.enter_context(
                nc.sbuf_tensor(B.name(nm), shape, dt))
            self.xn = [sbl("q_xn", [128, F], FP16) for _ in range(n_par)]
            self.an = [sbl("q_an", [128, F], FP16) for _ in range(n_par)]
            self.sg = [sbl("q_sg", [128, F], FP16) for _ in range(n_par)]
            self.sm = [sbl("q_sm", [128, 3 * nb], F32) for _ in range(n_par)]
            self.sc16 = [sbl("q_sc16", [128, nb], FP16) for _ in range(n_par)]
            self.sv_q = Sem(nc, ph, B.name("qv"))
            self.ss_q = Sem(nc, ph, B.name("qs"))
            self.ss_a = Sem(nc, ph, B.name("qa"))
            self.sv_qb = Sem(nc, ph, B.name("qb"))
            self.nA = 0
            self.nS = 0
            self.n = 0

        def emit_A(self, v, src_ap, W=None):
            F = W if W is not None else self.F
            nb, nbm = F // 16, self.F // 16
            i = self.nA
            self.nA += 1
            p = i % self.n_par
            sm = self.sm[p]
            if i >= self.n_par:
                v.wait_ge(self.ss_q.h, i - self.n_par + 1)
            v.drain()
            v.tensor_reduce(sm[:, 0:nb], src_ap.rearrange("p (b w) -> p b w", w=16),
                            AX.X, ALU.max, apply_absolute_value=True)
            v.drain()
            v.tensor_scalar(sm[:, 2 * nbm:2 * nbm + nb], sm[:, 0:nb], 1.0 / 6.0, 1e-12,
                            ALU.mult, ALU.max)
            v.drain()
            v.reciprocal(sm[:, nbm:nbm + nb], sm[:, 2 * nbm:2 * nbm + nb])
            v.tensor_scalar(self.sc16[p][:, 0:nb], sm[:, 2 * nbm:2 * nbm + nb], 1.0, None,
                            ALU.mult)
            v.drain()
            self.sv_q.inc(v.tensor_tensor(
                self.xn[p][:, 0:F], src_ap,
                sm[:, nbm:nbm + nb].rearrange("p b -> p b ()").broadcast_to([128, nb, 16]),
                ALU.mult))

        def emit_S(self, s, W=None):
            F = W if W is not None else self.F
            i = self.nS
            self.nS += 1
            p = i % self.n_par
            s.wait_ge(self.sv_q.h, i + 1)
            if i >= self.n_par:
                s.wait_ge(self.sv_qb.h, i - self.n_par + 1)
            s.drain()
            self.ss_a.inc(s.activation(self.an[p][:, 0:F], self.xn[p][:, 0:F], AF.Abs))
            self.ss_q.inc(s.activation(self.sg[p][:, 0:F], self.xn[p][:, 0:F], AF.Sign))

        def emit_B(self, v, out_ap, W=None):
            F = W if W is not None else self.F
            nb = F // 16
            p = self.n % self.n_par
            an, sg = self.an[p][:, 0:F], self.sg[p][:, 0:F]
            tA, tB, tC, tD = (t[:, 0:F] for t in self.B.qtb)
            v.wait_ge(self.ss_a.h, self.n + 1)
            v.drain()
            v.tensor_scalar(tA, an, 0.25, 0.5, ALU.is_gt, ALU.mult)
            v.tensor_scalar(tB, an, 0.75, 0.5, ALU.is_gt, ALU.mult)
            v.tensor_scalar(tC, an, 1.25, 0.5, ALU.is_gt, ALU.mult)
            v.tensor_scalar(tD, an, 1.75, 0.5, ALU.is_gt, ALU.mult)
            v.drain()
            v.tensor_tensor(tA, tA, tB, ALU.add)
            v.tensor_tensor(tC, tC, tD, ALU.add)
            v.drain()
            v.tensor_scalar(tB, an, 2.5, 1.0, ALU.is_gt, ALU.mult)
            v.tensor_scalar(tD, an, 3.5, 1.0, ALU.is_gt, ALU.mult)
            v.tensor_tensor(tA, tA, tC, ALU.add)
            v.drain()
            v.tensor_scalar(tC, an, 5.0, 2.0, ALU.is_gt, ALU.mult)
            v.tensor_tensor(tB, tB, tD, ALU.add)
            v.drain()
            v.tensor_tensor(tA, tA, tB, ALU.add)
            v.drain()
            v.tensor_tensor(tA, tA, tC, ALU.add)
            v.wait_ge(self.ss_q.h, self.n + 1)
            v.drain()
            v.tensor_tensor(tB, tA, sg, ALU.mult)
            v.drain()
            i = v.tensor_tensor(
                out_ap, tB,
                self.sc16[p][:, 0:nb].rearrange("p b -> p b ()").broadcast_to([128, nb, 16]),
                ALU.mult)
            self.sv_qb.inc(i)
            self.n += 1
            return i

    def _wq_init(self):
        nc = self.nc
        ph = ExitStack()
        sbl = lambda nm, shape, dt: ph.enter_context(nc.sbuf_tensor(self.name(nm), shape, dt))
        self.wq_ph = ph
        self.wq_cin = [sbl("wq_in", [128, 1408], F32) for _ in range(2)]
        self.wq_cout = [sbl("wq_out", [128, 1408], FP16) for _ in range(2)]
        self.qtb = [sbl("qtb", [128, 1408], FP16) for _ in range(4)]
        self.wq_q = self._Quant(self, ph, 1408)
        self.wq_ld = Sem(nc, ph, self.name("wqld"))
        self.wq_st = Sem(nc, ph, self.name("wqst"))
        pieces = []
        for c in range(IC):
            pieces.append((self.wg_in, self.wgq_d, c * 128, 0, D))
        for c in range(IC):
            pieces.append((self.wu_in, self.wuq_d, c * 128, 0, D))
        for c in range(8):
            pieces.append((self.wd_in, self.wdq_d, c * 128, 0, 1408))
            pieces.append((self.wd_in, self.wdq_d, c * 128, 1408, 1408))
        self.wq_pieces = pieces
        self.wq_j = 0
        q = self.wq_q
        self.wq_extras = [q.sv_q, q.ss_q, q.ss_a, q.sv_qb, self.wq_ld, self.wq_st]

    def _wq_ldb(self, g, j):
        srcd, dstd, r0, c0, W = self.wq_pieces[j]
        g.wait_ge(self.wq_ld.h, j * 16)
        if j >= 2:
            g.wait_ge(self.wq_q.sv_q.h, j - 1)
        self.wq_ld.inc(g.dma_start(out=self.wq_cin[j % 2][:, 0:W],
                                   in_=srcd[r0:r0 + 128, c0:c0 + W]), 16)

    def _wq_stb(self, sync, j):
        srcd, dstd, r0, c0, W = self.wq_pieces[j]
        sync.wait_ge(self.wq_st.h, j * 16)
        sync.wait_ge(self.wq_q.sv_qb.h, j + 1)
        self.wq_st.inc(sync.dma_start(out=dstd[r0:r0 + 128, c0:c0 + W],
                                      in_=self.wq_cout[j % 2][:, 0:W]), 16)

    def _wq_step(self):
        NP = len(self.wq_pieces)
        j = self.wq_j
        if j > NP:
            return
        block, q = self.block, self.wq_q
        if j < NP:
            srcd, dstd, r0, c0, W = self.wq_pieces[j]
            if j == 0:
                block.sync(lambda sync: self._wq_ldb(sync, 0))
                block.sync(lambda sync: self._wq_ldb(sync, 1))
            elif j + 1 < NP:
                block.sync(lambda sync, k=j + 1: self._wq_ldb(sync, k))

            def va(v, j=j, W=W):
                v.wait_ge(self.wq_ld.h, (j + 1) * 16)
                q.emit_A(v, self.wq_cin[j % 2][:, 0:W], W)
            block.vector(va)

            def sa(s, W=W):
                q.emit_S(s, W)
            block.scalar(sa)
        if j >= 1:
            srcd, dstd, r0, c0, W = self.wq_pieces[j - 1]

            def vb(v, j=j, W=W):
                if j - 1 >= 2:
                    v.wait_ge(self.wq_st.h, (j - 2) * 16)
                q.emit_B(v, self.wq_cout[(j - 1) % 2][:, 0:W]
                         .rearrange("p (b w) -> p b w", w=16), W)
            block.vector(vb)
            if j - 2 >= 0:
                block.sync(lambda sync, k=j - 2: self._wq_stb(sync, k))
        self.wq_j += 1

    def _wq_drain(self):
        NP = len(self.wq_pieces)
        while self.wq_j <= NP:
            self._wq_step()
        self.block.sync(lambda sync: self._wq_stb(sync, NP - 1))

    def _emit_ffn4(self):
        nc, block = self.nc, self.block
        ld, sv, ss, st_, sst, sg = (self.sems[k] for k in ["ld", "v", "s", "t", "st", "g"])
        pb = self.pb
        stat, cst = self.stat, self.cst

        scope_f41 = nc.enter_named_scope("f41", False)
        ph1 = ExitStack()
        sbl1 = lambda nm, shape, dt: ph1.enter_context(nc.sbuf_tensor(self.name(nm), shape, dt))
        wgT = sbl1("wgT", [128, 8 * I1], FP16)
        wuT = sbl1("wuT", [128, 8 * I1], FP16)
        xcb = [sbl1("f4x", [128, D], F32) for _ in range(2)]
        xnf = sbl1("f4xn", [128, D], F32)
        xq4c = [sbl1("f4xq", [128, D], FP16) for _ in range(2)]
        xq4Tc = [sbl1("f4xqT", [128, 8 * 128], FP16) for _ in range(2)]
        sgb = [sbl1("f4sg", [128, 512], FP16) for _ in range(2)]
        upe = [sbl1("f4up", [128, 512], FP16) for _ in range(2)]
        hb = [sbl1("f4h", [128, I1], F32) for _ in range(2)]
        hqp = [sbl1("f4hq", [128, 1408], FP16) for _ in range(2)]
        self.qtb = [sbl1("qtb", [128, 1408], FP16) for _ in range(4)]
        qx = self._Quant(self, ph1, D)
        qh = self._Quant(self, ph1, 1408)
        se_x = Sem(nc, ph1, self.name("f4ex"))
        s_hst = Sem(nc, ph1, self.name("f4hs"))
        sgh = Sem(nc, ph1, self.name("f4gh"))
        NT = [512, 512, 512, 512, 512, 256]
        NPH = 2 * TC

        ldw0 = ld.n

        def f41_wsync(sync):
            sync.wait_ge(ld.h, ldw0)
            for k in range(8):
                ld.inc(sync.dma_start_transpose(
                    wgT[:, k * I1:(k + 1) * I1], self.wgq_d[:, k * 128:(k + 1) * 128]), 16)
        def f41_wsync2(sync):
            for k in range(8):
                ld.inc(sync.dma_start_transpose(
                    wuT[:, k * I1:(k + 1) * I1], self.wuq_d[:, k * 128:(k + 1) * 128]), 16)
        block.sync(f41_wsync)
        block.sync(f41_wsync2)
        n_wld = ld.n
        n_wsg = sg.n

        lp1 = ld.n
        v0 = sv.n
        for c in range(TC):
            def ld1(sync, c=c):
                sync.wait_ge(ld.h, lp1 + c * 16)
                if c >= 2:
                    sync.wait_ge(sv.h, v0 + (c - 1))
                ld.inc(sync.dma_start(
                    out=xcb[c % 2][:],
                    in_=self.xd.rearrange("(c p) d -> p c d", p=128)[:, c, :]), 16)
            block.sync(ld1)

            def v1(v, c=c):
                v.wait_ge(ld.h, lp1 + (c + 1) * 16)
                v.drain()
                v.tensor_reduce(stat[:, c:c + 1], xcb[c % 2][:], AX.X, ALU.add)
                v.scalar_tensor_tensor(xnf[:], xcb[c % 2][:], 1.0, xcb[c % 2][:],
                                       ALU.mult, ALU.mult,
                                       accum_out=stat[:, 16 + c:17 + c])
                v.drain()
                sv.inc(v.nop())
            block.vector(v1)
        v_p1 = sv.n

        def v_stats(v):
            v.drain()
            v.tensor_scalar(stat[:, 32:48], stat[:, 0:16], 1.0 / D, None, ALU.mult)
            v.drain()
            v.tensor_tensor(stat[:, 64:80], stat[:, 32:48], stat[:, 32:48], ALU.mult)
            v.drain()
            sv.inc(v.scalar_tensor_tensor(stat[:, 48:64], stat[:, 16:32], 1.0 / D,
                                          stat[:, 64:80], ALU.mult, ALU.subtract))
        block.vector(v_stats)
        n_v1 = sv.n

        def sc1(s):
            s.wait_ge(sv.h, n_v1)
            ss.inc(s.activation(stat[:, 80:96], stat[:, 48:64], AF.Sqrt,
                                bias=cst[:, C_EPS:C_EPS + 1]))
        block.scalar(sc1)
        n_s1p = ss.n

        def v_rstd(v):
            v.wait_ge(ss.h, n_s1p)
            sv.inc(v.reciprocal(stat[:, 48:64], stat[:, 80:96]))
            v.wait_ge(sv.h, sv.n)
        block.vector(v_rstd)
        v1b = sv.n
        s1b = ss.n
        lp2 = sg.n

        marks_trans = {}
        marks_gu = {}

        def f4_t(t):
            t.wait_ge(ld.h, n_wld)

            def trans(c):
                t.wait_ge(qx.sv_qb.h, c + 1)
                if c >= 2:
                    t.wait_ge(se_x.h, c - 1)
                b0 = pb[4 + 2 * (c % 2)]
                b1 = pb[5 + 2 * (c % 2)]
                for k in range(8):
                    bank = b0 if k < 4 else b1
                    i = t.matmul(bank[:, (k % 4) * 128:(k % 4) * 128 + 128],
                                 xq4c[c % 2][:, k * 128:(k + 1) * 128],
                                 self.ident[:], start=True, stop=True)
                    if k % 4 == 3:
                        st_.inc(i)
                marks_trans[c] = st_.n

            def gu(c):
                t.wait_ge(se_x.h, c + 1)
                off = 0
                for n, w in enumerate(NT):
                    pg, pu = pb[(n % 2) * 2], pb[(n % 2) * 2 + 1]
                    cnt = c * 6 + n - 1
                    if cnt >= 0:
                        t.wait_ge(ss.h, s1b + 2 * cnt)
                    for k in range(8):
                        ig = t.matmul(pg[:, 0:w], xq4Tc[c % 2][:, k * 128:(k + 1) * 128],
                                      wgT[:, k * I1 + off:k * I1 + off + w],
                                      start=(k == 0), stop=(k == 7))
                    st_.inc(ig)
                    for k in range(8):
                        iu = t.matmul(pu[:, 0:w], xq4Tc[c % 2][:, k * 128:(k + 1) * 128],
                                      wuT[:, k * I1 + off:k * I1 + off + w],
                                      start=(k == 0), stop=(k == 7))
                    st_.inc(iu)
                    marks_gu[(c, n)] = st_.n
                    off += w

            trans(0)
            trans(1)
            for c in range(TC):
                gu(c)
                if c + 2 < TC:
                    trans(c + 2)
        block.tensor(f4_t)

        for c in range(TC + 4):
            if c < TC:
                def s_qxS(s):
                    qx.emit_S(s)
                block.scalar(s_qxS)
            if 1 <= c <= TC:
                def s_evacx(s, cc=c - 1):
                    s.wait_ge(st_.h, marks_trans[cc])
                    b0 = pb[4 + 2 * (cc % 2)]
                    b1 = pb[5 + 2 * (cc % 2)]
                    s.activation(xq4Tc[cc % 2][:, 0:512], b0[:], AF.Copy)
                    se_x.inc(s.activation(xq4Tc[cc % 2][:, 512:1024], b1[:], AF.Copy))
                block.scalar(s_evacx)
            if 2 <= c < TC + 2:
                def s_silu(s, cc=c - 2):
                    for n, w in enumerate(NT):
                        s.wait_ge(st_.h, marks_gu[(cc, n)])
                        pg, pu = pb[(n % 2) * 2], pb[(n % 2) * 2 + 1]
                        prev = cc * 6 + n - 1
                        if prev >= 0:
                            s.wait_ge(sgh.h, prev)
                        ss.inc(s.activation(sgb[n % 2][:, 0:w], pg[:, 0:w], AF.Silu))
                        ss.inc(s.activation(upe[n % 2][:, 0:w], pu[:, 0:w], AF.Copy))
                block.scalar(s_silu)
            if 3 <= c < TC + 3:
                def s_qhS(s, cc=c - 3):
                    for q in range(2):
                        qh.emit_S(s, 1408)
                block.scalar(s_qhS)

        def emit_qh_A(cc, q):
            def vA(v):
                if q == 0:
                    v.wait_ge(sgh.h, (cc + 1) * 6)
                qh.emit_A(v, hb[cc % 2][:, q * 1408:(q + 1) * 1408], 1408)
            block.vector(vA)

        def emit_qh_B(j):
            def vB(v):
                if j >= 2:
                    v.wait_ge(s_hst.h, (j - 1) * 16)
                qh.emit_B(v, hqp[j % 2][:]
                          .rearrange("p (b w) -> p b w", w=16), 1408)
            block.vector(vB)

        for c in range(TC + 4):
            if c < TC:
                def v_norm_qxA(v, c=c):
                    v.wait_ge(sg.h, lp2 + (c + 1) * 16)
                    v.drain()
                    v.tensor_scalar(xnf[:], xcb[c % 2][:],
                                    stat[:, 32 + c:33 + c], stat[:, 48 + c:49 + c],
                                    ALU.subtract, ALU.mult)
                    qx.emit_A(v, xnf[:])
                block.vector(v_norm_qxA)
            if 1 <= c <= TC:
                def v_qxB(v, cc=c - 1):
                    if cc >= 2:
                        v.wait_ge(st_.h, marks_trans[cc - 2])
                    qx.emit_B(v, xq4c[cc % 2][:].rearrange("p (b w) -> p b w", w=16))
                block.vector(v_qxB)
            if 3 <= c < TC + 3:
                cc = c - 3
                for q in range(2):
                    j = cc * 2 + q
                    emit_qh_A(cc, q)
                    if j >= 1:
                        emit_qh_B(j - 1)
        emit_qh_B(NPH - 1)

        def ld2(g, c):
            g.wait_ge(sg.h, lp2 + c * 16)
            if c < 2:
                g.wait_ge(sv.h, v0 + 15 + c)
            else:
                g.wait_ge(qx.sv_q.h, c - 1)
            sg.inc(g.dma_start(
                out=xcb[c % 2][:],
                in_=self.xd.rearrange("(c p) d -> p c d", p=128)[:, c, :]), 16)

        def sth(sync, j):
            cc, q = j // 2, j % 2
            sync.wait_ge(s_hst.h, j * 16)
            sync.wait_ge(qh.sv_qb.h, j + 1)
            s_hst.inc(sync.dma_start(
                out=self.hq_d[cc * 128:(cc + 1) * 128, q * 1408:(q + 1) * 1408],
                in_=hqp[j % 2][:]), 16)

        def g_Vh(g, cc):
            g.drain()
            off = 0
            for n, w in enumerate(NT):
                g.wait_ge(ss.h, s1b + 2 * (cc * 6 + n + 1))
                if n == 0 and cc >= 2:
                    g.wait_ge(qh.sv_q.h, 2 * (cc - 1))
                sgh.inc(g.tensor_tensor(hb[cc % 2][:, off:off + w],
                                        sgb[n % 2][:, 0:w], upe[n % 2][:, 0:w], ALU.mult))
                off += w

        for c in range(TC + 3):
            if c < TC:
                block.gpsimd(lambda g, c=c: ld2(g, c))
            if 3 <= c:
                block.gpsimd(lambda g, cc=c - 3: g_Vh(g, cc))
        for j in range(NPH):
            block.sync(lambda sync, j=j: sth(sync, j))

        self.barrier(extra=[qx.sv_q, qx.ss_q, qx.ss_a, qx.sv_qb,
                            qh.sv_q, qh.ss_q, qh.ss_a, qh.sv_qb, se_x, s_hst, sgh])
        ph1.close()
        nc.leave_named_scope("f41", scope_f41[0] if isinstance(scope_f41, tuple) else scope_f41, False)

        scope_f42 = nc.enter_named_scope("f42", False)
        ph2 = ExitStack()
        sbl2 = lambda nm, shape, dt: ph2.enter_context(nc.sbuf_tensor(self.name(nm), shape, dt))
        hqT = sbl2("hqT", [128, IC * T // 1], BF16) if False else             sbl2("hqT", [128, 22 * T], FP16)
        wdT = sbl2("wdT", [128, 22 * D], FP16)
        obuf = [sbl2("f4o", [128, D], F32) for _ in range(2)]
        xc = [sbl2("f4xc", [128, D], F32) for _ in range(2)]

        ld2 = ld.n

        def f42_w(sync):
            sync.wait_ge(ld.h, ld2)
            for i in range(22):
                ld.inc(sync.dma_start_transpose(
                    hqT[:, i * T:(i + 1) * T], self.hq_d[:, i * 128:(i + 1) * 128]), 16)
            for i in range(22):
                ld.inc(sync.dma_start_transpose(
                    wdT[:, i * D:(i + 1) * D], self.wdq_d[:, i * 128:(i + 1) * 128]), 16)
        block.sync(f42_w)
        n_w2 = ld.n

        t2 = st_.n
        s2 = ss.n
        v2 = sv.n
        ldx2 = ld.n
        stx2 = sst.n

        def f42_xld(sync, c):
            sync.wait_ge(ld.h, ldx2 + c * 16)
            if c >= 2:
                sync.wait_ge(sst.h, stx2 + (c - 1) * 16)
            ld.inc(sync.dma_start(
                out=xc[c % 2][:],
                in_=self.xd.rearrange("(c p) d -> p c d", p=128)[:, c, :]), 16)

        def f42_xst(sync, c):
            sync.wait_ge(sst.h, stx2 + c * 16)
            sync.wait_ge(sv.h, v2 + c + 1)
            sst.inc(sync.dma_start(
                out=self.xd.rearrange("(c p) d -> p c d", p=128)[:, c, :],
                in_=xc[c % 2][:]), 16)

        for c in range(TC):
            if c >= 2:
                block.sync(lambda sync, c=c: f42_xst(sync, c - 2))
            block.sync(lambda sync, c=c: f42_xld(sync, c))
        block.sync(lambda sync: f42_xst(sync, TC - 2))
        block.sync(lambda sync: f42_xst(sync, TC - 1))

        def f42_t(t):
            t.wait_ge(ld.h, n_w2)
            for c in range(TC):
                if c >= 2:
                    t.wait_ge(ss.h, s2 + (c - 1) * 2)
                pa, pbk = pb[(c % 2) * 2], pb[(c % 2) * 2 + 1]
                for i in range(22):
                    i1 = t.matmul(pa[:], hqT[:, i * T + c * 128:i * T + (c + 1) * 128],
                                  wdT[:, i * D:i * D + 512], start=(i == 0), stop=(i == 21))
                st_.inc(i1)
                for i in range(22):
                    i2 = t.matmul(pbk[:], hqT[:, i * T + c * 128:i * T + (c + 1) * 128],
                                  wdT[:, i * D + 512:(i + 1) * D], start=(i == 0), stop=(i == 21))
                st_.inc(i2)
        block.tensor(f42_t)

        def f42_s(s):
            for c in range(TC):
                s.wait_ge(st_.h, t2 + (c + 1) * 2)
                if c >= 2:
                    s.wait_ge(sv.h, v2 + (c - 1))
                pa, pbk = pb[(c % 2) * 2], pb[(c % 2) * 2 + 1]
                s.activation(obuf[c % 2][:, 0:512], pa[:], AF.Copy)
                ss.inc(s.activation(obuf[c % 2][:, 512:1024], pbk[:], AF.Copy))
                ss.inc(s.nop())
        block.scalar(f42_s)

        def f42_v(v):
            for c in range(TC):
                v.wait_ge(ss.h, s2 + (c + 1) * 2)
                v.wait_ge(ld.h, ldx2 + (c + 1) * 16)
                v.drain()
                sv.inc(v.scalar_tensor_tensor(xc[c % 2][:], obuf[c % 2][:], 0.5,
                                              xc[c % 2][:], ALU.mult, ALU.add))
        block.vector(f42_v)

        self.barrier()
        ph2.close()
        nc.leave_named_scope("f42", scope_f42[0] if isinstance(scope_f42, tuple) else scope_f42, False)

    def _emit_attn(self):
        nc, block = self.nc, self.block
        ld, sv, ss, st_, sst, sg = (self.sems[k] for k in ["ld", "v", "s", "t", "st", "g"])
        pb = self.pb

        phh = ExitStack()
        h_tm = phh.enter_context(nc.sbuf_tensor(self.name("h_tm"), [128, TC * D], FP16))
        hT = phh.enter_context(nc.sbuf_tensor(self.name("hT"), [128, 8 * T], FP16))
        ph0 = ExitStack()
        xsb = ph0.enter_context(nc.sbuf_tensor(self.name("atx"), [128, TC * D], F32))
        self._load_x(xsb)
        self._emit_ln(xsb, h_tm)
        n_ln = sv.n

        def a0_stg(sync):
            sync.wait_ge(sv.h, n_ln)
            sst.inc(sync.dma_start(
                out=self.stg_td16.rearrange("(c p) d -> p c d", p=128),
                in_=h_tm[:].rearrange("p (c d) -> p c d", c=TC)), 16)
            sync.wait_ge(sst.h, sst.n)
            for k in range(8):
                sst.inc(sync.dma_start_transpose(
                    hT[:, k * T:(k + 1) * T], self.stg_td16[:, k * 128:(k + 1) * 128]), 16)
        block.sync(a0_stg)
        n_hT = sst.n
        self.barrier()
        ph0.close()

        ph1 = ExitStack()
        sbl = lambda nm, shape, dt: ph1.enter_context(nc.sbuf_tensor(self.name(nm), shape, dt))
        Gb = sbl("Gb", [128, 8 * D], FP16)
        qTb = sbl("qTb", [128, 8 * D], FP16)
        qfm = sbl("qfm", [128, 8 * T], FP16)
        obuf = [sbl("ato", [128, D], F32) for _ in range(2)]
        xc = [sbl("atxc", [128, D], F32) for _ in range(2)]

        g1 = sg.n

        def a_qw(g):
            sg.inc(g.dma_start(
                out=qTb[:].rearrange("p (k d) -> p k d", k=8),
                in_=self.qT_in.rearrange("(k p) d -> p k d", p=128)), 16)
        block.gpsimd(a_qw)

        t1 = st_.n
        s1 = ss.n

        def a1_t(t):
            for d1 in range(8):
                for n2 in range(2):
                    bk = pb[(d1 * 2 + n2) % 8]
                    if d1 * 2 + n2 >= 8:
                        t.wait_ge(ss.h, s1 + (d1 * 2 + n2) - 8 + 1)
                    for tk in range(TC):
                        i = t.matmul(bk[:], h_tm[:, tk * D + d1 * 128:tk * D + (d1 + 1) * 128],
                                     h_tm[:, tk * D + n2 * 512:tk * D + (n2 + 1) * 512],
                                     start=(tk == 0), stop=(tk == TC - 1))
                    st_.inc(i)
            t.wait_ge(sg.h, g1 + 16)
            t.wait_ge(sst.h, n_hT)
            t.wait_ge(ss.h, s1 + 16)
            for e in range(8):
                for tt in range(4):
                    bk = pb[(e % 2) * 4 + tt]
                    if e >= 2:
                        t.wait_ge(ss.h, s1 + 16 + (e - 1) * 4)
                    for k in range(8):
                        i = t.matmul(bk[:], qTb[:, k * D + e * 128:k * D + (e + 1) * 128],
                                     hT[:, k * T + tt * 512:k * T + (tt + 1) * 512],
                                     start=(k == 0), stop=(k == 7))
                    st_.inc(i)
        block.tensor(a1_t)

        def a1_s(s):
            for d1 in range(8):
                for n2 in range(2):
                    bk = pb[(d1 * 2 + n2) % 8]
                    s.wait_ge(st_.h, t1 + d1 * 2 + n2 + 1)
                    ss.inc(s.activation(Gb[:, d1 * D + n2 * 512:d1 * D + (n2 + 1) * 512],
                                        bk[:], AF.Copy))
            for e in range(8):
                for tt in range(4):
                    bk = pb[(e % 2) * 4 + tt]
                    s.wait_ge(st_.h, t1 + 16 + e * 4 + tt + 1)
                    ss.inc(s.activation(qfm[:, e * T + tt * 512:e * T + (tt + 1) * 512],
                                        bk[:], AF.Copy))
        block.scalar(a1_s)
        n_g_q = ss.n
        self.barrier()

        t3 = st_.n
        s3 = ss.n
        v3 = sv.n
        ldx3 = ld.n
        stx3 = sst.n

        def a3_xld(sync, c):
            sync.wait_ge(ld.h, ldx3 + c * 16)
            if c >= 2:
                sync.wait_ge(sst.h, stx3 + (c - 1) * 16)
            ld.inc(sync.dma_start(
                out=xc[c % 2][:],
                in_=self.xd.rearrange("(c p) d -> p c d", p=128)[:, c, :]), 16)

        def a3_xst(sync, c):
            sync.wait_ge(sst.h, stx3 + c * 16)
            sync.wait_ge(sv.h, v3 + c + 1)
            sst.inc(sync.dma_start(
                out=self.xd.rearrange("(c p) d -> p c d", p=128)[:, c, :],
                in_=xc[c % 2][:]), 16)

        for c in range(TC):
            if c >= 2:
                block.sync(lambda sync, c=c: a3_xst(sync, c - 2))
            block.sync(lambda sync, c=c: a3_xld(sync, c))
        block.sync(lambda sync: a3_xst(sync, TC - 2))
        block.sync(lambda sync: a3_xst(sync, TC - 1))

        def a3_t(t):
            for c in range(TC):
                if c >= 2:
                    t.wait_ge(ss.h, s3 + (c - 1) * 2)
                pa, pbk = pb[(c % 2) * 2], pb[(c % 2) * 2 + 1]
                for e in range(8):
                    i1 = t.matmul(pa[:], qfm[:, e * T + c * 128:e * T + (c + 1) * 128],
                                  Gb[:, e * D:e * D + 512], start=(e == 0), stop=(e == 7))
                st_.inc(i1)
                for e in range(8):
                    i2 = t.matmul(pbk[:], qfm[:, e * T + c * 128:e * T + (c + 1) * 128],
                                  Gb[:, e * D + 512:(e + 1) * D], start=(e == 0), stop=(e == 7))
                st_.inc(i2)
        block.tensor(a3_t)

        def a3_s(s):
            for c in range(TC):
                s.wait_ge(st_.h, t3 + (c + 1) * 2)
                if c >= 2:
                    s.wait_ge(sv.h, v3 + (c - 1))
                pa, pbk = pb[(c % 2) * 2], pb[(c % 2) * 2 + 1]
                s.activation(obuf[c % 2][:, 0:512], pa[:], AF.Copy)
                ss.inc(s.activation(obuf[c % 2][:, 512:1024], pbk[:], AF.Copy))
                ss.inc(s.nop())
        block.scalar(a3_s)

        def a3_v(v):
            for c in range(TC):
                v.wait_ge(ss.h, s3 + (c + 1) * 2)
                v.wait_ge(ld.h, ldx3 + (c + 1) * 16)
                v.drain()
                sv.inc(v.tensor_tensor(xc[c % 2][:], xc[c % 2][:], obuf[c % 2][:], ALU.add))
        block.vector(a3_v)

        self.barrier()
        ph1.close()
        phh.close()

    def _emit_fin(self):
        nc, block = self.nc, self.block
        ld, sfin, sst = self.sems["ld"], self.sems["fin"], self.sems["st"]
        ph = ExitStack()
        fbuf = ph.enter_context(nc.sbuf_tensor("fbuf", [128, TC * D], F32))

        def fin_sync(sync):
            ld.inc(sync.dma_start(
                out=fbuf[:].rearrange("p (c d) -> p c d", c=TC),
                in_=self.xd.rearrange("(c p) d -> p c d", p=128)), 16)
            sync.wait_ge(ld.h, ld.n)
            sfin.inc(sync.dma_start(
                out=self.out.rearrange("(c p) d -> p c d", p=128),
                in_=fbuf[:].rearrange("p (c d) -> p c d", c=TC)), 16)
            sync.wait_ge(sfin.h, sfin.n)
        block.sync(fin_sync)
        ph.close()


def prep_core_inputs(xb, W):
    f = np.ascontiguousarray
    cvec = lambda a, nch: f(np.asarray(a, np.float32).reshape(nch, 128).T)
    consts = np.zeros((128, 8), np.float32)
    consts[:, C_EPS] = 1e-5
    consts[:, C_I1024] = 1.0 / 1024
    consts[:, C_HALF] = 0.5
    consts[:, C_QUART] = 0.25
    consts[:, C_ONE] = 1.0
    return dict(
        x=f(np.asarray(xb, np.float32)),
        winT=f(np.asarray(W["L0_attn_in_proj_w"], np.float32).T),
        inb=cvec(W["L0_attn_in_proj_b"], 32),
        convw=f(np.asarray(W["L0_attn_conv_w"], np.float32).reshape(16, 128, 4)
                .transpose(1, 0, 2).reshape(128, 64)),
        convb=cvec(W["L0_attn_conv_b"], 16),
        dtwT=f(np.asarray(W["L0_attn_dt_w"], np.float32).T),
        bwT=f(np.asarray(W["L0_attn_b_w"], np.float32).T),
        cwT=f(np.asarray(W["L0_attn_c_w"], np.float32).T),
        dtb=cvec(W["L0_attn_dt_b"], 16),
        alog=cvec(W["L0_attn_A_log"], 16),
        dpar=cvec(W["L0_attn_D_param"], 16),
        outwT=f(np.asarray(W["L0_attn_out_w"], np.float32).T),
        outb_rep=f(np.broadcast_to(np.asarray(W["L0_attn_out_b"], np.float32), (128, D))),
        kern=f(np.asarray(W["L0_ffn_kernel"])),
        sin_c=f(np.broadcast_to(np.asarray(W["L0_ffn_input_scale"], np.float32).reshape(1, 1),
                                (128, 1))),
        skl_c=f(np.broadcast_to(np.asarray(W["L0_ffn_kernel_scale"], np.float32).reshape(1, 1),
                                (128, 1))),
        fbias_rep=f(np.broadcast_to(np.asarray(W["L0_ffn_bias"]), (128, D))),
        qT=f(np.asarray(W["L1_attn_queries"], np.float32).T),
        wg=f(np.asarray(W["L1_ffn_w_gate"], np.float32)),
        wu=f(np.asarray(W["L1_ffn_w_up"], np.float32)),
        wd=f(np.asarray(W["L1_ffn_w_down"], np.float32)),
        consts=consts,
        ident=np.eye(128, dtype=np.float16),
    )


def run(inputs, n_stages=6, n_cores=8):
    nc = Builder(n_stages).build()
    x = np.asarray(inputs["L0_attn_x"])
    in_maps = [prep_core_inputs(x[b], inputs) for b in range(n_cores)]
    res = run_bass_kernel_spmd(nc, in_maps, core_ids=list(range(n_cores)))
    return np.stack([r["out"] for r in res.results], axis=0)




def kernel(**inputs):
    nc = Builder(6).build()
    x = np.asarray(inputs["L0_attn_x"])
    in_maps = [prep_core_inputs(x[b], inputs) for b in range(8)]
    res = run_bass_kernel_spmd(nc, in_maps, core_ids=list(range(8)))
    return np.stack([np.asarray(r["out"]) for r in res.results], axis=0)


# version 15
# speedup vs baseline: 1.1708x; 1.1341x over previous
import numpy as np
import ml_dtypes
import concourse.bass as bass
import concourse.mybir as mybir
from concourse.bass_utils import run_bass_kernel_spmd
from contextlib import ExitStack

F32 = mybir.dt.float32
BF16 = mybir.dt.bfloat16
FP16 = mybir.dt.float16
FP8 = mybir.dt.float8e4
AF = mybir.ActivationFunctionType
ALU = mybir.AluOpType
AX = mybir.AxisListType

T = 2048
D = 1024
TC = T // 128
DI = 2048
E4 = 4096
I1 = 2816
IC = I1 // 128

C_EPS, C_I1024, C_HALF, C_QUART, C_ONE = 0, 1, 2, 3, 4


class Sem:
    def __init__(self, nc, es, name):
        self.h = es.enter_context(nc.semaphore(name))
        self.n = 0

    def inc(self, instr, k=1):
        instr.then_inc(self.h, k)
        self.n += k
        return instr


class Builder:
    def __init__(self, n_stages=6, wq_force=False):
        self.n_stages = n_stages
        self.wq_on = False
        self.wq_force = wq_force
        self.uid = 0

    def name(self, tag):
        self.uid += 1
        return f"{tag}_{self.uid}"

    def build(self):
        nc = bass.Bass("TRN2", target_bir_lowering=False)
        self.nc = nc
        dram = lambda name, shape, dt, kind: nc.dram_tensor(name, shape, dt, kind=kind).ap()
        IN, OUT, SC = "ExternalInput", "ExternalOutput", "Internal"

        self.x_in = dram("x", [T, D], F32, IN)
        self.winT = dram("winT", [D, E4], F32, IN)
        self.inb = dram("inb", [128, 32], F32, IN)
        self.convw = dram("convw", [128, 64], F32, IN)
        self.convb = dram("convb", [128, 16], F32, IN)
        self.dtwT = dram("dtwT", [DI, DI], F32, IN)
        self.bwT = dram("bwT", [DI, DI], F32, IN)
        self.cwT = dram("cwT", [DI, DI], F32, IN)
        self.dtb = dram("dtb", [128, 16], F32, IN)
        self.alog = dram("alog", [128, 16], F32, IN)
        self.dpar = dram("dpar", [128, 16], F32, IN)
        self.outwT = dram("outwT", [DI, D], F32, IN)
        self.outb_rep = dram("outb_rep", [128, D], F32, IN)
        self.kern_in = dram("kern", [D, D], BF16, IN)
        self.sin_c = dram("sin_c", [128, 1], F32, IN)
        self.skl_c = dram("skl_c", [128, 1], F32, IN)
        self.fbias_rep = dram("fbias_rep", [128, D], BF16, IN)
        self.qT_in = dram("qT", [D, D], F32, IN)
        self.wg_in = dram("wg", [I1, D], F32, IN)
        self.wu_in = dram("wu", [I1, D], F32, IN)
        self.wd_in = dram("wd", [D, I1], F32, IN)
        self.consts = dram("consts", [128, 8], F32, IN)
        self.ident_in = dram("ident", [128, 128], FP16, IN)

        self.out = dram("out", [T, D], F32, OUT)

        self.xd = dram("xd", [T, D], F32, SC)
        self.stg_td = dram("stg_td", [T, D], BF16, SC)
        self.stg_td16 = dram("stg_td16", [T, D], FP16, SC)
        self.stg_ti = dram("stg_ti", [T, I1], BF16, SC)
        self.stg_y2 = dram("stg_y2", [DI, T], FP16, SC)
        self.stg_tz = dram("stg_tz", [DI, T], FP16, SC)
        self.stg_b = dram("stg_b", [DI, T], FP16, SC)
        self.stg_c = dram("stg_c", [DI, T], FP16, SC)
        self.wgq_d = dram("wgq_d", [I1, D], FP16, SC)
        self.wuq_d = dram("wuq_d", [I1, D], FP16, SC)
        self.wdq_d = dram("wdq_d", [D, I1], FP16, SC)
        self.hq_d = dram("hq_d", [T, I1], FP16, SC)

        es = ExitStack()
        self.es = es
        with es:
            self._alloc_persistent()
            self.block = es.enter_context(nc.Block())
            with nc.named_scope("p0"):
                self._emit_p0()
            if self.n_stages >= 1:
                with nc.named_scope("ffn8a"):
                    self._emit_ffn8()
            if self.n_stages >= 2:
                self.wq_on = self.n_stages >= 4 or self.wq_force
                with nc.named_scope("mixer"):
                    self._emit_mixer()
            if self.n_stages >= 3:
                with nc.named_scope("ffn8b"):
                    self._emit_ffn8()
            if self.n_stages >= 4:
                with nc.named_scope("ffn4a"):
                    self._emit_ffn4()
            if self.n_stages >= 5:
                with nc.named_scope("attn"):
                    self._emit_attn()
            if self.n_stages >= 6:
                with nc.named_scope("ffn4b"):
                    self._emit_ffn4()
            with nc.named_scope("fin"):
                self._emit_fin()
        return nc

    def _alloc_persistent(self):
        nc, es = self.nc, self.es
        sb = lambda nm, shape, dt: es.enter_context(nc.sbuf_tensor(nm, shape, dt))
        self.cst = sb("cst", [128, 8], F32)
        self.inb_t = sb("inb_t", [128, 32], F32)
        self.convw_t = sb("convw_t", [128, 64], F32)
        self.convb_t = sb("convb_t", [128, 16], F32)
        self.dtb_t = sb("dtb_t", [128, 16], F32)
        self.alog_t = sb("alog_t", [128, 16], F32)
        self.negA = sb("negA", [128, 16], F32)
        self.nb2 = sb("nb2", [128, 16], F32)
        self.hbz = sb("hbz", [128, 16], F32)
        self.dpar_t = sb("dpar_t", [128, 16], F32)
        self.outb_t = sb("outb_t", [128, D], F32)
        self.fbias_t = sb("fbias_t", [128, D], BF16)
        self.sin_t = sb("sin_t", [128, 1], F32)
        self.skl_t = sb("skl_t", [128, 1], F32)
        self.rs_in = sb("rs_in", [128, 1], F32)
        self.rs_k = sb("rs_k", [128, 1], F32)
        self.sprod = sb("sprod", [128, 1], F32)
        self.stat = sb("stat", [128, 96], F32)
        self.tmp16 = sb("tmp16", [128, 16], F32)
        self.ident = sb("ident_sb", [128, 128], FP16)
        self.pb = [es.enter_context(nc.psum_tensor(f"pb{i}", [128, 512], F32))
                   for i in range(8)]
        self.sems = {n: Sem(nc, es, n) for n in
                     ["ld", "v", "s", "t", "st", "ph", "fin", "g"]}

    def barrier(self, extra=()):
        sph = self.sems["ph"]
        waits = [(self.sems[k].h, self.sems[k].n) for k in ["ld", "v", "s", "t", "st", "g"]]
        waits += [(s.h, s.n) for s in extra]
        for eng in ["vector", "scalar", "tensor", "sync", "gpsimd"]:
            def body(e):
                for h, n in waits:
                    e.wait_ge(h, n)
                sph.inc(e.nop())
            getattr(self.block, eng)(body)
        n = sph.n
        for eng in ["vector", "scalar", "tensor", "sync", "gpsimd"]:
            def body2(e):
                e.wait_ge(sph.h, n)
            getattr(self.block, eng)(body2)

    def _emit_p0(self):
        nc, block = self.nc, self.block
        ld, sv, ss = self.sems["ld"], self.sems["v"], self.sems["s"]

        def p0_sync(sync):
            for dst, src in [(self.cst, self.consts), (self.inb_t, self.inb),
                             (self.convw_t, self.convw), (self.convb_t, self.convb),
                             (self.dtb_t, self.dtb), (self.alog_t, self.alog),
                             (self.dpar_t, self.dpar), (self.outb_t, self.outb_rep),
                             (self.fbias_t, self.fbias_rep), (self.sin_t, self.sin_c),
                             (self.skl_t, self.skl_c)]:
                ld.inc(sync.dma_start(out=dst[:], in_=src), 16)
            ld.inc(sync.dma_start(out=self.ident[:], in_=self.ident_in), 16)
            ld.inc(sync.dma_start(out=self.xd, in_=self.x_in), 16)
            sync.wait_ge(ld.h, ld.n)
        block.sync(p0_sync)
        n_ld = ld.n

        def p0_scalar(s):
            s.wait_ge(ld.h, n_ld)
            ss.inc(s.activation(self.tmp16[:], self.alog_t[:], AF.Exp))
        block.scalar(p0_scalar)
        n_s = ss.n

        def p0_vector(v):
            v.wait_ge(ld.h, n_ld)
            v.reciprocal(self.rs_in[:], self.sin_t[:])
            v.reciprocal(self.rs_k[:], self.skl_t[:])
            v.tensor_tensor(self.sprod[:], self.sin_t[:], self.skl_t[:], ALU.mult)
            v.tensor_scalar(self.hbz[:], self.inb_t[:, 16:32], 0.5, None, ALU.mult)
            v.wait_ge(ss.h, n_s)
            v.tensor_scalar(self.nb2[:], self.tmp16[:], -1e-4, None, ALU.mult)
            sv.inc(v.tensor_scalar(self.negA[:], self.tmp16[:], -1.0, None, ALU.mult))
            v.wait_ge(sv.h, sv.n)
        block.vector(p0_vector)
        self.n_consts_ready = sv.n
        self.barrier()

    def _emit_ln(self, xsb, xn_out):
        block = self.block
        sv, ss = self.sems["v"], self.sems["s"]
        stat, cst = self.stat, self.cst

        def vec1(v):
            for c in range(TC):
                xc = xsb[:, c * D:(c + 1) * D]
                v.tensor_reduce(stat[:, c:c + 1], xc, AX.X, ALU.add)
                v.scalar_tensor_tensor(xn_out[:, c * D:(c + 1) * D], xc, 1.0, xc,
                                       ALU.mult, ALU.mult,
                                       accum_out=stat[:, 16 + c:17 + c])
            v.drain()
            v.tensor_scalar(stat[:, 32:48], stat[:, 0:16], 1.0 / D, None, ALU.mult)
            v.drain()
            v.tensor_tensor(stat[:, 64:80], stat[:, 32:48], stat[:, 32:48], ALU.mult)
            v.drain()
            sv.inc(v.scalar_tensor_tensor(stat[:, 48:64], stat[:, 16:32], 1.0 / D,
                                          stat[:, 64:80], ALU.mult, ALU.subtract))
        block.vector(vec1)
        n_v1 = sv.n

        def sc1(s):
            s.wait_ge(sv.h, n_v1)
            ss.inc(s.activation(stat[:, 80:96], stat[:, 48:64], AF.Sqrt,
                                bias=cst[:, C_EPS:C_EPS + 1]))
        block.scalar(sc1)
        n_s1 = ss.n

        def vec2(v):
            v.wait_ge(ss.h, n_s1)
            sv.inc(v.reciprocal(stat[:, 48:64], stat[:, 80:96]))
            v.wait_ge(sv.h, sv.n)
            for c in range(TC):
                v.tensor_scalar(xn_out[:, c * D:(c + 1) * D], xsb[:, c * D:(c + 1) * D],
                                stat[:, 32 + c:33 + c], stat[:, 48 + c:49 + c],
                                ALU.subtract, ALU.mult)
            v.drain()
            sv.inc(v.nop())
        block.vector(vec2)
        return sv.n

    def _load_x(self, xsb):
        ld = self.sems["ld"]

        sst = self.sems["st"]

        n0 = ld.n

        def ldx(sync):
            sync.wait_ge(sst.h, sst.n)
            sync.wait_ge(ld.h, n0)
            ld.inc(sync.dma_start(
                out=xsb[:].rearrange("p (c d) -> p c d", c=TC),
                in_=self.xd.rearrange("(c p) d -> p c d", p=128)), 16)
        self.block.sync(ldx)
        n = ld.n

        def vwait(v):
            v.wait_ge(ld.h, n)
        self.block.vector(vwait)

    def _store_x(self, xsb, n_ready, to_out=False):
        sst = self.sems["st"]
        sv = self.sems["v"]
        dst = self.out if to_out else self.xd

        def stx(sync):
            sync.wait_ge(sv.h, n_ready)
            sst.inc(sync.dma_start(
                out=dst.rearrange("(c p) d -> p c d", p=128),
                in_=xsb[:].rearrange("p (c d) -> p c d", c=TC)), 16)
        self.block.sync(stx)
        return sst.n

    def _emit_ffn8(self):
        nc, block = self.nc, self.block
        ld, sv, ss, st_, sst, sg = (self.sems[k] for k in ["ld", "v", "s", "t", "st", "g"])
        pb = self.pb
        stat, cst = self.stat, self.cst
        ph = ExitStack()
        sbl = lambda nm, shape, dt: ph.enter_context(nc.sbuf_tensor(self.name(nm), shape, dt))
        xcb = [sbl("x8", [128, D], F32) for _ in range(3)]
        xnbc = sbl("xn8", [128, D], BF16)
        q8c = [sbl("q8", [128, D], FP8) for _ in range(2)]
        xq8c = [sbl("xq8c", [128, D], FP16) for _ in range(2)]
        xq8Tc = [sbl("xq8T", [128, 8 * 128], FP16) for _ in range(2)]
        ybuf = [sbl("y8", [128, D], BF16) for _ in range(2)]
        y2b = sbl("y28", [128, D], BF16)
        k8raw = sbl("k8r", [128, 8 * D], BF16)
        k8b = sbl("k8b", [128, 8 * D], FP16)
        kf8 = sbl("kf8", [128, 8 * D], FP8)

        ldk0 = ld.n

        def k8_ld(sync):
            sync.wait_ge(ld.h, ldk0)
            ld.inc(sync.dma_start(
                out=k8raw[:].rearrange("p (k d) -> p k d", k=8),
                in_=self.kern_in.rearrange("(k p) d -> p k d", p=128)), 16)
        block.sync(k8_ld)
        n_k8ld = ld.n

        def k8_v(v):
            v.wait_ge(ld.h, n_k8ld)
            v.drain()
            sv.inc(v.tensor_scalar(kf8[:], k8raw[:], self.rs_k[:, 0:1], None, ALU.mult))
        block.vector(k8_v)
        n_k8v = sv.n

        def k8_s(s):
            s.wait_ge(sv.h, n_k8v)
            s.drain()
            ss.inc(s.activation(k8b[:], kf8[:], AF.Copy))
        block.scalar(k8_s)
        n_k8 = ss.n

        lp1 = ld.n
        v0 = sv.n
        for c in range(TC):
            def ld1(sync, c=c):
                sync.wait_ge(ld.h, lp1 + c * 16)
                if c >= 3:
                    sync.wait_ge(sv.h, v0 + (c - 2))
                ld.inc(sync.dma_start(
                    out=xcb[c % 3][:],
                    in_=self.xd.rearrange("(c p) d -> p c d", p=128)[:, c, :]), 16)
            block.sync(ld1)

            def v1(v, c=c):
                v.wait_ge(ld.h, lp1 + (c + 1) * 16)
                v.drain()
                v.tensor_reduce(stat[:, c:c + 1], xcb[c % 3][:], AX.X, ALU.add)
                v.scalar_tensor_tensor(y2b[:] if False else xnbc[:], xcb[c % 3][:], 1.0,
                                       xcb[c % 3][:], ALU.mult, ALU.mult,
                                       accum_out=stat[:, 16 + c:17 + c])
                v.drain()
                sv.inc(v.nop())
            block.vector(v1)
        v_p1 = sv.n

        def v_stats(v):
            v.drain()
            v.tensor_scalar(stat[:, 32:48], stat[:, 0:16], 1.0 / D, None, ALU.mult)
            v.drain()
            v.tensor_tensor(stat[:, 64:80], stat[:, 32:48], stat[:, 32:48], ALU.mult)
            v.drain()
            sv.inc(v.scalar_tensor_tensor(stat[:, 48:64], stat[:, 16:32], 1.0 / D,
                                          stat[:, 64:80], ALU.mult, ALU.subtract))
        block.vector(v_stats)
        n_v1 = sv.n

        def sc1(s):
            s.wait_ge(sv.h, n_v1)
            ss.inc(s.activation(stat[:, 80:96], stat[:, 48:64], AF.Sqrt,
                                bias=cst[:, C_EPS:C_EPS + 1]))
        block.scalar(sc1)
        n_s1p = ss.n

        def v_rstd(v):
            v.wait_ge(ss.h, n_s1p)
            sv.inc(v.reciprocal(stat[:, 48:64], stat[:, 80:96]))
            v.wait_ge(sv.h, sv.n)
        block.vector(v_rstd)

        lp2 = sg.n
        svc = sv.n
        q8mark, vrmark = {}, {}
        n = svc
        for c in range(TC + 2):
            if c < TC:
                n += 1
                q8mark[c] = n
            if c >= 2:
                n += 1
                vrmark[c - 2] = n
        ssc = ss.n
        sqmark, exmark, evmark = {}, {}, {}
        n = ssc
        for c in range(TC + 2):
            if c < TC:
                n += 1
                sqmark[c] = n
            if 1 <= c <= TC:
                n += 1
                exmark[c - 1] = n
            if c >= 2:
                n += 1
                evmark[c - 2] = n
        stc = st_.n
        transmark, mmmark = {}, {}
        n = stc
        n += 2; transmark[0] = n
        n += 2; transmark[1] = n
        for c in range(TC):
            n += 2
            mmmark[c] = n
            if c + 2 < TC:
                n += 2
                transmark[c + 2] = n

        def f8_t(t):
            t.wait_ge(ss.h, n_k8)

            def trans(c):
                t.wait_ge(ss.h, sqmark[c])
                if c >= 2:
                    t.wait_ge(ss.h, exmark[c - 2])
                b0 = pb[4 + 2 * (c % 2)]
                b1 = pb[5 + 2 * (c % 2)]
                for k in range(8):
                    bank = b0 if k < 4 else b1
                    i = t.matmul(bank[:, (k % 4) * 128:(k % 4) * 128 + 128],
                                 xq8c[c % 2][:, k * 128:(k + 1) * 128],
                                 self.ident[:], start=True, stop=True)
                    if k % 4 == 3:
                        st_.inc(i)

            def mm(c):
                t.wait_ge(ss.h, exmark[c])
                if c >= 2:
                    t.wait_ge(ss.h, evmark[c - 2])
                pa, pbk = pb[(c % 2) * 2], pb[(c % 2) * 2 + 1]
                for k in range(8):
                    i1 = t.matmul(pa[:], xq8Tc[c % 2][:, k * 128:(k + 1) * 128],
                                  k8b[:, k * D:k * D + 512],
                                  start=(k == 0), stop=(k == 7))
                st_.inc(i1)
                for k in range(8):
                    i2 = t.matmul(pbk[:], xq8Tc[c % 2][:, k * 128:(k + 1) * 128],
                                  k8b[:, k * D + 512:(k + 1) * D],
                                  start=(k == 0), stop=(k == 7))
                st_.inc(i2)

            trans(0)
            trans(1)
            for c in range(TC):
                mm(c)
                if c + 2 < TC:
                    trans(c + 2)
        block.tensor(f8_t)

        for c in range(TC + 2):
            if c < TC:
                def s_q8(s, c=c):
                    s.wait_ge(sv.h, q8mark[c])
                    if c >= 2:
                        s.wait_ge(st_.h, transmark[c - 2])
                    s.drain()
                    ss.inc(s.activation(xq8c[c % 2][:], q8c[c % 2][:], AF.Copy))
                block.scalar(s_q8)
            if 1 <= c <= TC:
                def s_ex(s, cc=c - 1):
                    s.wait_ge(st_.h, transmark[cc])
                    b0 = pb[4 + 2 * (cc % 2)]
                    b1 = pb[5 + 2 * (cc % 2)]
                    s.activation(xq8Tc[cc % 2][:, 0:512], b0[:], AF.Copy)
                    ss.inc(s.activation(xq8Tc[cc % 2][:, 512:1024], b1[:], AF.Copy))
                block.scalar(s_ex)
            if c >= 2:
                def s_ev(s, cc=c - 2):
                    s.wait_ge(st_.h, mmmark[cc])
                    if cc >= 2:
                        s.wait_ge(sv.h, vrmark[cc - 2])
                    pa, pbk = pb[(cc % 2) * 2], pb[(cc % 2) * 2 + 1]
                    yb = ybuf[cc % 2]
                    s.activation(yb[:, 0:512], pa[:], AF.Copy, scale=self.sprod[:, 0:1])
                    ss.inc(s.activation(yb[:, 512:1024], pbk[:], AF.Copy,
                                        scale=self.sprod[:, 0:1]))
                block.scalar(s_ev)

        for c in range(TC + 2):
            if c < TC:
                def v_nq(v, c=c):
                    v.wait_ge(sg.h, lp2 + (c + 1) * 16)
                    if c >= 2:
                        v.wait_ge(ss.h, sqmark[c - 2])
                    v.drain()
                    v.tensor_scalar(xnbc[:], xcb[c % 3][:],
                                    stat[:, 32 + c:33 + c], stat[:, 48 + c:49 + c],
                                    ALU.subtract, ALU.mult)
                    v.drain()
                    sv.inc(v.tensor_scalar(q8c[c % 2][:], xnbc[:], self.rs_in[:, 0:1],
                                           None, ALU.mult))
                block.vector(v_nq)
            if c >= 2:
                def v_res(v, cc=c - 2):
                    v.wait_ge(ss.h, evmark[cc])
                    v.drain()
                    v.tensor_tensor(y2b[:], ybuf[cc % 2][:], self.fbias_t[:], ALU.add)
                    v.drain()
                    sv.inc(v.scalar_tensor_tensor(
                        xcb[cc % 3][:], y2b[:], 0.5, xcb[cc % 3][:], ALU.mult, ALU.add))
                block.vector(v_res)

        def ld2(g, c):
            g.wait_ge(sg.h, lp2 + c * 16)
            if c < 3:
                g.wait_ge(sv.h, v_p1)
            else:
                g.wait_ge(sst.h, stx0 + (c - 2) * 16)
            sg.inc(g.dma_start(
                out=xcb[c % 3][:],
                in_=self.xd.rearrange("(c p) d -> p c d", p=128)[:, c, :]), 16)

        stx0 = sst.n

        def stx(sync, c):
            sync.wait_ge(sst.h, stx0 + c * 16)
            sync.wait_ge(sv.h, vrmark[c])
            sst.inc(sync.dma_start(
                out=self.xd.rearrange("(c p) d -> p c d", p=128)[:, c, :],
                in_=xcb[c % 3][:]), 16)

        for c in range(TC):
            block.gpsimd(lambda g, c=c: ld2(g, c))
        for c in range(TC):
            block.sync(lambda sync, c=c: stx(sync, c))

        self.barrier()
        ph.close()

    def _emit_mixer(self):
        nc, block = self.nc, self.block
        ld, sv, ss, st_, sst, sg = (self.sems[k] for k in ["ld", "v", "s", "t", "st", "g"])
        pb = self.pb

        sc_m = self.nc.enter_named_scope("m0", False)
        ph = ExitStack()
        sbl = lambda nm, shape, dt: ph.enter_context(nc.sbuf_tensor(self.name(nm), shape, dt))
        xsb = sbl("mx", [128, TC * D], F32)
        xnb = sbl("mxn", [128, TC * D], FP16)
        self._load_x(xsb)
        self._emit_ln(xsb, xnb)
        n_ln = self.sems["v"].n

        def stg0(sync):
            sync.wait_ge(sv.h, n_ln)
            sst.inc(sync.dma_start(
                out=self.stg_td16.rearrange("(c p) d -> p c d", p=128),
                in_=xnb[:].rearrange("p (c d) -> p c d", c=TC)), 16)
        block.sync(stg0)
        self.barrier()
        ph.close()
        self.nc.leave_named_scope("m0", sc_m[0], False)
        sc_m = self.nc.enter_named_scope("m1", False)

        ph_u = ExitStack()
        ubf = ph_u.enter_context(nc.sbuf_tensor(self.name("ubf"), [128, 16 * T], FP16))
        if self.wq_on:
            self._wq_init()
        ph = ExitStack()
        sbl = lambda nm, shape, dt: ph.enter_context(nc.sbuf_tensor(self.name(nm), shape, dt))
        xnT = sbl("xnT", [128, 8 * T], FP16)
        wsl = [sbl("wsl", [128, 8 * 128], FP16) for _ in range(2)]
        xzpad = sbl("xzpad", [128, 3 + T], FP16)
        cacc = [sbl("cacc", [128, T], F32) for _ in range(2)]
        thb = [sbl("thb", [128, T], FP16) for _ in range(2)]
        tzc = [sbl("tzc", [128, T], FP16) for _ in range(2)]
        sem_evac = Sem(nc, ph, self.name("m1ev"))
        sem_th = Sem(nc, ph, self.name("m1th"))
        sem_tz = Sem(nc, ph, self.name("m1tz"))
        sem_conv = Sem(nc, ph, self.name("m1cv"))
        sem_u = Sem(nc, ph, self.name("m1u"))

        def m1_tload(sync):
            sync.wait_ge(sst.h, sst.n)
            for k in range(8):
                sst.inc(sync.dma_start_transpose(
                    xnT[:, k * T:(k + 1) * T], self.stg_td16[:, k * 128:(k + 1) * 128]), 16)
        block.sync(m1_tload)
        n_xnT = sst.n

        g_base = sg.n
        t_base = st_.n
        tzst_base = sst.n

        for e in range(32):
            def m1_g(g, e=e):
                g.wait_ge(sg.h, g_base + e * 16)
                if e >= 2:
                    g.wait_ge(st_.h, t_base + (e - 1) * 4)
                sg.inc(g.dma_start(
                    out=wsl[e % 2][:].rearrange("p (k e) -> p k e", k=8),
                    in_=self.winT.rearrange("(k p) e -> p k e", p=128)[:, :, e * 128:(e + 1) * 128]), 16)
            block.gpsimd(m1_g)

        def m1_t(t):
            t.wait_ge(sst.h, n_xnT)
            for e in range(32):
                t.wait_ge(sg.h, g_base + (e + 1) * 16)
                if e >= 2:
                    if e < 16 + 2:
                        pass
                    if (e - 2) < 16:
                        t.wait_ge(sem_evac.h, (e - 2) + 1)
                    else:
                        t.wait_ge(sem_tz.h, ((e - 2) - 16 + 1) * 4)
                w = wsl[e % 2]
                quad = [pb[(e % 2) * 4 + i] for i in range(4)]
                for tt in range(4):
                    for k in range(8):
                        i = t.matmul(quad[tt][:],
                                     w[:, k * 128:(k + 1) * 128],
                                     xnT[:, k * T + tt * 512:(k * T) + (tt + 1) * 512],
                                     start=(k == 0), stop=(k == 7))
                    st_.inc(i)
        block.tensor(m1_t)

        def v_init(v):
            v.memset(xzpad[:, 0:3], 0.0)
            v.drain()
        block.vector(v_init)

        for e in range(16):
            def s_x(s, e=e):
                s.wait_ge(st_.h, t_base + (e + 1) * 4)
                if e >= 1:
                    s.wait_ge(sem_conv.h, e)
                quad = [pb[(e % 2) * 4 + i] for i in range(4)]
                for tt in range(4):
                    s.activation(xzpad[:, 3 + tt * 512:3 + (tt + 1) * 512],
                                 quad[tt][:], AF.Identity, bias=self.inb_t[:, e:e + 1])
                s.drain()
                sem_evac.inc(s.nop())
                if e >= 1:
                    ee = e - 1
                    if ee >= 2:
                        s.wait_ge(sem_u.h, ee - 1)
                    s.drain()
                    sem_th.inc(s.activation(thb[ee % 2][:], cacc[ee % 2][:], AF.Tanh,
                                            scale=self.cst[:, C_HALF:C_HALF + 1]))
            block.scalar(s_x)

            def v_conv(v, e=e):
                v.wait_ge(sem_evac.h, e + 1)
                if e >= 2:
                    v.wait_ge(sem_th.h, e - 1)
                ca = cacc[e % 2]
                v.drain()
                v.tensor_scalar(ca[:], xzpad[:, 0:T], self.convw_t[:, 4 * e:4 * e + 1],
                                self.convb_t[:, e:e + 1], ALU.mult, ALU.add)
                for tau in range(1, 4):
                    v.drain()
                    i = v.scalar_tensor_tensor(ca[:], xzpad[:, tau:tau + T],
                                               self.convw_t[:, 4 * e + tau:4 * e + tau + 1],
                                               ca[:], ALU.mult, ALU.add)
                sem_conv.inc(i)
                if e >= 1:
                    ee = e - 1
                    v.wait_ge(sem_th.h, ee + 1)
                    sem_u.inc(v.scalar_tensor_tensor(ubf[:, ee * T:(ee + 1) * T], thb[ee % 2][:],
                                                     1.0, cacc[ee % 2][:], ALU.add, ALU.mult))
            block.vector(v_conv)

        def s_thf(s):
            s.wait_ge(sem_conv.h, 16)
            s.drain()
            sem_th.inc(s.activation(thb[15 % 2][:], cacc[15 % 2][:], AF.Tanh,
                                    scale=self.cst[:, C_HALF:C_HALF + 1]))
        block.scalar(s_thf)

        def v_uf(v):
            v.wait_ge(sem_th.h, 16)
            sem_u.inc(v.scalar_tensor_tensor(ubf[:, 15 * T:16 * T], thb[15 % 2][:],
                                             1.0, cacc[15 % 2][:], ALU.add, ALU.mult))
            v.drain()
            sv.inc(v.nop())
        block.vector(v_uf)

        for e in range(16, 32):
            def s_z(s, e=e):
                s.wait_ge(st_.h, t_base + (e + 1) * 4)
                if e >= 18:
                    s.wait_ge(sst.h, tzst_base + (e - 16 - 1) * 16)
                quad = [pb[(e % 2) * 4 + i] for i in range(4)]
                for tt in range(4):
                    sem_tz.inc(s.activation(tzc[e % 2][:, tt * 512:(tt + 1) * 512], quad[tt][:],
                                            AF.Tanh, scale=self.cst[:, C_HALF:C_HALF + 1],
                                            bias=self.hbz[:, e - 16:e - 15]))
            block.scalar(s_z)

            def m1_tzstore(sync, e=e):
                sync.wait_ge(sst.h, tzst_base + (e - 16) * 16)
                sync.wait_ge(sem_tz.h, (e - 16 + 1) * 4)
                sst.inc(sync.dma_start(
                    out=self.stg_tz[(e - 16) * 128:(e - 15) * 128, :],
                    in_=tzc[e % 2][:]), 16)
            block.sync(m1_tzstore)
            if self.wq_on:
                self._wq_step()

        self.barrier()
        ph.close()
        self.nc.leave_named_scope("m1", sc_m[0], False)
        sc_m = self.nc.enter_named_scope("m2", False)

        ph2 = ExitStack()
        sbl2 = lambda nm, shape, dt: ph2.enter_context(nc.sbuf_tensor(self.name(nm), shape, dt))
        bsl = [sbl2("bsl", [128, 16 * 128], FP16) for _ in range(2)]
        bcc = [sbl2("bcc", [128, T], FP16) for _ in range(2)]

        for wsrc, dsts in [(self.bwT, self.stg_b), (self.cwT, self.stg_c)]:
            g2 = sg.n
            t2 = st_.n
            s2 = ss.n
            d2 = sst.n

            def m2_t(t, g2=g2, s2=s2, t2=t2):
                for e in range(16):
                    t.wait_ge(sg.h, g2 + (e + 1) * 16)
                    t.wait_ge(ss.h, s2 + max(0, e - 1) * 4)
                    w = bsl[e % 2]
                    quad = [pb[(e % 2) * 4 + i] for i in range(4)]
                    for tt in range(4):
                        for k in range(16):
                            i = t.matmul(quad[tt][:], w[:, k * 128:(k + 1) * 128],
                                         ubf[:, k * T + tt * 512: k * T + (tt + 1) * 512],
                                         start=(k == 0), stop=(k == 15))
                        st_.inc(i)
            block.tensor(m2_t)

            for e in range(16):
                def m2_g(g, wsrc=wsrc, t2=t2, g2=g2, e=e):
                    g.wait_ge(sg.h, g2 + e * 16)
                    g.wait_ge(st_.h, t2 + max(0, e - 1) * 4)
                    sg.inc(g.dma_start(
                        out=bsl[e % 2][:].rearrange("p (k e) -> p k e", k=16),
                        in_=wsrc.rearrange("(k p) e -> p k e", p=128)[:, :, e * 128:(e + 1) * 128]), 16)
                block.gpsimd(m2_g)

                def m2_s_e(s, t2=t2, d2=d2, e=e):
                    s.wait_ge(st_.h, t2 + (e + 1) * 4)
                    s.wait_ge(sst.h, d2 + max(0, e - 1) * 16)
                    quad = [pb[(e % 2) * 4 + i] for i in range(4)]
                    for tt in range(4):
                        ss.inc(s.activation(bcc[e % 2][:, tt * 512:(tt + 1) * 512],
                                            quad[tt][:], AF.Tanh,
                                            scale=self.cst[:, C_HALF:C_HALF + 1]))
                block.scalar(m2_s_e)

                def m2_st(sync, s2=s2, dsts=dsts, d2=d2, e=e):
                    sync.wait_ge(sst.h, d2 + e * 16)
                    sync.wait_ge(ss.h, s2 + (e + 1) * 4)
                    sst.inc(sync.dma_start(out=dsts[e * 128:(e + 1) * 128, :],
                                           in_=bcc[e % 2][:]), 16)
                block.sync(m2_st)
                if self.wq_on:
                    self._wq_step()
        if self.wq_on:
            self._wq_drain()
            self.barrier(extra=self.wq_extras)
            ph2.close()
            self.wq_ph.close()
        else:
            self.barrier()
            ph2.close()
        self.nc.leave_named_scope("m2", sc_m[0], False)
        sc_m = self.nc.enter_named_scope("m3", False)

        ph3 = ExitStack()
        sbl3 = lambda nm, shape, dt: ph3.enter_context(nc.sbuf_tensor(self.name(nm), shape, dt))
        dsl = [sbl3("dsl", [128, 16 * 128], FP16) for _ in range(2)]
        ebuf = sbl3("ebuf", [128, T], F32)
        spbuf = sbl3("spbuf", [128, T], F32)
        dbuf = [sbl3("dbuf", [128, T], F32) for _ in range(2)]
        ombuf = sbl3("ombuf", [128, T], F32)
        t1buf = sbl3("t1buf", [128, T], F32)
        incb = sbl3("incb", [128, T], F32)
        stsb = sbl3("stsb", [128, T], F32)
        t2buf = sbl3("t2buf", [128, T], F32)
        ypb = sbl3("ypb", [128, T], F32)
        y2c = [sbl3("y2c", [128, T], FP16) for _ in range(2)]
        bld = [sbl3("bld", [128, T], FP16) for _ in range(2)]
        cld = [sbl3("cld", [128, T], FP16) for _ in range(2)]
        tzld = [sbl3("tzld", [128, T], FP16) for _ in range(2)]

        g3 = sg.n
        t3 = st_.n
        s3 = ss.n
        v3 = sv.n
        d3 = sst.n
        ld3 = ld.n

        for e in range(16):
            def m3_g(g, e=e):
                g.wait_ge(sg.h, g3 + e * 16)
                if e >= 2:
                    g.wait_ge(st_.h, t3 + (e - 1) * 4)
                sg.inc(g.dma_start(
                    out=dsl[e % 2][:].rearrange("p (k e) -> p k e", k=16),
                    in_=self.dtwT.rearrange("(k p) e -> p k e", p=128)[:, :, e * 128:(e + 1) * 128]), 16)
            block.gpsimd(m3_g)

        def m3_ld_body(sync, e):
            sync.wait_ge(ld.h, ld3 + e * 48)
            if e >= 2:
                sync.wait_ge(sv.h, v3 + (e - 1) * 7)
            ld.inc(sync.dma_start(out=bld[e % 2][:], in_=self.stg_b[e * 128:(e + 1) * 128, :]), 16)
            ld.inc(sync.dma_start(out=cld[e % 2][:], in_=self.stg_c[e * 128:(e + 1) * 128, :]), 16)
            ld.inc(sync.dma_start(out=tzld[e % 2][:], in_=self.stg_tz[e * 128:(e + 1) * 128, :]), 16)

        def m3_st_body(sync, e):
            sync.wait_ge(sst.h, d3 + e * 16)
            sync.wait_ge(sv.h, v3 + (e + 1) * 7)
            sst.inc(sync.dma_start(out=self.stg_y2[e * 128:(e + 1) * 128, :],
                                   in_=y2c[e % 2][:]), 16)

        for e in range(16):
            block.sync(lambda sync, e=e: m3_ld_body(sync, e))
            if e >= 2:
                block.sync(lambda sync, e=e: m3_st_body(sync, e - 2))
        block.sync(lambda sync: m3_st_body(sync, 14))
        block.sync(lambda sync: m3_st_body(sync, 15))

        def m3_t(t):
            for e in range(16):
                t.wait_ge(sg.h, g3 + (e + 1) * 16)
                if e >= 2:
                    t.wait_ge(ss.h, s3 + (e - 1) * 6)
                w = dsl[e % 2]
                quad = [pb[(e % 2) * 4 + i] for i in range(4)]
                for tt in range(4):
                    for k in range(16):
                        i = t.matmul(quad[tt][:], w[:, k * 128:(k + 1) * 128],
                                     ubf[:, k * T + tt * 512: k * T + (tt + 1) * 512],
                                     start=(k == 0), stop=(k == 15))
                    st_.inc(i)
        block.tensor(m3_t)

        def m3_s(s):
            for e in range(16):
                s.wait_ge(st_.h, t3 + (e + 1) * 4)
                if e >= 2:
                    s.wait_ge(sv.h, v3 + (e - 1) * 7)
                quad = [pb[(e % 2) * 4 + i] for i in range(4)]
                for tt in range(4):
                    i = s.activation(ebuf[:, tt * 512:(tt + 1) * 512], quad[tt][:], AF.Exp,
                                     scale=self.cst[:, C_HALF:C_HALF + 1],
                                     bias=self.dtb[:, e:e + 1]) if False else                         s.activation(ebuf[:, tt * 512:(tt + 1) * 512], quad[tt][:], AF.Exp,
                                     scale=self.cst[:, C_HALF:C_HALF + 1],
                                     bias=self.dtb_t[:, e:e + 1])
                    if tt == 3:
                        ss.inc(i)
                s.drain()
                s.activation(spbuf[:], ebuf[:], AF.Ln, bias=self.cst[:, C_ONE:C_ONE + 1])
                s.drain()
                i = s.activation(dbuf[e % 2][:], spbuf[:], AF.Exp,
                                 scale=self.negA[:, e:e + 1], bias=self.nb2[:, e:e + 1])
                ss.inc(i)
                ss.inc(s.nop()); ss.inc(s.nop()); ss.inc(s.nop()); ss.inc(s.nop())
        block.scalar(m3_s)

        def m3_v(v):
            for e in range(16):
                v.wait_ge(ss.h, s3 + (e + 1) * 6)
                v.wait_ge(ld.h, ld3 + (e + 1) * 48)
                if e >= 2:
                    v.wait_ge(sst.h, d3 + (e - 1) * 16)
                v.tensor_scalar(ombuf[:], dbuf[e % 2][:], -1.0, 1.0, ALU.mult, ALU.add)
                v.tensor_tensor(t1buf[:], bld[e % 2][:], ubf[:, e * T:(e + 1) * T], ALU.mult)
                v.drain()
                v.tensor_tensor(incb[:], ombuf[:], t1buf[:], ALU.mult)
                v.drain()
                v.tensor_tensor_scan(stsb[:], dbuf[e % 2][:], incb[:], 0.0, ALU.mult, ALU.add)
                v.drain()
                v.tensor_tensor(t2buf[:], cld[e % 2][:], stsb[:], ALU.mult)
                v.drain()
                v.scalar_tensor_tensor(ypb[:], ubf[:, e * T:(e + 1) * T],
                                       self.dpar_t[:, e:e + 1], t2buf[:], ALU.mult, ALU.add)
                v.drain()
                i = v.scalar_tensor_tensor(y2c[e % 2][:], tzld[e % 2][:], 1.0,
                                           ypb[:], ALU.add, ALU.mult)
                for _ in range(7):
                    pass
                sv.inc(i)
                sv.inc(v.nop()); sv.inc(v.nop()); sv.inc(v.nop())
                sv.inc(v.nop()); sv.inc(v.nop()); sv.inc(v.nop())
        block.vector(m3_v)



        self.barrier()
        ph3.close()
        ph_u.close()

        self.nc.leave_named_scope("m3", sc_m[0], False)
        sc_m = self.nc.enter_named_scope("m4", False)
        ph4 = ExitStack()
        sbl4 = lambda nm, shape, dt: ph4.enter_context(nc.sbuf_tensor(self.name(nm), shape, dt))
        owb = sbl4("owb", [128, 16 * D], FP16)
        xsb4 = sbl4("mx4", [128, TC * D], F32)
        y2g = [sbl4("y2g", [128, 16 * 128], FP16) for _ in range(2)]
        obuf = [sbl4("obuf", [128, D], F32) for _ in range(2)]
        r1b = [sbl4("r1b", [128, D], F32) for _ in range(2)]

        g4 = sg.n

        def m4_g(g):
            sg.inc(g.dma_start(
                out=owb[:].rearrange("p (k d) -> p k d", k=16),
                in_=self.outwT.rearrange("(k p) d -> p k d", p=128)), 16)
        block.gpsimd(m4_g)
        self._load_x(xsb4)

        ld4 = ld.n
        t4 = st_.n
        s4 = ss.n
        v4 = sv.n
        d4 = sst.n

        for c in range(TC):
            def m4_ld(sync, c=c):
                sync.wait_ge(ld.h, ld4 + c * 16)
                if c >= 2:
                    sync.wait_ge(st_.h, t4 + (c - 1) * 2)
                ld.inc(sync.dma_start(
                    out=y2g[c % 2][:].rearrange("p (e q) -> p e q", e=16),
                    in_=self.stg_y2.rearrange("(e p) t -> p e t", p=128)[:, :, c * 128:(c + 1) * 128]), 16)
            block.sync(m4_ld)

        def m4_t(t):
            t.wait_ge(sg.h, g4 + 16)
            for c in range(TC):
                t.wait_ge(ld.h, ld4 + (c + 1) * 16)
                if c >= 2:
                    t.wait_ge(ss.h, s4 + (c - 1) * 2)
                pa, pbk = pb[(c % 2) * 2], pb[(c % 2) * 2 + 1]
                for e in range(16):
                    i1 = t.matmul(pa[:], y2g[c % 2][:, e * 128:(e + 1) * 128],
                                  owb[:, e * D:e * D + 512], start=(e == 0), stop=(e == 15))
                st_.inc(i1)
                for e in range(16):
                    i2 = t.matmul(pbk[:], y2g[c % 2][:, e * 128:(e + 1) * 128],
                                  owb[:, e * D + 512:(e + 1) * D], start=(e == 0), stop=(e == 15))
                st_.inc(i2)
        block.tensor(m4_t)

        def m4_s(s):
            for c in range(TC):
                s.wait_ge(st_.h, t4 + (c + 1) * 2)
                if c >= 2:
                    s.wait_ge(sv.h, v4 + (c - 1) * 2)
                pa, pbk = pb[(c % 2) * 2], pb[(c % 2) * 2 + 1]
                s.activation(obuf[c % 2][:, 0:512], pa[:], AF.Copy,
                             scale=self.cst[:, C_QUART:C_QUART + 1])
                i = s.activation(obuf[c % 2][:, 512:1024], pbk[:], AF.Copy,
                                 scale=self.cst[:, C_QUART:C_QUART + 1])
                ss.inc(i)
                ss.inc(s.nop())
        block.scalar(m4_s)

        def m4_v(v):
            for c in range(TC):
                v.wait_ge(ss.h, s4 + (c + 1) * 2)
                v.tensor_tensor(r1b[c % 2][:], obuf[c % 2][:], self.outb_t[:], ALU.add)
                v.drain()
                i = v.tensor_tensor(xsb4[:, c * D:(c + 1) * D], xsb4[:, c * D:(c + 1) * D],
                                    r1b[c % 2][:], ALU.add)
                sv.inc(i)
                sv.inc(v.nop())
        block.vector(m4_v)

        self._store_x(xsb4, sv.n)
        self.barrier()
        ph4.close()
        self.nc.leave_named_scope("m4", sc_m[0], False)

    class _Quant:

        def __init__(self, B, ph, F, n_par=2):
            nc = B.nc
            self.B = B
            self.F = F
            self.n_par = n_par
            nb = F // 16
            sbl = lambda nm, shape, dt: ph.enter_context(
                nc.sbuf_tensor(B.name(nm), shape, dt))
            self.xn = [sbl("q_xn", [128, F], FP16) for _ in range(n_par)]
            self.an = [sbl("q_an", [128, F], FP16) for _ in range(n_par)]
            self.sg = [sbl("q_sg", [128, F], FP16) for _ in range(n_par)]
            self.sm = [sbl("q_sm", [128, 3 * nb], F32) for _ in range(n_par)]
            self.sc16 = [sbl("q_sc16", [128, nb], FP16) for _ in range(n_par)]
            self.sv_q = Sem(nc, ph, B.name("qv"))
            self.ss_q = Sem(nc, ph, B.name("qs"))
            self.ss_a = Sem(nc, ph, B.name("qa"))
            self.sv_qb = Sem(nc, ph, B.name("qb"))
            self.nA = 0
            self.nS = 0
            self.n = 0

        def emit_A(self, v, src_ap, W=None):
            F = W if W is not None else self.F
            nb, nbm = F // 16, self.F // 16
            i = self.nA
            self.nA += 1
            p = i % self.n_par
            sm = self.sm[p]
            if i >= self.n_par:
                v.wait_ge(self.ss_q.h, i - self.n_par + 1)
            v.drain()
            v.tensor_reduce(sm[:, 0:nb], src_ap.rearrange("p (b w) -> p b w", w=16),
                            AX.X, ALU.max, apply_absolute_value=True)
            v.drain()
            v.tensor_scalar(sm[:, 2 * nbm:2 * nbm + nb], sm[:, 0:nb], 1.0 / 6.0, 1e-12,
                            ALU.mult, ALU.max)
            v.drain()
            v.reciprocal(sm[:, nbm:nbm + nb], sm[:, 2 * nbm:2 * nbm + nb])
            v.tensor_scalar(self.sc16[p][:, 0:nb], sm[:, 2 * nbm:2 * nbm + nb], 1.0, None,
                            ALU.mult)
            v.drain()
            self.sv_q.inc(v.tensor_tensor(
                self.xn[p][:, 0:F], src_ap,
                sm[:, nbm:nbm + nb].rearrange("p b -> p b ()").broadcast_to([128, nb, 16]),
                ALU.mult))

        def emit_S(self, s, W=None):
            F = W if W is not None else self.F
            i = self.nS
            self.nS += 1
            p = i % self.n_par
            s.wait_ge(self.sv_q.h, i + 1)
            if i >= self.n_par:
                s.wait_ge(self.sv_qb.h, i - self.n_par + 1)
            s.drain()
            self.ss_a.inc(s.activation(self.an[p][:, 0:F], self.xn[p][:, 0:F], AF.Abs))
            self.ss_q.inc(s.activation(self.sg[p][:, 0:F], self.xn[p][:, 0:F], AF.Sign))

        def emit_B(self, v, out_ap, W=None):
            F = W if W is not None else self.F
            nb = F // 16
            p = self.n % self.n_par
            an, sg = self.an[p][:, 0:F], self.sg[p][:, 0:F]
            tA, tB, tC, tD = (t[:, 0:F] for t in self.B.qtb)
            v.wait_ge(self.ss_a.h, self.n + 1)
            v.drain()
            v.tensor_scalar(tA, an, 0.25, 0.5, ALU.is_gt, ALU.mult)
            v.tensor_scalar(tB, an, 0.75, 0.5, ALU.is_gt, ALU.mult)
            v.tensor_scalar(tC, an, 1.25, 0.5, ALU.is_gt, ALU.mult)
            v.tensor_scalar(tD, an, 1.75, 0.5, ALU.is_gt, ALU.mult)
            v.drain()
            v.tensor_tensor(tA, tA, tB, ALU.add)
            v.tensor_tensor(tC, tC, tD, ALU.add)
            v.drain()
            v.tensor_scalar(tB, an, 2.5, 1.0, ALU.is_gt, ALU.mult)
            v.tensor_scalar(tD, an, 3.5, 1.0, ALU.is_gt, ALU.mult)
            v.tensor_tensor(tA, tA, tC, ALU.add)
            v.drain()
            v.tensor_scalar(tC, an, 5.0, 2.0, ALU.is_gt, ALU.mult)
            v.tensor_tensor(tB, tB, tD, ALU.add)
            v.drain()
            v.tensor_tensor(tA, tA, tB, ALU.add)
            v.drain()
            v.tensor_tensor(tA, tA, tC, ALU.add)
            v.wait_ge(self.ss_q.h, self.n + 1)
            v.drain()
            v.tensor_tensor(tB, tA, sg, ALU.mult)
            v.drain()
            i = v.tensor_tensor(
                out_ap, tB,
                self.sc16[p][:, 0:nb].rearrange("p b -> p b ()").broadcast_to([128, nb, 16]),
                ALU.mult)
            self.sv_qb.inc(i)
            self.n += 1
            return i

    def _wq_init(self):
        nc = self.nc
        ph = ExitStack()
        sbl = lambda nm, shape, dt: ph.enter_context(nc.sbuf_tensor(self.name(nm), shape, dt))
        self.wq_ph = ph
        self.wq_cin = [sbl("wq_in", [128, 1408], F32) for _ in range(2)]
        self.wq_cout = [sbl("wq_out", [128, 1408], FP16) for _ in range(2)]
        self.qtb = [sbl("qtb", [128, 1408], FP16) for _ in range(4)]
        self.wq_q = self._Quant(self, ph, 1408)
        self.wq_ld = Sem(nc, ph, self.name("wqld"))
        self.wq_st = Sem(nc, ph, self.name("wqst"))
        pieces = []
        for c in range(IC):
            pieces.append((self.wg_in, self.wgq_d, c * 128, 0, D))
        for c in range(IC):
            pieces.append((self.wu_in, self.wuq_d, c * 128, 0, D))
        for c in range(8):
            pieces.append((self.wd_in, self.wdq_d, c * 128, 0, 1408))
            pieces.append((self.wd_in, self.wdq_d, c * 128, 1408, 1408))
        self.wq_pieces = pieces
        self.wq_j = 0
        q = self.wq_q
        self.wq_extras = [q.sv_q, q.ss_q, q.ss_a, q.sv_qb, self.wq_ld, self.wq_st]

    def _wq_ldb(self, g, j):
        srcd, dstd, r0, c0, W = self.wq_pieces[j]
        g.wait_ge(self.wq_ld.h, j * 16)
        if j >= 2:
            g.wait_ge(self.wq_q.sv_q.h, j - 1)
        self.wq_ld.inc(g.dma_start(out=self.wq_cin[j % 2][:, 0:W],
                                   in_=srcd[r0:r0 + 128, c0:c0 + W]), 16)

    def _wq_stb(self, sync, j):
        srcd, dstd, r0, c0, W = self.wq_pieces[j]
        sync.wait_ge(self.wq_st.h, j * 16)
        sync.wait_ge(self.wq_q.sv_qb.h, j + 1)
        self.wq_st.inc(sync.dma_start(out=dstd[r0:r0 + 128, c0:c0 + W],
                                      in_=self.wq_cout[j % 2][:, 0:W]), 16)

    def _wq_step(self):
        NP = len(self.wq_pieces)
        j = self.wq_j
        if j > NP:
            return
        block, q = self.block, self.wq_q
        if j < NP:
            srcd, dstd, r0, c0, W = self.wq_pieces[j]
            if j == 0:
                block.sync(lambda sync: self._wq_ldb(sync, 0))
                block.sync(lambda sync: self._wq_ldb(sync, 1))
            elif j + 1 < NP:
                block.sync(lambda sync, k=j + 1: self._wq_ldb(sync, k))

            def va(v, j=j, W=W):
                v.wait_ge(self.wq_ld.h, (j + 1) * 16)
                q.emit_A(v, self.wq_cin[j % 2][:, 0:W], W)
            block.vector(va)

            def sa(s, W=W):
                q.emit_S(s, W)
            block.scalar(sa)
        if j >= 1:
            srcd, dstd, r0, c0, W = self.wq_pieces[j - 1]

            def vb(v, j=j, W=W):
                if j - 1 >= 2:
                    v.wait_ge(self.wq_st.h, (j - 2) * 16)
                q.emit_B(v, self.wq_cout[(j - 1) % 2][:, 0:W]
                         .rearrange("p (b w) -> p b w", w=16), W)
            block.vector(vb)
            if j - 2 >= 0:
                block.sync(lambda sync, k=j - 2: self._wq_stb(sync, k))
        self.wq_j += 1

    def _wq_drain(self):
        NP = len(self.wq_pieces)
        while self.wq_j <= NP:
            self._wq_step()
        self.block.sync(lambda sync: self._wq_stb(sync, NP - 1))

    def _emit_ffn4(self):
        nc, block = self.nc, self.block
        ld, sv, ss, st_, sst, sg = (self.sems[k] for k in ["ld", "v", "s", "t", "st", "g"])
        pb = self.pb
        stat, cst = self.stat, self.cst

        scope_f41 = nc.enter_named_scope("f41", False)
        ph1 = ExitStack()
        sbl1 = lambda nm, shape, dt: ph1.enter_context(nc.sbuf_tensor(self.name(nm), shape, dt))
        wgT = sbl1("wgT", [128, 8 * I1], FP16)
        wuT = sbl1("wuT", [128, 8 * I1], FP16)
        xcb = [sbl1("f4x", [128, D], F32) for _ in range(2)]
        xnf = sbl1("f4xn", [128, D], F32)
        xq4c = [sbl1("f4xq", [128, D], FP16) for _ in range(2)]
        xq4Tc = [sbl1("f4xqT", [128, 8 * 128], FP16) for _ in range(2)]
        sgb = [sbl1("f4sg", [128, 512], FP16) for _ in range(2)]
        upe = [sbl1("f4up", [128, 512], FP16) for _ in range(2)]
        hb = [sbl1("f4h", [128, I1], F32) for _ in range(2)]
        hqp = [sbl1("f4hq", [128, 1408], FP16) for _ in range(2)]
        self.qtb = [sbl1("qtb", [128, 1408], FP16) for _ in range(4)]
        qx = self._Quant(self, ph1, D)
        qh = self._Quant(self, ph1, 1408)
        se_x = Sem(nc, ph1, self.name("f4ex"))
        s_hst = Sem(nc, ph1, self.name("f4hs"))
        sgh = Sem(nc, ph1, self.name("f4gh"))
        NT = [512, 512, 512, 512, 512, 256]
        NPH = 2 * TC

        ldw0 = ld.n

        def f41_wsync(sync):
            sync.wait_ge(ld.h, ldw0)
            for k in range(8):
                ld.inc(sync.dma_start_transpose(
                    wgT[:, k * I1:(k + 1) * I1], self.wgq_d[:, k * 128:(k + 1) * 128]), 16)
        def f41_wsync2(sync):
            for k in range(8):
                ld.inc(sync.dma_start_transpose(
                    wuT[:, k * I1:(k + 1) * I1], self.wuq_d[:, k * 128:(k + 1) * 128]), 16)
        block.sync(f41_wsync)
        block.sync(f41_wsync2)
        n_wld = ld.n
        n_wsg = sg.n

        lp1 = ld.n
        v0 = sv.n
        for c in range(TC):
            def ld1(sync, c=c):
                sync.wait_ge(ld.h, lp1 + c * 16)
                if c >= 2:
                    sync.wait_ge(sv.h, v0 + (c - 1))
                ld.inc(sync.dma_start(
                    out=xcb[c % 2][:],
                    in_=self.xd.rearrange("(c p) d -> p c d", p=128)[:, c, :]), 16)
            block.sync(ld1)

            def v1(v, c=c):
                v.wait_ge(ld.h, lp1 + (c + 1) * 16)
                v.drain()
                v.tensor_reduce(stat[:, c:c + 1], xcb[c % 2][:], AX.X, ALU.add)
                v.scalar_tensor_tensor(xnf[:], xcb[c % 2][:], 1.0, xcb[c % 2][:],
                                       ALU.mult, ALU.mult,
                                       accum_out=stat[:, 16 + c:17 + c])
                v.drain()
                sv.inc(v.nop())
            block.vector(v1)
        v_p1 = sv.n

        def v_stats(v):
            v.drain()
            v.tensor_scalar(stat[:, 32:48], stat[:, 0:16], 1.0 / D, None, ALU.mult)
            v.drain()
            v.tensor_tensor(stat[:, 64:80], stat[:, 32:48], stat[:, 32:48], ALU.mult)
            v.drain()
            sv.inc(v.scalar_tensor_tensor(stat[:, 48:64], stat[:, 16:32], 1.0 / D,
                                          stat[:, 64:80], ALU.mult, ALU.subtract))
        block.vector(v_stats)
        n_v1 = sv.n

        def sc1(s):
            s.wait_ge(sv.h, n_v1)
            ss.inc(s.activation(stat[:, 80:96], stat[:, 48:64], AF.Sqrt,
                                bias=cst[:, C_EPS:C_EPS + 1]))
        block.scalar(sc1)
        n_s1p = ss.n

        def v_rstd(v):
            v.wait_ge(ss.h, n_s1p)
            sv.inc(v.reciprocal(stat[:, 48:64], stat[:, 80:96]))
            v.wait_ge(sv.h, sv.n)
        block.vector(v_rstd)
        v1b = sv.n
        s1b = ss.n
        lp2 = sg.n

        marks_trans = {}
        marks_gu = {}

        def f4_t(t):
            t.wait_ge(ld.h, n_wld)

            def trans(c):
                t.wait_ge(qx.sv_qb.h, c + 1)
                if c >= 2:
                    t.wait_ge(se_x.h, c - 1)
                b0 = pb[4 + 2 * (c % 2)]
                b1 = pb[5 + 2 * (c % 2)]
                for k in range(8):
                    bank = b0 if k < 4 else b1
                    i = t.matmul(bank[:, (k % 4) * 128:(k % 4) * 128 + 128],
                                 xq4c[c % 2][:, k * 128:(k + 1) * 128],
                                 self.ident[:], start=True, stop=True)
                    if k % 4 == 3:
                        st_.inc(i)
                marks_trans[c] = st_.n

            def gu(c):
                t.wait_ge(se_x.h, c + 1)
                off = 0
                for n, w in enumerate(NT):
                    pg, pu = pb[(n % 2) * 2], pb[(n % 2) * 2 + 1]
                    cnt = c * 6 + n - 1
                    if cnt >= 0:
                        t.wait_ge(ss.h, s1b + 2 * cnt)
                    for k in range(8):
                        ig = t.matmul(pg[:, 0:w], xq4Tc[c % 2][:, k * 128:(k + 1) * 128],
                                      wgT[:, k * I1 + off:k * I1 + off + w],
                                      start=(k == 0), stop=(k == 7))
                    st_.inc(ig)
                    for k in range(8):
                        iu = t.matmul(pu[:, 0:w], xq4Tc[c % 2][:, k * 128:(k + 1) * 128],
                                      wuT[:, k * I1 + off:k * I1 + off + w],
                                      start=(k == 0), stop=(k == 7))
                    st_.inc(iu)
                    marks_gu[(c, n)] = st_.n
                    off += w

            trans(0)
            trans(1)
            for c in range(TC):
                gu(c)
                if c + 2 < TC:
                    trans(c + 2)
        block.tensor(f4_t)

        for c in range(TC + 4):
            if c < TC:
                def s_qxS(s):
                    qx.emit_S(s)
                block.scalar(s_qxS)
            if 1 <= c <= TC:
                def s_evacx(s, cc=c - 1):
                    s.wait_ge(st_.h, marks_trans[cc])
                    b0 = pb[4 + 2 * (cc % 2)]
                    b1 = pb[5 + 2 * (cc % 2)]
                    s.activation(xq4Tc[cc % 2][:, 0:512], b0[:], AF.Copy)
                    se_x.inc(s.activation(xq4Tc[cc % 2][:, 512:1024], b1[:], AF.Copy))
                block.scalar(s_evacx)
            if 2 <= c < TC + 2:
                def s_silu(s, cc=c - 2):
                    for n, w in enumerate(NT):
                        s.wait_ge(st_.h, marks_gu[(cc, n)])
                        pg, pu = pb[(n % 2) * 2], pb[(n % 2) * 2 + 1]
                        prev = cc * 6 + n - 1
                        if prev >= 0:
                            s.wait_ge(sgh.h, prev)
                        ss.inc(s.activation(sgb[n % 2][:, 0:w], pg[:, 0:w], AF.Silu))
                        ss.inc(s.activation(upe[n % 2][:, 0:w], pu[:, 0:w], AF.Copy))
                block.scalar(s_silu)
            if 3 <= c < TC + 3:
                def s_qhS(s, cc=c - 3):
                    for q in range(2):
                        qh.emit_S(s, 1408)
                block.scalar(s_qhS)

        def emit_qh_A(cc, q):
            def vA(v):
                if q == 0:
                    v.wait_ge(sgh.h, (cc + 1) * 6)
                qh.emit_A(v, hb[cc % 2][:, q * 1408:(q + 1) * 1408], 1408)
            block.vector(vA)

        def emit_qh_B(j):
            def vB(v):
                if j >= 2:
                    v.wait_ge(s_hst.h, (j - 1) * 16)
                qh.emit_B(v, hqp[j % 2][:]
                          .rearrange("p (b w) -> p b w", w=16), 1408)
            block.vector(vB)

        for c in range(TC + 4):
            if c < TC:
                def v_norm_qxA(v, c=c):
                    v.wait_ge(sg.h, lp2 + (c + 1) * 16)
                    v.drain()
                    v.tensor_scalar(xnf[:], xcb[c % 2][:],
                                    stat[:, 32 + c:33 + c], stat[:, 48 + c:49 + c],
                                    ALU.subtract, ALU.mult)
                    qx.emit_A(v, xnf[:])
                block.vector(v_norm_qxA)
            if 1 <= c <= TC:
                def v_qxB(v, cc=c - 1):
                    if cc >= 2:
                        v.wait_ge(st_.h, marks_trans[cc - 2])
                    qx.emit_B(v, xq4c[cc % 2][:].rearrange("p (b w) -> p b w", w=16))
                block.vector(v_qxB)
            if 3 <= c < TC + 3:
                cc = c - 3
                for q in range(2):
                    j = cc * 2 + q
                    emit_qh_A(cc, q)
                    if j >= 1:
                        emit_qh_B(j - 1)
        emit_qh_B(NPH - 1)

        def ld2(g, c):
            g.wait_ge(sg.h, lp2 + c * 16)
            if c < 2:
                g.wait_ge(sv.h, v0 + 15 + c)
            else:
                g.wait_ge(qx.sv_q.h, c - 1)
            sg.inc(g.dma_start(
                out=xcb[c % 2][:],
                in_=self.xd.rearrange("(c p) d -> p c d", p=128)[:, c, :]), 16)

        def sth(sync, j):
            cc, q = j // 2, j % 2
            sync.wait_ge(s_hst.h, j * 16)
            sync.wait_ge(qh.sv_qb.h, j + 1)
            s_hst.inc(sync.dma_start(
                out=self.hq_d[cc * 128:(cc + 1) * 128, q * 1408:(q + 1) * 1408],
                in_=hqp[j % 2][:]), 16)

        def g_Vh(g, cc):
            g.drain()
            off = 0
            for n, w in enumerate(NT):
                g.wait_ge(ss.h, s1b + 2 * (cc * 6 + n + 1))
                if n == 0 and cc >= 2:
                    g.wait_ge(qh.sv_q.h, 2 * (cc - 1))
                sgh.inc(g.tensor_tensor(hb[cc % 2][:, off:off + w],
                                        sgb[n % 2][:, 0:w], upe[n % 2][:, 0:w], ALU.mult))
                off += w

        for c in range(TC + 3):
            if c < TC:
                block.gpsimd(lambda g, c=c: ld2(g, c))
            if 3 <= c:
                block.gpsimd(lambda g, cc=c - 3: g_Vh(g, cc))
        for j in range(NPH):
            block.sync(lambda sync, j=j: sth(sync, j))

        self.barrier(extra=[qx.sv_q, qx.ss_q, qx.ss_a, qx.sv_qb,
                            qh.sv_q, qh.ss_q, qh.ss_a, qh.sv_qb, se_x, s_hst, sgh])
        ph1.close()
        nc.leave_named_scope("f41", scope_f41[0] if isinstance(scope_f41, tuple) else scope_f41, False)

        scope_f42 = nc.enter_named_scope("f42", False)
        ph2 = ExitStack()
        sbl2 = lambda nm, shape, dt: ph2.enter_context(nc.sbuf_tensor(self.name(nm), shape, dt))
        hqT = sbl2("hqT", [128, IC * T // 1], BF16) if False else             sbl2("hqT", [128, 22 * T], FP16)
        wdT = sbl2("wdT", [128, 22 * D], FP16)
        obuf = [sbl2("f4o", [128, D], F32) for _ in range(2)]
        xc = [sbl2("f4xc", [128, D], F32) for _ in range(2)]

        ld2 = ld.n

        def f42_w(sync):
            sync.wait_ge(ld.h, ld2)
            for i in range(22):
                ld.inc(sync.dma_start_transpose(
                    hqT[:, i * T:(i + 1) * T], self.hq_d[:, i * 128:(i + 1) * 128]), 16)
            for i in range(22):
                ld.inc(sync.dma_start_transpose(
                    wdT[:, i * D:(i + 1) * D], self.wdq_d[:, i * 128:(i + 1) * 128]), 16)
        block.sync(f42_w)
        n_w2 = ld.n

        t2 = st_.n
        s2 = ss.n
        v2 = sv.n
        ldx2 = ld.n
        stx2 = sst.n

        def f42_xld(sync, c):
            sync.wait_ge(ld.h, ldx2 + c * 16)
            if c >= 2:
                sync.wait_ge(sst.h, stx2 + (c - 1) * 16)
            ld.inc(sync.dma_start(
                out=xc[c % 2][:],
                in_=self.xd.rearrange("(c p) d -> p c d", p=128)[:, c, :]), 16)

        def f42_xst(sync, c):
            sync.wait_ge(sst.h, stx2 + c * 16)
            sync.wait_ge(sv.h, v2 + c + 1)
            sst.inc(sync.dma_start(
                out=self.xd.rearrange("(c p) d -> p c d", p=128)[:, c, :],
                in_=xc[c % 2][:]), 16)

        for c in range(TC):
            if c >= 2:
                block.sync(lambda sync, c=c: f42_xst(sync, c - 2))
            block.sync(lambda sync, c=c: f42_xld(sync, c))
        block.sync(lambda sync: f42_xst(sync, TC - 2))
        block.sync(lambda sync: f42_xst(sync, TC - 1))

        def f42_t(t):
            t.wait_ge(ld.h, n_w2)
            for c in range(TC):
                if c >= 2:
                    t.wait_ge(ss.h, s2 + (c - 1) * 2)
                pa, pbk = pb[(c % 2) * 2], pb[(c % 2) * 2 + 1]
                for i in range(22):
                    i1 = t.matmul(pa[:], hqT[:, i * T + c * 128:i * T + (c + 1) * 128],
                                  wdT[:, i * D:i * D + 512], start=(i == 0), stop=(i == 21))
                st_.inc(i1)
                for i in range(22):
                    i2 = t.matmul(pbk[:], hqT[:, i * T + c * 128:i * T + (c + 1) * 128],
                                  wdT[:, i * D + 512:(i + 1) * D], start=(i == 0), stop=(i == 21))
                st_.inc(i2)
        block.tensor(f42_t)

        def f42_s(s):
            for c in range(TC):
                s.wait_ge(st_.h, t2 + (c + 1) * 2)
                if c >= 2:
                    s.wait_ge(sv.h, v2 + (c - 1))
                pa, pbk = pb[(c % 2) * 2], pb[(c % 2) * 2 + 1]
                s.activation(obuf[c % 2][:, 0:512], pa[:], AF.Copy)
                ss.inc(s.activation(obuf[c % 2][:, 512:1024], pbk[:], AF.Copy))
                ss.inc(s.nop())
        block.scalar(f42_s)

        def f42_v(v):
            for c in range(TC):
                v.wait_ge(ss.h, s2 + (c + 1) * 2)
                v.wait_ge(ld.h, ldx2 + (c + 1) * 16)
                v.drain()
                sv.inc(v.scalar_tensor_tensor(xc[c % 2][:], obuf[c % 2][:], 0.5,
                                              xc[c % 2][:], ALU.mult, ALU.add))
        block.vector(f42_v)

        self.barrier()
        ph2.close()
        nc.leave_named_scope("f42", scope_f42[0] if isinstance(scope_f42, tuple) else scope_f42, False)

    def _emit_attn(self):
        nc, block = self.nc, self.block
        ld, sv, ss, st_, sst, sg = (self.sems[k] for k in ["ld", "v", "s", "t", "st", "g"])
        pb = self.pb

        phh = ExitStack()
        h_tm = phh.enter_context(nc.sbuf_tensor(self.name("h_tm"), [128, TC * D], FP16))
        hT = phh.enter_context(nc.sbuf_tensor(self.name("hT"), [128, 8 * T], FP16))
        ph0 = ExitStack()
        xsb = ph0.enter_context(nc.sbuf_tensor(self.name("atx"), [128, TC * D], F32))
        self._load_x(xsb)
        self._emit_ln(xsb, h_tm)
        n_ln = sv.n

        def a0_stg(sync):
            sync.wait_ge(sv.h, n_ln)
            sst.inc(sync.dma_start(
                out=self.stg_td16.rearrange("(c p) d -> p c d", p=128),
                in_=h_tm[:].rearrange("p (c d) -> p c d", c=TC)), 16)
            sync.wait_ge(sst.h, sst.n)
            for k in range(8):
                sst.inc(sync.dma_start_transpose(
                    hT[:, k * T:(k + 1) * T], self.stg_td16[:, k * 128:(k + 1) * 128]), 16)
        block.sync(a0_stg)
        n_hT = sst.n
        self.barrier()
        ph0.close()

        ph1 = ExitStack()
        sbl = lambda nm, shape, dt: ph1.enter_context(nc.sbuf_tensor(self.name(nm), shape, dt))
        Gb = sbl("Gb", [128, 8 * D], FP16)
        qTb = sbl("qTb", [128, 8 * D], FP16)
        qfm = sbl("qfm", [128, 8 * T], FP16)
        obuf = [sbl("ato", [128, D], F32) for _ in range(2)]
        xc = [sbl("atxc", [128, D], F32) for _ in range(2)]

        g1 = sg.n

        def a_qw(g):
            sg.inc(g.dma_start(
                out=qTb[:].rearrange("p (k d) -> p k d", k=8),
                in_=self.qT_in.rearrange("(k p) d -> p k d", p=128)), 16)
        block.gpsimd(a_qw)

        t1 = st_.n
        s1 = ss.n

        def a1_t(t):
            for d1 in range(8):
                for n2 in range(2):
                    bk = pb[(d1 * 2 + n2) % 8]
                    if d1 * 2 + n2 >= 8:
                        t.wait_ge(ss.h, s1 + (d1 * 2 + n2) - 8 + 1)
                    for tk in range(TC):
                        i = t.matmul(bk[:], h_tm[:, tk * D + d1 * 128:tk * D + (d1 + 1) * 128],
                                     h_tm[:, tk * D + n2 * 512:tk * D + (n2 + 1) * 512],
                                     start=(tk == 0), stop=(tk == TC - 1))
                    st_.inc(i)
            t.wait_ge(sg.h, g1 + 16)
            t.wait_ge(sst.h, n_hT)
            t.wait_ge(ss.h, s1 + 16)
            for e in range(8):
                for tt in range(4):
                    bk = pb[(e % 2) * 4 + tt]
                    if e >= 2:
                        t.wait_ge(ss.h, s1 + 16 + (e - 1) * 4)
                    for k in range(8):
                        i = t.matmul(bk[:], qTb[:, k * D + e * 128:k * D + (e + 1) * 128],
                                     hT[:, k * T + tt * 512:k * T + (tt + 1) * 512],
                                     start=(k == 0), stop=(k == 7))
                    st_.inc(i)
        block.tensor(a1_t)

        def a1_s(s):
            for d1 in range(8):
                for n2 in range(2):
                    bk = pb[(d1 * 2 + n2) % 8]
                    s.wait_ge(st_.h, t1 + d1 * 2 + n2 + 1)
                    ss.inc(s.activation(Gb[:, d1 * D + n2 * 512:d1 * D + (n2 + 1) * 512],
                                        bk[:], AF.Copy))
            for e in range(8):
                for tt in range(4):
                    bk = pb[(e % 2) * 4 + tt]
                    s.wait_ge(st_.h, t1 + 16 + e * 4 + tt + 1)
                    ss.inc(s.activation(qfm[:, e * T + tt * 512:e * T + (tt + 1) * 512],
                                        bk[:], AF.Copy))
        block.scalar(a1_s)
        n_g_q = ss.n
        self.barrier()

        t3 = st_.n
        s3 = ss.n
        v3 = sv.n
        ldx3 = ld.n
        stx3 = sst.n

        def a3_xld(sync, c):
            sync.wait_ge(ld.h, ldx3 + c * 16)
            if c >= 2:
                sync.wait_ge(sst.h, stx3 + (c - 1) * 16)
            ld.inc(sync.dma_start(
                out=xc[c % 2][:],
                in_=self.xd.rearrange("(c p) d -> p c d", p=128)[:, c, :]), 16)

        def a3_xst(sync, c):
            sync.wait_ge(sst.h, stx3 + c * 16)
            sync.wait_ge(sv.h, v3 + c + 1)
            sst.inc(sync.dma_start(
                out=self.xd.rearrange("(c p) d -> p c d", p=128)[:, c, :],
                in_=xc[c % 2][:]), 16)

        for c in range(TC):
            if c >= 2:
                block.sync(lambda sync, c=c: a3_xst(sync, c - 2))
            block.sync(lambda sync, c=c: a3_xld(sync, c))
        block.sync(lambda sync: a3_xst(sync, TC - 2))
        block.sync(lambda sync: a3_xst(sync, TC - 1))

        def a3_t(t):
            for c in range(TC):
                if c >= 2:
                    t.wait_ge(ss.h, s3 + (c - 1) * 2)
                pa, pbk = pb[(c % 2) * 2], pb[(c % 2) * 2 + 1]
                for e in range(8):
                    i1 = t.matmul(pa[:], qfm[:, e * T + c * 128:e * T + (c + 1) * 128],
                                  Gb[:, e * D:e * D + 512], start=(e == 0), stop=(e == 7))
                st_.inc(i1)
                for e in range(8):
                    i2 = t.matmul(pbk[:], qfm[:, e * T + c * 128:e * T + (c + 1) * 128],
                                  Gb[:, e * D + 512:(e + 1) * D], start=(e == 0), stop=(e == 7))
                st_.inc(i2)
        block.tensor(a3_t)

        def a3_s(s):
            for c in range(TC):
                s.wait_ge(st_.h, t3 + (c + 1) * 2)
                if c >= 2:
                    s.wait_ge(sv.h, v3 + (c - 1))
                pa, pbk = pb[(c % 2) * 2], pb[(c % 2) * 2 + 1]
                s.activation(obuf[c % 2][:, 0:512], pa[:], AF.Copy)
                ss.inc(s.activation(obuf[c % 2][:, 512:1024], pbk[:], AF.Copy))
                ss.inc(s.nop())
        block.scalar(a3_s)

        def a3_v(v):
            for c in range(TC):
                v.wait_ge(ss.h, s3 + (c + 1) * 2)
                v.wait_ge(ld.h, ldx3 + (c + 1) * 16)
                v.drain()
                sv.inc(v.tensor_tensor(xc[c % 2][:], xc[c % 2][:], obuf[c % 2][:], ALU.add))
        block.vector(a3_v)

        self.barrier()
        ph1.close()
        phh.close()

    def _emit_fin(self):
        nc, block = self.nc, self.block
        ld, sfin, sst = self.sems["ld"], self.sems["fin"], self.sems["st"]
        ph = ExitStack()
        fbuf = ph.enter_context(nc.sbuf_tensor("fbuf", [128, TC * D], F32))

        def fin_sync(sync):
            ld.inc(sync.dma_start(
                out=fbuf[:].rearrange("p (c d) -> p c d", c=TC),
                in_=self.xd.rearrange("(c p) d -> p c d", p=128)), 16)
            sync.wait_ge(ld.h, ld.n)
            sfin.inc(sync.dma_start(
                out=self.out.rearrange("(c p) d -> p c d", p=128),
                in_=fbuf[:].rearrange("p (c d) -> p c d", c=TC)), 16)
            sync.wait_ge(sfin.h, sfin.n)
        block.sync(fin_sync)
        ph.close()


def prep_core_inputs(xb, W):
    f = np.ascontiguousarray
    cvec = lambda a, nch: f(np.asarray(a, np.float32).reshape(nch, 128).T)
    consts = np.zeros((128, 8), np.float32)
    consts[:, C_EPS] = 1e-5
    consts[:, C_I1024] = 1.0 / 1024
    consts[:, C_HALF] = 0.5
    consts[:, C_QUART] = 0.25
    consts[:, C_ONE] = 1.0
    return dict(
        x=f(np.asarray(xb, np.float32)),
        winT=f(np.asarray(W["L0_attn_in_proj_w"], np.float32).T),
        inb=cvec(W["L0_attn_in_proj_b"], 32),
        convw=f(np.asarray(W["L0_attn_conv_w"], np.float32).reshape(16, 128, 4)
                .transpose(1, 0, 2).reshape(128, 64)),
        convb=cvec(W["L0_attn_conv_b"], 16),
        dtwT=f(np.asarray(W["L0_attn_dt_w"], np.float32).T),
        bwT=f(np.asarray(W["L0_attn_b_w"], np.float32).T),
        cwT=f(np.asarray(W["L0_attn_c_w"], np.float32).T),
        dtb=cvec(W["L0_attn_dt_b"], 16),
        alog=cvec(W["L0_attn_A_log"], 16),
        dpar=cvec(W["L0_attn_D_param"], 16),
        outwT=f(np.asarray(W["L0_attn_out_w"], np.float32).T),
        outb_rep=f(np.broadcast_to(np.asarray(W["L0_attn_out_b"], np.float32), (128, D))),
        kern=f(np.asarray(W["L0_ffn_kernel"])),
        sin_c=f(np.broadcast_to(np.asarray(W["L0_ffn_input_scale"], np.float32).reshape(1, 1),
                                (128, 1))),
        skl_c=f(np.broadcast_to(np.asarray(W["L0_ffn_kernel_scale"], np.float32).reshape(1, 1),
                                (128, 1))),
        fbias_rep=f(np.broadcast_to(np.asarray(W["L0_ffn_bias"]), (128, D))),
        qT=f(np.asarray(W["L1_attn_queries"], np.float32).T),
        wg=f(np.asarray(W["L1_ffn_w_gate"], np.float32)),
        wu=f(np.asarray(W["L1_ffn_w_up"], np.float32)),
        wd=f(np.asarray(W["L1_ffn_w_down"], np.float32)),
        consts=consts,
        ident=np.eye(128, dtype=np.float16),
    )


def run(inputs, n_stages=6, n_cores=8):
    nc = Builder(n_stages).build()
    x = np.asarray(inputs["L0_attn_x"])
    in_maps = [prep_core_inputs(x[b], inputs) for b in range(n_cores)]
    res = run_bass_kernel_spmd(nc, in_maps, core_ids=list(range(n_cores)))
    return np.stack([r["out"] for r in res.results], axis=0)




def kernel(**inputs):
    nc = Builder(6).build()
    x = np.asarray(inputs["L0_attn_x"])
    in_maps = [prep_core_inputs(x[b], inputs) for b in range(8)]
    res = run_bass_kernel_spmd(nc, in_maps, core_ids=list(range(8)))
    return np.stack([np.asarray(r["out"]) for r in res.results], axis=0)


# version 16
# speedup vs baseline: 1.2078x; 1.0316x over previous
import numpy as np
import ml_dtypes
import concourse.bass as bass
import concourse.mybir as mybir
from concourse.bass_utils import run_bass_kernel_spmd
from contextlib import ExitStack

F32 = mybir.dt.float32
BF16 = mybir.dt.bfloat16
FP16 = mybir.dt.float16
FP8 = mybir.dt.float8e4
U16 = mybir.dt.uint16
AF = mybir.ActivationFunctionType
ALU = mybir.AluOpType
AX = mybir.AxisListType

T = 2048
D = 1024
TC = T // 128
DI = 2048
E4 = 4096
I1 = 2816
IC = I1 // 128

C_EPS, C_I1024, C_HALF, C_QUART, C_ONE = 0, 1, 2, 3, 4


class Sem:
    def __init__(self, nc, es, name):
        self.h = es.enter_context(nc.semaphore(name))
        self.n = 0

    def inc(self, instr, k=1):
        instr.then_inc(self.h, k)
        self.n += k
        return instr


class Builder:
    def __init__(self, n_stages=6, wq_force=False):
        self.n_stages = n_stages
        self.wq_on = False
        self.wq_force = wq_force
        self.uid = 0

    def name(self, tag):
        self.uid += 1
        return f"{tag}_{self.uid}"

    def build(self):
        nc = bass.Bass("TRN2", target_bir_lowering=False)
        self.nc = nc
        dram = lambda name, shape, dt, kind: nc.dram_tensor(name, shape, dt, kind=kind).ap()
        IN, OUT, SC = "ExternalInput", "ExternalOutput", "Internal"

        self.x_in = dram("x", [T, D], F32, IN)
        self.winT = dram("winT", [D, E4], F32, IN)
        self.inb = dram("inb", [128, 32], F32, IN)
        self.convw = dram("convw", [128, 64], F32, IN)
        self.convb = dram("convb", [128, 16], F32, IN)
        self.dtwT = dram("dtwT", [DI, DI], F32, IN)
        self.bwT = dram("bwT", [DI, DI], F32, IN)
        self.cwT = dram("cwT", [DI, DI], F32, IN)
        self.dtb = dram("dtb", [128, 16], F32, IN)
        self.alog = dram("alog", [128, 16], F32, IN)
        self.dpar = dram("dpar", [128, 16], F32, IN)
        self.outwT = dram("outwT", [DI, D], F32, IN)
        self.outb_rep = dram("outb_rep", [128, D], F32, IN)
        self.kern_in = dram("kern", [D, D], BF16, IN)
        self.sin_c = dram("sin_c", [128, 1], F32, IN)
        self.skl_c = dram("skl_c", [128, 1], F32, IN)
        self.fbias_rep = dram("fbias_rep", [128, D], BF16, IN)
        self.qT_in = dram("qT", [D, D], F32, IN)
        self.wg_in = dram("wg", [I1, D], F32, IN)
        self.wu_in = dram("wu", [I1, D], F32, IN)
        self.wd_in = dram("wd", [D, I1], F32, IN)
        self.consts = dram("consts", [128, 8], F32, IN)
        self.ident_in = dram("ident", [128, 128], FP16, IN)

        self.out = dram("out", [T, D], F32, OUT)

        self.xd = dram("xd", [T, D], F32, SC)
        self.stg_td = dram("stg_td", [T, D], BF16, SC)
        self.stg_td16 = dram("stg_td16", [T, D], FP16, SC)
        self.stg_ti = dram("stg_ti", [T, I1], BF16, SC)
        self.stg_y2 = dram("stg_y2", [DI, T], FP16, SC)
        self.stg_tz = dram("stg_tz", [DI, T], FP16, SC)
        self.stg_b = dram("stg_b", [DI, T], FP16, SC)
        self.stg_c = dram("stg_c", [DI, T], FP16, SC)
        self.wgq_d = dram("wgq_d", [I1, D], FP16, SC)
        self.wuq_d = dram("wuq_d", [I1, D], FP16, SC)
        self.wdq_d = dram("wdq_d", [D, I1], FP16, SC)
        self.hq_d = dram("hq_d", [T, I1], FP16, SC)

        es = ExitStack()
        self.es = es
        with es:
            self._alloc_persistent()
            self.block = es.enter_context(nc.Block())
            with nc.named_scope("p0"):
                self._emit_p0()
            if self.n_stages >= 1:
                with nc.named_scope("ffn8a"):
                    self._emit_ffn8()
            if self.n_stages >= 2:
                self.wq_on = self.n_stages >= 4 or self.wq_force
                with nc.named_scope("mixer"):
                    self._emit_mixer()
            if self.n_stages >= 3:
                with nc.named_scope("ffn8b"):
                    self._emit_ffn8()
            if self.n_stages >= 4:
                with nc.named_scope("ffn4a"):
                    self._emit_ffn4()
            if self.n_stages >= 5:
                with nc.named_scope("attn"):
                    self._emit_attn()
            if self.n_stages >= 6:
                with nc.named_scope("ffn4b"):
                    self._emit_ffn4()
            with nc.named_scope("fin"):
                self._emit_fin()
        return nc

    def _alloc_persistent(self):
        nc, es = self.nc, self.es
        sb = lambda nm, shape, dt: es.enter_context(nc.sbuf_tensor(nm, shape, dt))
        self.cst = sb("cst", [128, 8], F32)
        self.inb_t = sb("inb_t", [128, 32], F32)
        self.convw_t = sb("convw_t", [128, 64], F32)
        self.convb_t = sb("convb_t", [128, 16], F32)
        self.dtb_t = sb("dtb_t", [128, 16], F32)
        self.alog_t = sb("alog_t", [128, 16], F32)
        self.negA = sb("negA", [128, 16], F32)
        self.nb2 = sb("nb2", [128, 16], F32)
        self.hbz = sb("hbz", [128, 16], F32)
        self.dpar_t = sb("dpar_t", [128, 16], F32)
        self.outb_t = sb("outb_t", [128, D], F32)
        self.fbias_t = sb("fbias_t", [128, D], BF16)
        self.sin_t = sb("sin_t", [128, 1], F32)
        self.skl_t = sb("skl_t", [128, 1], F32)
        self.rs_in = sb("rs_in", [128, 1], F32)
        self.rs_k = sb("rs_k", [128, 1], F32)
        self.sprod = sb("sprod", [128, 1], F32)
        self.stat = sb("stat", [128, 96], F32)
        self.tmp16 = sb("tmp16", [128, 16], F32)
        self.ident = sb("ident_sb", [128, 128], FP16)
        self.pb = [es.enter_context(nc.psum_tensor(f"pb{i}", [128, 512], F32))
                   for i in range(8)]
        self.sems = {n: Sem(nc, es, n) for n in
                     ["ld", "v", "s", "t", "st", "ph", "fin", "g"]}

    def barrier(self, extra=()):
        sph = self.sems["ph"]
        waits = [(self.sems[k].h, self.sems[k].n) for k in ["ld", "v", "s", "t", "st", "g"]]
        waits += [(s.h, s.n) for s in extra]
        for eng in ["vector", "scalar", "tensor", "sync", "gpsimd"]:
            def body(e):
                for h, n in waits:
                    e.wait_ge(h, n)
                sph.inc(e.nop())
            getattr(self.block, eng)(body)
        n = sph.n
        for eng in ["vector", "scalar", "tensor", "sync", "gpsimd"]:
            def body2(e):
                e.wait_ge(sph.h, n)
            getattr(self.block, eng)(body2)

    def _emit_p0(self):
        nc, block = self.nc, self.block
        ld, sv, ss = self.sems["ld"], self.sems["v"], self.sems["s"]

        def p0_sync(sync):
            for dst, src in [(self.cst, self.consts), (self.inb_t, self.inb),
                             (self.convw_t, self.convw), (self.convb_t, self.convb),
                             (self.dtb_t, self.dtb), (self.alog_t, self.alog),
                             (self.dpar_t, self.dpar), (self.outb_t, self.outb_rep),
                             (self.fbias_t, self.fbias_rep), (self.sin_t, self.sin_c),
                             (self.skl_t, self.skl_c)]:
                ld.inc(sync.dma_start(out=dst[:], in_=src), 16)
            ld.inc(sync.dma_start(out=self.ident[:], in_=self.ident_in), 16)
            ld.inc(sync.dma_start(out=self.xd, in_=self.x_in), 16)
            sync.wait_ge(ld.h, ld.n)
        block.sync(p0_sync)
        n_ld = ld.n

        def p0_scalar(s):
            s.wait_ge(ld.h, n_ld)
            ss.inc(s.activation(self.tmp16[:], self.alog_t[:], AF.Exp))
        block.scalar(p0_scalar)
        n_s = ss.n

        def p0_vector(v):
            v.wait_ge(ld.h, n_ld)
            v.reciprocal(self.rs_in[:], self.sin_t[:])
            v.reciprocal(self.rs_k[:], self.skl_t[:])
            v.tensor_tensor(self.sprod[:], self.sin_t[:], self.skl_t[:], ALU.mult)
            v.tensor_scalar(self.hbz[:], self.inb_t[:, 16:32], 0.5, None, ALU.mult)
            v.wait_ge(ss.h, n_s)
            v.tensor_scalar(self.nb2[:], self.tmp16[:], -1e-4, None, ALU.mult)
            sv.inc(v.tensor_scalar(self.negA[:], self.tmp16[:], -1.0, None, ALU.mult))
            v.wait_ge(sv.h, sv.n)
        block.vector(p0_vector)
        self.n_consts_ready = sv.n
        self.barrier()

    def _emit_ln(self, xsb, xn_out):
        block = self.block
        sv, ss = self.sems["v"], self.sems["s"]
        stat, cst = self.stat, self.cst

        def vec1(v):
            for c in range(TC):
                xc = xsb[:, c * D:(c + 1) * D]
                v.tensor_reduce(stat[:, c:c + 1], xc, AX.X, ALU.add)
                v.scalar_tensor_tensor(xn_out[:, c * D:(c + 1) * D], xc, 1.0, xc,
                                       ALU.mult, ALU.mult,
                                       accum_out=stat[:, 16 + c:17 + c])
            v.drain()
            v.tensor_scalar(stat[:, 32:48], stat[:, 0:16], 1.0 / D, None, ALU.mult)
            v.drain()
            v.tensor_tensor(stat[:, 64:80], stat[:, 32:48], stat[:, 32:48], ALU.mult)
            v.drain()
            sv.inc(v.scalar_tensor_tensor(stat[:, 48:64], stat[:, 16:32], 1.0 / D,
                                          stat[:, 64:80], ALU.mult, ALU.subtract))
        block.vector(vec1)
        n_v1 = sv.n

        def sc1(s):
            s.wait_ge(sv.h, n_v1)
            ss.inc(s.activation(stat[:, 80:96], stat[:, 48:64], AF.Sqrt,
                                bias=cst[:, C_EPS:C_EPS + 1]))
        block.scalar(sc1)
        n_s1 = ss.n

        def vec2(v):
            v.wait_ge(ss.h, n_s1)
            sv.inc(v.reciprocal(stat[:, 48:64], stat[:, 80:96]))
            v.wait_ge(sv.h, sv.n)
            for c in range(TC):
                v.tensor_scalar(xn_out[:, c * D:(c + 1) * D], xsb[:, c * D:(c + 1) * D],
                                stat[:, 32 + c:33 + c], stat[:, 48 + c:49 + c],
                                ALU.subtract, ALU.mult)
            v.drain()
            sv.inc(v.nop())
        block.vector(vec2)
        return sv.n

    def _load_x(self, xsb):
        ld = self.sems["ld"]

        sst = self.sems["st"]

        n0 = ld.n

        def ldx(sync):
            sync.wait_ge(sst.h, sst.n)
            sync.wait_ge(ld.h, n0)
            ld.inc(sync.dma_start(
                out=xsb[:].rearrange("p (c d) -> p c d", c=TC),
                in_=self.xd.rearrange("(c p) d -> p c d", p=128)), 16)
        self.block.sync(ldx)
        n = ld.n

        def vwait(v):
            v.wait_ge(ld.h, n)
        self.block.vector(vwait)

    def _store_x(self, xsb, n_ready, to_out=False):
        sst = self.sems["st"]
        sv = self.sems["v"]
        dst = self.out if to_out else self.xd

        def stx(sync):
            sync.wait_ge(sv.h, n_ready)
            sst.inc(sync.dma_start(
                out=dst.rearrange("(c p) d -> p c d", p=128),
                in_=xsb[:].rearrange("p (c d) -> p c d", c=TC)), 16)
        self.block.sync(stx)
        return sst.n

    def _emit_ffn8(self):
        nc, block = self.nc, self.block
        ld, sv, ss, st_, sst = (self.sems[k] for k in ["ld", "v", "s", "t", "st"])
        ph = ExitStack()
        sbl = lambda nm, shape, dt: ph.enter_context(nc.sbuf_tensor(self.name(nm), shape, dt))
        xsb = sbl("x", [128, TC * D], F32)
        xnb = sbl("xn", [128, TC * D], BF16)
        q8 = sbl("q8", [128, TC * D], FP8)
        xq8T = sbl("xq8T", [128, 8 * T], BF16)
        ybuf = sbl("y", [128, 2 * D], BF16)
        y2buf = sbl("y2", [128, 2 * D], BF16)
        k8b = sbl("k8b", [128, 8 * D], BF16)
        kf8 = sbl("kf8", [128, 8 * D], FP8)
        pb = self.pb

        ldk0 = ld.n

        def k8_ld(sync):
            sync.wait_ge(ld.h, ldk0)
            ld.inc(sync.dma_start(
                out=k8b[:].rearrange("p (k d) -> p k d", k=8),
                in_=self.kern_in.rearrange("(k p) d -> p k d", p=128)), 16)
        block.sync(k8_ld)
        n_k8ld = ld.n

        self._load_x(xsb)
        self._emit_ln(xsb, xnb)

        def vq(v):
            v.drain()
            sv.inc(v.tensor_scalar(q8[:], xnb[:], self.rs_in[:, 0:1], None, ALU.mult))
        block.vector(vq)
        n_vq = sv.n

        def k8_v(v):
            v.wait_ge(ld.h, n_k8ld)
            v.drain()
            sv.inc(v.tensor_scalar(kf8[:], k8b[:], self.rs_k[:, 0:1], None, ALU.mult))
        block.vector(k8_v)
        n_k8v = sv.n

        def sq_(s):
            s.wait_ge(sv.h, n_vq)
            ss.inc(s.activation(xnb[:], q8[:], AF.Copy))
            s.wait_ge(sv.h, n_k8v)
            s.drain()
            ss.inc(s.activation(k8b[:], kf8[:], AF.Copy))
        block.scalar(sq_)
        n_sq = ss.n
        n_k8 = ss.n

        def v_launder(v):
            v.wait_ge(ss.h, n_sq)
            v.drain()
            v.tensor_copy(xnb[:].bitcast(U16), xnb[:].bitcast(U16))
            v.drain()
            sv.inc(v.tensor_copy(k8b[:].bitcast(U16), k8b[:].bitcast(U16)))
        block.vector(v_launder)
        n_lau = sv.n

        def stg(sync):
            sync.wait_ge(sv.h, n_lau)
            sst.inc(sync.dma_start(
                out=self.stg_td.rearrange("(c p) d -> p c d", p=128),
                in_=xnb[:].rearrange("p (c d) -> p c d", c=TC)), 16)
            sync.wait_ge(sst.h, sst.n)
            for k in range(8):
                sst.inc(sync.dma_start_transpose(
                    xq8T[:, k * T:(k + 1) * T], self.stg_td[:, k * 128:(k + 1) * 128]), 16)
        block.sync(stg)
        n_stg = sst.n

        evac_base = ss.n
        mm_base = st_.n


        def mm(t):
            t.wait_ge(sst.h, n_stg)
            t.wait_ge(sv.h, n_lau)
            for c in range(TC):
                if c >= 2:
                    t.wait_ge(ss.h, evac_base + (c - 2) * 2 + 2)
                pa, pbk = pb[(c % 2) * 2], pb[(c % 2) * 2 + 1]
                for k in range(8):
                    i1 = t.matmul(pa[:], xq8T[:, k * T + c * 128: k * T + (c + 1) * 128],
                                  k8b[:, k * D:k * D + 512],
                                  start=(k == 0), stop=(k == 7))
                st_.inc(i1)
                for k in range(8):
                    i2 = t.matmul(pbk[:], xq8T[:, k * T + c * 128: k * T + (c + 1) * 128],
                                  k8b[:, k * D + 512:(k + 1) * D],
                                  start=(k == 0), stop=(k == 7))
                st_.inc(i2)
        block.tensor(mm)

        vres_base = sv.n

        def ev(s):
            for c in range(TC):
                s.wait_ge(st_.h, mm_base + (c + 1) * 2)
                if c >= 2:
                    s.wait_ge(sv.h, vres_base + (c - 1))
                pa, pbk = pb[(c % 2) * 2], pb[(c % 2) * 2 + 1]
                yb = ybuf[:, (c % 2) * D:(c % 2 + 1) * D]
                s.activation(yb[:, 0:512], pa[:], AF.Copy, scale=self.sprod[:, 0:1])
                ss.inc(s.activation(yb[:, 512:1024], pbk[:], AF.Copy,
                                    scale=self.sprod[:, 0:1]))
                ss.inc(s.nop())
        block.scalar(ev)

        def vres(v):
            for c in range(TC):
                v.wait_ge(ss.h, evac_base + c * 2 + 2)
                yb = ybuf[:, (c % 2) * D:(c % 2 + 1) * D]
                y2 = y2buf[:, (c % 2) * D:(c % 2 + 1) * D]
                v.tensor_tensor(y2, yb, self.fbias_t[:], ALU.add)
                v.drain()
                v.tensor_copy(y2.bitcast(U16), y2.bitcast(U16))
                v.drain()
                sv.inc(v.scalar_tensor_tensor(
                    xsb[:, c * D:(c + 1) * D], y2, 0.5,
                    xsb[:, c * D:(c + 1) * D], ALU.mult, ALU.add))
        block.vector(vres)

        self._store_x(xsb, sv.n)
        self.barrier()
        ph.close()


    def _emit_mixer(self):
        nc, block = self.nc, self.block
        ld, sv, ss, st_, sst, sg = (self.sems[k] for k in ["ld", "v", "s", "t", "st", "g"])
        pb = self.pb

        sc_m = self.nc.enter_named_scope("m0", False)
        ph = ExitStack()
        sbl = lambda nm, shape, dt: ph.enter_context(nc.sbuf_tensor(self.name(nm), shape, dt))
        xsb = sbl("mx", [128, TC * D], F32)
        xnb = sbl("mxn", [128, TC * D], FP16)
        self._load_x(xsb)
        self._emit_ln(xsb, xnb)
        n_ln = self.sems["v"].n

        def stg0(sync):
            sync.wait_ge(sv.h, n_ln)
            sst.inc(sync.dma_start(
                out=self.stg_td16.rearrange("(c p) d -> p c d", p=128),
                in_=xnb[:].rearrange("p (c d) -> p c d", c=TC)), 16)
        block.sync(stg0)
        self.barrier()
        ph.close()
        self.nc.leave_named_scope("m0", sc_m[0], False)
        sc_m = self.nc.enter_named_scope("m1", False)

        ph_u = ExitStack()
        ubf = ph_u.enter_context(nc.sbuf_tensor(self.name("ubf"), [128, 16 * T], FP16))
        if self.wq_on:
            self._wq_init()
        ph = ExitStack()
        sbl = lambda nm, shape, dt: ph.enter_context(nc.sbuf_tensor(self.name(nm), shape, dt))
        xnT = sbl("xnT", [128, 8 * T], FP16)
        wsl = [sbl("wsl", [128, 8 * 128], FP16) for _ in range(2)]
        xzpad = sbl("xzpad", [128, 3 + T], FP16)
        cacc = [sbl("cacc", [128, T], F32) for _ in range(2)]
        thb = [sbl("thb", [128, T], FP16) for _ in range(2)]
        tzc = [sbl("tzc", [128, T], FP16) for _ in range(2)]
        sem_evac = Sem(nc, ph, self.name("m1ev"))
        sem_th = Sem(nc, ph, self.name("m1th"))
        sem_tz = Sem(nc, ph, self.name("m1tz"))
        sem_conv = Sem(nc, ph, self.name("m1cv"))
        sem_u = Sem(nc, ph, self.name("m1u"))

        def m1_tload(sync):
            sync.wait_ge(sst.h, sst.n)
            for k in range(8):
                sst.inc(sync.dma_start_transpose(
                    xnT[:, k * T:(k + 1) * T], self.stg_td16[:, k * 128:(k + 1) * 128]), 16)
        block.sync(m1_tload)
        n_xnT = sst.n

        g_base = sg.n
        t_base = st_.n
        tzst_base = sst.n

        for e in range(32):
            def m1_g(g, e=e):
                g.wait_ge(sg.h, g_base + e * 16)
                if e >= 2:
                    g.wait_ge(st_.h, t_base + (e - 1) * 4)
                sg.inc(g.dma_start(
                    out=wsl[e % 2][:].rearrange("p (k e) -> p k e", k=8),
                    in_=self.winT.rearrange("(k p) e -> p k e", p=128)[:, :, e * 128:(e + 1) * 128]), 16)
            block.gpsimd(m1_g)

        def m1_t(t):
            t.wait_ge(sst.h, n_xnT)
            for e in range(32):
                t.wait_ge(sg.h, g_base + (e + 1) * 16)
                if e >= 2:
                    if e < 16 + 2:
                        pass
                    if (e - 2) < 16:
                        t.wait_ge(sem_evac.h, (e - 2) + 1)
                    else:
                        t.wait_ge(sem_tz.h, ((e - 2) - 16 + 1) * 4)
                w = wsl[e % 2]
                quad = [pb[(e % 2) * 4 + i] for i in range(4)]
                for tt in range(4):
                    for k in range(8):
                        i = t.matmul(quad[tt][:],
                                     w[:, k * 128:(k + 1) * 128],
                                     xnT[:, k * T + tt * 512:(k * T) + (tt + 1) * 512],
                                     start=(k == 0), stop=(k == 7))
                    st_.inc(i)
        block.tensor(m1_t)

        def v_init(v):
            v.memset(xzpad[:, 0:3], 0.0)
            v.drain()
        block.vector(v_init)

        for e in range(16):
            def s_x(s, e=e):
                s.wait_ge(st_.h, t_base + (e + 1) * 4)
                if e >= 1:
                    s.wait_ge(sem_conv.h, e)
                quad = [pb[(e % 2) * 4 + i] for i in range(4)]
                for tt in range(4):
                    s.activation(xzpad[:, 3 + tt * 512:3 + (tt + 1) * 512],
                                 quad[tt][:], AF.Identity, bias=self.inb_t[:, e:e + 1])
                s.drain()
                sem_evac.inc(s.nop())
                if e >= 1:
                    ee = e - 1
                    if ee >= 2:
                        s.wait_ge(sem_u.h, ee - 1)
                    s.drain()
                    sem_th.inc(s.activation(thb[ee % 2][:], cacc[ee % 2][:], AF.Tanh,
                                            scale=self.cst[:, C_HALF:C_HALF + 1]))
            block.scalar(s_x)

            def v_conv(v, e=e):
                v.wait_ge(sem_evac.h, e + 1)
                if e >= 2:
                    v.wait_ge(sem_th.h, e - 1)
                ca = cacc[e % 2]
                v.drain()
                v.tensor_scalar(ca[:], xzpad[:, 0:T], self.convw_t[:, 4 * e:4 * e + 1],
                                self.convb_t[:, e:e + 1], ALU.mult, ALU.add)
                for tau in range(1, 4):
                    v.drain()
                    i = v.scalar_tensor_tensor(ca[:], xzpad[:, tau:tau + T],
                                               self.convw_t[:, 4 * e + tau:4 * e + tau + 1],
                                               ca[:], ALU.mult, ALU.add)
                sem_conv.inc(i)
                if e >= 1:
                    ee = e - 1
                    v.wait_ge(sem_th.h, ee + 1)
                    sem_u.inc(v.scalar_tensor_tensor(ubf[:, ee * T:(ee + 1) * T], thb[ee % 2][:],
                                                     1.0, cacc[ee % 2][:], ALU.add, ALU.mult))
            block.vector(v_conv)

        def s_thf(s):
            s.wait_ge(sem_conv.h, 16)
            s.drain()
            sem_th.inc(s.activation(thb[15 % 2][:], cacc[15 % 2][:], AF.Tanh,
                                    scale=self.cst[:, C_HALF:C_HALF + 1]))
        block.scalar(s_thf)

        def v_uf(v):
            v.wait_ge(sem_th.h, 16)
            sem_u.inc(v.scalar_tensor_tensor(ubf[:, 15 * T:16 * T], thb[15 % 2][:],
                                             1.0, cacc[15 % 2][:], ALU.add, ALU.mult))
            v.drain()
            sv.inc(v.nop())
        block.vector(v_uf)

        for e in range(16, 32):
            def s_z(s, e=e):
                s.wait_ge(st_.h, t_base + (e + 1) * 4)
                if e >= 18:
                    s.wait_ge(sst.h, tzst_base + (e - 16 - 1) * 16)
                quad = [pb[(e % 2) * 4 + i] for i in range(4)]
                for tt in range(4):
                    sem_tz.inc(s.activation(tzc[e % 2][:, tt * 512:(tt + 1) * 512], quad[tt][:],
                                            AF.Tanh, scale=self.cst[:, C_HALF:C_HALF + 1],
                                            bias=self.hbz[:, e - 16:e - 15]))
            block.scalar(s_z)

            def m1_tzstore(sync, e=e):
                sync.wait_ge(sst.h, tzst_base + (e - 16) * 16)
                sync.wait_ge(sem_tz.h, (e - 16 + 1) * 4)
                sst.inc(sync.dma_start(
                    out=self.stg_tz[(e - 16) * 128:(e - 15) * 128, :],
                    in_=tzc[e % 2][:]), 16)
            block.sync(m1_tzstore)
            if self.wq_on:
                self._wq_step()

        self.barrier()
        ph.close()
        self.nc.leave_named_scope("m1", sc_m[0], False)
        sc_m = self.nc.enter_named_scope("m2", False)

        ph2 = ExitStack()
        sbl2 = lambda nm, shape, dt: ph2.enter_context(nc.sbuf_tensor(self.name(nm), shape, dt))
        bsl = [sbl2("bsl", [128, 16 * 128], FP16) for _ in range(2)]
        bcc = [sbl2("bcc", [128, T], FP16) for _ in range(2)]

        for wsrc, dsts in [(self.bwT, self.stg_b), (self.cwT, self.stg_c)]:
            g2 = sg.n
            t2 = st_.n
            s2 = ss.n
            d2 = sst.n

            def m2_t(t, g2=g2, s2=s2, t2=t2):
                for e in range(16):
                    t.wait_ge(sg.h, g2 + (e + 1) * 16)
                    t.wait_ge(ss.h, s2 + max(0, e - 1) * 4)
                    w = bsl[e % 2]
                    quad = [pb[(e % 2) * 4 + i] for i in range(4)]
                    for tt in range(4):
                        for k in range(16):
                            i = t.matmul(quad[tt][:], w[:, k * 128:(k + 1) * 128],
                                         ubf[:, k * T + tt * 512: k * T + (tt + 1) * 512],
                                         start=(k == 0), stop=(k == 15))
                        st_.inc(i)
            block.tensor(m2_t)

            for e in range(16):
                def m2_g(g, wsrc=wsrc, t2=t2, g2=g2, e=e):
                    g.wait_ge(sg.h, g2 + e * 16)
                    g.wait_ge(st_.h, t2 + max(0, e - 1) * 4)
                    sg.inc(g.dma_start(
                        out=bsl[e % 2][:].rearrange("p (k e) -> p k e", k=16),
                        in_=wsrc.rearrange("(k p) e -> p k e", p=128)[:, :, e * 128:(e + 1) * 128]), 16)
                block.gpsimd(m2_g)

                def m2_s_e(s, t2=t2, d2=d2, e=e):
                    s.wait_ge(st_.h, t2 + (e + 1) * 4)
                    s.wait_ge(sst.h, d2 + max(0, e - 1) * 16)
                    quad = [pb[(e % 2) * 4 + i] for i in range(4)]
                    for tt in range(4):
                        ss.inc(s.activation(bcc[e % 2][:, tt * 512:(tt + 1) * 512],
                                            quad[tt][:], AF.Tanh,
                                            scale=self.cst[:, C_HALF:C_HALF + 1]))
                block.scalar(m2_s_e)

                def m2_st(sync, s2=s2, dsts=dsts, d2=d2, e=e):
                    sync.wait_ge(sst.h, d2 + e * 16)
                    sync.wait_ge(ss.h, s2 + (e + 1) * 4)
                    sst.inc(sync.dma_start(out=dsts[e * 128:(e + 1) * 128, :],
                                           in_=bcc[e % 2][:]), 16)
                block.sync(m2_st)
                if self.wq_on:
                    self._wq_step()
        if self.wq_on:
            self._wq_drain()
            self.barrier(extra=self.wq_extras)
            ph2.close()
            self.wq_ph.close()
        else:
            self.barrier()
            ph2.close()
        self.nc.leave_named_scope("m2", sc_m[0], False)
        sc_m = self.nc.enter_named_scope("m3", False)

        ph3 = ExitStack()
        sbl3 = lambda nm, shape, dt: ph3.enter_context(nc.sbuf_tensor(self.name(nm), shape, dt))
        dsl = [sbl3("dsl", [128, 16 * 128], FP16) for _ in range(2)]
        ebuf = sbl3("ebuf", [128, T], F32)
        spbuf = sbl3("spbuf", [128, T], F32)
        dbuf = [sbl3("dbuf", [128, T], F32) for _ in range(2)]
        ombuf = sbl3("ombuf", [128, T], F32)
        t1buf = sbl3("t1buf", [128, T], F32)
        incb = sbl3("incb", [128, T], F32)
        stsb = sbl3("stsb", [128, T], F32)
        t2buf = sbl3("t2buf", [128, T], F32)
        ypb = sbl3("ypb", [128, T], F32)
        y2c = [sbl3("y2c", [128, T], FP16) for _ in range(2)]
        bld = [sbl3("bld", [128, T], FP16) for _ in range(2)]
        cld = [sbl3("cld", [128, T], FP16) for _ in range(2)]
        tzld = [sbl3("tzld", [128, T], FP16) for _ in range(2)]

        g3 = sg.n
        t3 = st_.n
        s3 = ss.n
        v3 = sv.n
        d3 = sst.n
        ld3 = ld.n

        for e in range(16):
            def m3_g(g, e=e):
                g.wait_ge(sg.h, g3 + e * 16)
                if e >= 2:
                    g.wait_ge(st_.h, t3 + (e - 1) * 4)
                sg.inc(g.dma_start(
                    out=dsl[e % 2][:].rearrange("p (k e) -> p k e", k=16),
                    in_=self.dtwT.rearrange("(k p) e -> p k e", p=128)[:, :, e * 128:(e + 1) * 128]), 16)
            block.gpsimd(m3_g)

        def m3_ld_body(sync, e):
            sync.wait_ge(ld.h, ld3 + e * 48)
            if e >= 2:
                sync.wait_ge(sv.h, v3 + (e - 1) * 7)
            ld.inc(sync.dma_start(out=bld[e % 2][:], in_=self.stg_b[e * 128:(e + 1) * 128, :]), 16)
            ld.inc(sync.dma_start(out=cld[e % 2][:], in_=self.stg_c[e * 128:(e + 1) * 128, :]), 16)
            ld.inc(sync.dma_start(out=tzld[e % 2][:], in_=self.stg_tz[e * 128:(e + 1) * 128, :]), 16)

        def m3_st_body(sync, e):
            sync.wait_ge(sst.h, d3 + e * 16)
            sync.wait_ge(sv.h, v3 + (e + 1) * 7)
            sst.inc(sync.dma_start(out=self.stg_y2[e * 128:(e + 1) * 128, :],
                                   in_=y2c[e % 2][:]), 16)

        for e in range(16):
            block.sync(lambda sync, e=e: m3_ld_body(sync, e))
            if e >= 2:
                block.sync(lambda sync, e=e: m3_st_body(sync, e - 2))
        block.sync(lambda sync: m3_st_body(sync, 14))
        block.sync(lambda sync: m3_st_body(sync, 15))

        def m3_t(t):
            for e in range(16):
                t.wait_ge(sg.h, g3 + (e + 1) * 16)
                if e >= 2:
                    t.wait_ge(ss.h, s3 + (e - 1) * 6)
                w = dsl[e % 2]
                quad = [pb[(e % 2) * 4 + i] for i in range(4)]
                for tt in range(4):
                    for k in range(16):
                        i = t.matmul(quad[tt][:], w[:, k * 128:(k + 1) * 128],
                                     ubf[:, k * T + tt * 512: k * T + (tt + 1) * 512],
                                     start=(k == 0), stop=(k == 15))
                    st_.inc(i)
        block.tensor(m3_t)

        def m3_s(s):
            for e in range(16):
                s.wait_ge(st_.h, t3 + (e + 1) * 4)
                if e >= 2:
                    s.wait_ge(sv.h, v3 + (e - 1) * 7)
                quad = [pb[(e % 2) * 4 + i] for i in range(4)]
                for tt in range(4):
                    i = s.activation(ebuf[:, tt * 512:(tt + 1) * 512], quad[tt][:], AF.Exp,
                                     scale=self.cst[:, C_HALF:C_HALF + 1],
                                     bias=self.dtb[:, e:e + 1]) if False else                         s.activation(ebuf[:, tt * 512:(tt + 1) * 512], quad[tt][:], AF.Exp,
                                     scale=self.cst[:, C_HALF:C_HALF + 1],
                                     bias=self.dtb_t[:, e:e + 1])
                    if tt == 3:
                        ss.inc(i)
                s.drain()
                s.activation(spbuf[:], ebuf[:], AF.Ln, bias=self.cst[:, C_ONE:C_ONE + 1])
                s.drain()
                i = s.activation(dbuf[e % 2][:], spbuf[:], AF.Exp,
                                 scale=self.negA[:, e:e + 1], bias=self.nb2[:, e:e + 1])
                ss.inc(i)
                ss.inc(s.nop()); ss.inc(s.nop()); ss.inc(s.nop()); ss.inc(s.nop())
        block.scalar(m3_s)

        def m3_v(v):
            for e in range(16):
                v.wait_ge(ss.h, s3 + (e + 1) * 6)
                v.wait_ge(ld.h, ld3 + (e + 1) * 48)
                if e >= 2:
                    v.wait_ge(sst.h, d3 + (e - 1) * 16)
                v.tensor_scalar(ombuf[:], dbuf[e % 2][:], -1.0, 1.0, ALU.mult, ALU.add)
                v.tensor_tensor(t1buf[:], bld[e % 2][:], ubf[:, e * T:(e + 1) * T], ALU.mult)
                v.drain()
                v.tensor_tensor(incb[:], ombuf[:], t1buf[:], ALU.mult)
                v.drain()
                v.tensor_tensor_scan(stsb[:], dbuf[e % 2][:], incb[:], 0.0, ALU.mult, ALU.add)
                v.drain()
                v.tensor_tensor(t2buf[:], cld[e % 2][:], stsb[:], ALU.mult)
                v.drain()
                v.scalar_tensor_tensor(ypb[:], ubf[:, e * T:(e + 1) * T],
                                       self.dpar_t[:, e:e + 1], t2buf[:], ALU.mult, ALU.add)
                v.drain()
                i = v.scalar_tensor_tensor(y2c[e % 2][:], tzld[e % 2][:], 1.0,
                                           ypb[:], ALU.add, ALU.mult)
                for _ in range(7):
                    pass
                sv.inc(i)
                sv.inc(v.nop()); sv.inc(v.nop()); sv.inc(v.nop())
                sv.inc(v.nop()); sv.inc(v.nop()); sv.inc(v.nop())
        block.vector(m3_v)



        self.barrier()
        ph3.close()
        ph_u.close()

        self.nc.leave_named_scope("m3", sc_m[0], False)
        sc_m = self.nc.enter_named_scope("m4", False)
        ph4 = ExitStack()
        sbl4 = lambda nm, shape, dt: ph4.enter_context(nc.sbuf_tensor(self.name(nm), shape, dt))
        owb = sbl4("owb", [128, 16 * D], FP16)
        xsb4 = sbl4("mx4", [128, TC * D], F32)
        y2g = [sbl4("y2g", [128, 16 * 128], FP16) for _ in range(2)]
        obuf = [sbl4("obuf", [128, D], F32) for _ in range(2)]
        r1b = [sbl4("r1b", [128, D], F32) for _ in range(2)]

        g4 = sg.n

        def m4_g(g):
            sg.inc(g.dma_start(
                out=owb[:].rearrange("p (k d) -> p k d", k=16),
                in_=self.outwT.rearrange("(k p) d -> p k d", p=128)), 16)
        block.gpsimd(m4_g)
        self._load_x(xsb4)

        ld4 = ld.n
        t4 = st_.n
        s4 = ss.n
        v4 = sv.n
        d4 = sst.n

        for c in range(TC):
            def m4_ld(sync, c=c):
                sync.wait_ge(ld.h, ld4 + c * 16)
                if c >= 2:
                    sync.wait_ge(st_.h, t4 + (c - 1) * 2)
                ld.inc(sync.dma_start(
                    out=y2g[c % 2][:].rearrange("p (e q) -> p e q", e=16),
                    in_=self.stg_y2.rearrange("(e p) t -> p e t", p=128)[:, :, c * 128:(c + 1) * 128]), 16)
            block.sync(m4_ld)

        def m4_t(t):
            t.wait_ge(sg.h, g4 + 16)
            for c in range(TC):
                t.wait_ge(ld.h, ld4 + (c + 1) * 16)
                if c >= 2:
                    t.wait_ge(ss.h, s4 + (c - 1) * 2)
                pa, pbk = pb[(c % 2) * 2], pb[(c % 2) * 2 + 1]
                for e in range(16):
                    i1 = t.matmul(pa[:], y2g[c % 2][:, e * 128:(e + 1) * 128],
                                  owb[:, e * D:e * D + 512], start=(e == 0), stop=(e == 15))
                st_.inc(i1)
                for e in range(16):
                    i2 = t.matmul(pbk[:], y2g[c % 2][:, e * 128:(e + 1) * 128],
                                  owb[:, e * D + 512:(e + 1) * D], start=(e == 0), stop=(e == 15))
                st_.inc(i2)
        block.tensor(m4_t)

        def m4_s(s):
            for c in range(TC):
                s.wait_ge(st_.h, t4 + (c + 1) * 2)
                if c >= 2:
                    s.wait_ge(sv.h, v4 + (c - 1) * 2)
                pa, pbk = pb[(c % 2) * 2], pb[(c % 2) * 2 + 1]
                s.activation(obuf[c % 2][:, 0:512], pa[:], AF.Copy,
                             scale=self.cst[:, C_QUART:C_QUART + 1])
                i = s.activation(obuf[c % 2][:, 512:1024], pbk[:], AF.Copy,
                                 scale=self.cst[:, C_QUART:C_QUART + 1])
                ss.inc(i)
                ss.inc(s.nop())
        block.scalar(m4_s)

        def m4_v(v):
            for c in range(TC):
                v.wait_ge(ss.h, s4 + (c + 1) * 2)
                v.tensor_tensor(r1b[c % 2][:], obuf[c % 2][:], self.outb_t[:], ALU.add)
                v.drain()
                i = v.tensor_tensor(xsb4[:, c * D:(c + 1) * D], xsb4[:, c * D:(c + 1) * D],
                                    r1b[c % 2][:], ALU.add)
                sv.inc(i)
                sv.inc(v.nop())
        block.vector(m4_v)

        self._store_x(xsb4, sv.n)
        self.barrier()
        ph4.close()
        self.nc.leave_named_scope("m4", sc_m[0], False)

    class _Quant:

        def __init__(self, B, ph, F, n_par=2):
            nc = B.nc
            self.B = B
            self.F = F
            self.n_par = n_par
            nb = F // 16
            sbl = lambda nm, shape, dt: ph.enter_context(
                nc.sbuf_tensor(B.name(nm), shape, dt))
            self.xn = [sbl("q_xn", [128, F], FP16) for _ in range(n_par)]
            self.an = [sbl("q_an", [128, F], FP16) for _ in range(n_par)]
            self.sg = [sbl("q_sg", [128, F], FP16) for _ in range(n_par)]
            self.sm = [sbl("q_sm", [128, 3 * nb], F32) for _ in range(n_par)]
            self.sc16 = [sbl("q_sc16", [128, nb], FP16) for _ in range(n_par)]
            self.sv_q = Sem(nc, ph, B.name("qv"))
            self.ss_q = Sem(nc, ph, B.name("qs"))
            self.ss_a = Sem(nc, ph, B.name("qa"))
            self.sv_qb = Sem(nc, ph, B.name("qb"))
            self.nA = 0
            self.nS = 0
            self.n = 0

        def emit_A(self, v, src_ap, W=None):
            F = W if W is not None else self.F
            nb, nbm = F // 16, self.F // 16
            i = self.nA
            self.nA += 1
            p = i % self.n_par
            sm = self.sm[p]
            if i >= self.n_par:
                v.wait_ge(self.ss_q.h, i - self.n_par + 1)
            v.drain()
            v.tensor_reduce(sm[:, 0:nb], src_ap.rearrange("p (b w) -> p b w", w=16),
                            AX.X, ALU.max, apply_absolute_value=True)
            v.drain()
            v.tensor_scalar(sm[:, 2 * nbm:2 * nbm + nb], sm[:, 0:nb], 1.0 / 6.0, 1e-12,
                            ALU.mult, ALU.max)
            v.drain()
            v.reciprocal(sm[:, nbm:nbm + nb], sm[:, 2 * nbm:2 * nbm + nb])
            v.tensor_scalar(self.sc16[p][:, 0:nb], sm[:, 2 * nbm:2 * nbm + nb], 1.0, None,
                            ALU.mult)
            v.drain()
            self.sv_q.inc(v.tensor_tensor(
                self.xn[p][:, 0:F], src_ap,
                sm[:, nbm:nbm + nb].rearrange("p b -> p b ()").broadcast_to([128, nb, 16]),
                ALU.mult))

        def emit_S(self, s, W=None):
            F = W if W is not None else self.F
            i = self.nS
            self.nS += 1
            p = i % self.n_par
            s.wait_ge(self.sv_q.h, i + 1)
            if i >= self.n_par:
                s.wait_ge(self.sv_qb.h, i - self.n_par + 1)
            s.drain()
            self.ss_a.inc(s.activation(self.an[p][:, 0:F], self.xn[p][:, 0:F], AF.Abs))
            self.ss_q.inc(s.activation(self.sg[p][:, 0:F], self.xn[p][:, 0:F], AF.Sign))

        def emit_B(self, v, out_ap, W=None):
            F = W if W is not None else self.F
            nb = F // 16
            p = self.n % self.n_par
            an, sg = self.an[p][:, 0:F], self.sg[p][:, 0:F]
            tA, tB, tC, tD = (t[:, 0:F] for t in self.B.qtb)
            v.wait_ge(self.ss_a.h, self.n + 1)
            v.drain()
            v.tensor_scalar(tA, an, 0.25, 0.5, ALU.is_gt, ALU.mult)
            v.tensor_scalar(tB, an, 0.75, 0.5, ALU.is_gt, ALU.mult)
            v.tensor_scalar(tC, an, 1.25, 0.5, ALU.is_gt, ALU.mult)
            v.tensor_scalar(tD, an, 1.75, 0.5, ALU.is_gt, ALU.mult)
            v.drain()
            v.tensor_tensor(tA, tA, tB, ALU.add)
            v.tensor_tensor(tC, tC, tD, ALU.add)
            v.drain()
            v.tensor_scalar(tB, an, 2.5, 1.0, ALU.is_gt, ALU.mult)
            v.tensor_scalar(tD, an, 3.5, 1.0, ALU.is_gt, ALU.mult)
            v.tensor_tensor(tA, tA, tC, ALU.add)
            v.drain()
            v.tensor_scalar(tC, an, 5.0, 2.0, ALU.is_gt, ALU.mult)
            v.tensor_tensor(tB, tB, tD, ALU.add)
            v.drain()
            v.tensor_tensor(tA, tA, tB, ALU.add)
            v.drain()
            v.tensor_tensor(tA, tA, tC, ALU.add)
            v.wait_ge(self.ss_q.h, self.n + 1)
            v.drain()
            v.tensor_tensor(tB, tA, sg, ALU.mult)
            v.drain()
            i = v.tensor_tensor(
                out_ap, tB,
                self.sc16[p][:, 0:nb].rearrange("p b -> p b ()").broadcast_to([128, nb, 16]),
                ALU.mult)
            self.sv_qb.inc(i)
            self.n += 1
            return i

    def _wq_init(self):
        nc = self.nc
        ph = ExitStack()
        sbl = lambda nm, shape, dt: ph.enter_context(nc.sbuf_tensor(self.name(nm), shape, dt))
        self.wq_ph = ph
        self.wq_cin = [sbl("wq_in", [128, 1408], F32) for _ in range(2)]
        self.wq_cout = [sbl("wq_out", [128, 1408], FP16) for _ in range(2)]
        self.qtb = [sbl("qtb", [128, 1408], FP16) for _ in range(4)]
        self.wq_q = self._Quant(self, ph, 1408)
        self.wq_ld = Sem(nc, ph, self.name("wqld"))
        self.wq_st = Sem(nc, ph, self.name("wqst"))
        pieces = []
        for c in range(IC):
            pieces.append((self.wg_in, self.wgq_d, c * 128, 0, D))
        for c in range(IC):
            pieces.append((self.wu_in, self.wuq_d, c * 128, 0, D))
        for c in range(8):
            pieces.append((self.wd_in, self.wdq_d, c * 128, 0, 1408))
            pieces.append((self.wd_in, self.wdq_d, c * 128, 1408, 1408))
        self.wq_pieces = pieces
        self.wq_j = 0
        q = self.wq_q
        self.wq_extras = [q.sv_q, q.ss_q, q.ss_a, q.sv_qb, self.wq_ld, self.wq_st]

    def _wq_ldb(self, g, j):
        srcd, dstd, r0, c0, W = self.wq_pieces[j]
        g.wait_ge(self.wq_ld.h, j * 16)
        if j >= 2:
            g.wait_ge(self.wq_q.sv_q.h, j - 1)
        self.wq_ld.inc(g.dma_start(out=self.wq_cin[j % 2][:, 0:W],
                                   in_=srcd[r0:r0 + 128, c0:c0 + W]), 16)

    def _wq_stb(self, sync, j):
        srcd, dstd, r0, c0, W = self.wq_pieces[j]
        sync.wait_ge(self.wq_st.h, j * 16)
        sync.wait_ge(self.wq_q.sv_qb.h, j + 1)
        self.wq_st.inc(sync.dma_start(out=dstd[r0:r0 + 128, c0:c0 + W],
                                      in_=self.wq_cout[j % 2][:, 0:W]), 16)

    def _wq_step(self):
        NP = len(self.wq_pieces)
        j = self.wq_j
        if j > NP:
            return
        block, q = self.block, self.wq_q
        if j < NP:
            srcd, dstd, r0, c0, W = self.wq_pieces[j]
            if j == 0:
                block.sync(lambda sync: self._wq_ldb(sync, 0))
                block.sync(lambda sync: self._wq_ldb(sync, 1))
            elif j + 1 < NP:
                block.sync(lambda sync, k=j + 1: self._wq_ldb(sync, k))

            def va(v, j=j, W=W):
                v.wait_ge(self.wq_ld.h, (j + 1) * 16)
                q.emit_A(v, self.wq_cin[j % 2][:, 0:W], W)
            block.vector(va)

            def sa(s, W=W):
                q.emit_S(s, W)
            block.scalar(sa)
        if j >= 1:
            srcd, dstd, r0, c0, W = self.wq_pieces[j - 1]

            def vb(v, j=j, W=W):
                if j - 1 >= 2:
                    v.wait_ge(self.wq_st.h, (j - 2) * 16)
                q.emit_B(v, self.wq_cout[(j - 1) % 2][:, 0:W]
                         .rearrange("p (b w) -> p b w", w=16), W)
            block.vector(vb)
            if j - 2 >= 0:
                block.sync(lambda sync, k=j - 2: self._wq_stb(sync, k))
        self.wq_j += 1

    def _wq_drain(self):
        NP = len(self.wq_pieces)
        while self.wq_j <= NP:
            self._wq_step()
        self.block.sync(lambda sync: self._wq_stb(sync, NP - 1))

    def _emit_ffn4(self):
        nc, block = self.nc, self.block
        ld, sv, ss, st_, sst, sg = (self.sems[k] for k in ["ld", "v", "s", "t", "st", "g"])
        pb = self.pb
        stat, cst = self.stat, self.cst

        scope_f41 = nc.enter_named_scope("f41", False)
        ph1 = ExitStack()
        sbl1 = lambda nm, shape, dt: ph1.enter_context(nc.sbuf_tensor(self.name(nm), shape, dt))
        wgT = sbl1("wgT", [128, 8 * I1], FP16)
        wuT = sbl1("wuT", [128, 8 * I1], FP16)
        xcb = [sbl1("f4x", [128, D], F32) for _ in range(2)]
        xnf = sbl1("f4xn", [128, D], F32)
        xq4c = [sbl1("f4xq", [128, D], FP16) for _ in range(2)]
        xq4Tc = [sbl1("f4xqT", [128, 8 * 128], FP16) for _ in range(2)]
        sgb = [sbl1("f4sg", [128, 512], FP16) for _ in range(2)]
        upe = [sbl1("f4up", [128, 512], FP16) for _ in range(2)]
        hb = [sbl1("f4h", [128, I1], F32) for _ in range(2)]
        hqp = [sbl1("f4hq", [128, 1408], FP16) for _ in range(2)]
        self.qtb = [sbl1("qtb", [128, 1408], FP16) for _ in range(4)]
        qx = self._Quant(self, ph1, D)
        qh = self._Quant(self, ph1, 1408)
        se_x = Sem(nc, ph1, self.name("f4ex"))
        s_hst = Sem(nc, ph1, self.name("f4hs"))
        sgh = Sem(nc, ph1, self.name("f4gh"))
        NT = [512, 512, 512, 512, 512, 256]
        NPH = 2 * TC

        ldw0 = ld.n

        def f41_wsync(sync):
            sync.wait_ge(ld.h, ldw0)
            for k in range(8):
                ld.inc(sync.dma_start_transpose(
                    wgT[:, k * I1:(k + 1) * I1], self.wgq_d[:, k * 128:(k + 1) * 128]), 16)
        def f41_wsync2(sync):
            for k in range(8):
                ld.inc(sync.dma_start_transpose(
                    wuT[:, k * I1:(k + 1) * I1], self.wuq_d[:, k * 128:(k + 1) * 128]), 16)
        block.sync(f41_wsync)
        block.sync(f41_wsync2)
        n_wld = ld.n
        n_wsg = sg.n

        lp1 = ld.n
        v0 = sv.n
        for c in range(TC):
            def ld1(sync, c=c):
                sync.wait_ge(ld.h, lp1 + c * 16)
                if c >= 2:
                    sync.wait_ge(sv.h, v0 + (c - 1))
                ld.inc(sync.dma_start(
                    out=xcb[c % 2][:],
                    in_=self.xd.rearrange("(c p) d -> p c d", p=128)[:, c, :]), 16)
            block.sync(ld1)

            def v1(v, c=c):
                v.wait_ge(ld.h, lp1 + (c + 1) * 16)
                v.drain()
                v.tensor_reduce(stat[:, c:c + 1], xcb[c % 2][:], AX.X, ALU.add)
                v.scalar_tensor_tensor(xnf[:], xcb[c % 2][:], 1.0, xcb[c % 2][:],
                                       ALU.mult, ALU.mult,
                                       accum_out=stat[:, 16 + c:17 + c])
                v.drain()
                sv.inc(v.nop())
            block.vector(v1)
        v_p1 = sv.n

        def v_stats(v):
            v.drain()
            v.tensor_scalar(stat[:, 32:48], stat[:, 0:16], 1.0 / D, None, ALU.mult)
            v.drain()
            v.tensor_tensor(stat[:, 64:80], stat[:, 32:48], stat[:, 32:48], ALU.mult)
            v.drain()
            sv.inc(v.scalar_tensor_tensor(stat[:, 48:64], stat[:, 16:32], 1.0 / D,
                                          stat[:, 64:80], ALU.mult, ALU.subtract))
        block.vector(v_stats)
        n_v1 = sv.n

        def sc1(s):
            s.wait_ge(sv.h, n_v1)
            ss.inc(s.activation(stat[:, 80:96], stat[:, 48:64], AF.Sqrt,
                                bias=cst[:, C_EPS:C_EPS + 1]))
        block.scalar(sc1)
        n_s1p = ss.n

        def v_rstd(v):
            v.wait_ge(ss.h, n_s1p)
            sv.inc(v.reciprocal(stat[:, 48:64], stat[:, 80:96]))
            v.wait_ge(sv.h, sv.n)
        block.vector(v_rstd)
        v1b = sv.n
        s1b = ss.n
        lp2 = sg.n

        marks_trans = {}
        marks_gu = {}

        def f4_t(t):
            t.wait_ge(ld.h, n_wld)

            def trans(c):
                t.wait_ge(qx.sv_qb.h, c + 1)
                if c >= 2:
                    t.wait_ge(se_x.h, c - 1)
                b0 = pb[4 + 2 * (c % 2)]
                b1 = pb[5 + 2 * (c % 2)]
                for k in range(8):
                    bank = b0 if k < 4 else b1
                    i = t.matmul(bank[:, (k % 4) * 128:(k % 4) * 128 + 128],
                                 xq4c[c % 2][:, k * 128:(k + 1) * 128],
                                 self.ident[:], start=True, stop=True)
                    if k % 4 == 3:
                        st_.inc(i)
                marks_trans[c] = st_.n

            def gu(c):
                t.wait_ge(se_x.h, c + 1)
                off = 0
                for n, w in enumerate(NT):
                    pg, pu = pb[(n % 2) * 2], pb[(n % 2) * 2 + 1]
                    cnt = c * 6 + n - 1
                    if cnt >= 0:
                        t.wait_ge(ss.h, s1b + 2 * cnt)
                    for k in range(8):
                        ig = t.matmul(pg[:, 0:w], xq4Tc[c % 2][:, k * 128:(k + 1) * 128],
                                      wgT[:, k * I1 + off:k * I1 + off + w],
                                      start=(k == 0), stop=(k == 7))
                    st_.inc(ig)
                    for k in range(8):
                        iu = t.matmul(pu[:, 0:w], xq4Tc[c % 2][:, k * 128:(k + 1) * 128],
                                      wuT[:, k * I1 + off:k * I1 + off + w],
                                      start=(k == 0), stop=(k == 7))
                    st_.inc(iu)
                    marks_gu[(c, n)] = st_.n
                    off += w

            trans(0)
            trans(1)
            for c in range(TC):
                gu(c)
                if c + 2 < TC:
                    trans(c + 2)
        block.tensor(f4_t)

        for c in range(TC + 4):
            if c < TC:
                def s_qxS(s):
                    qx.emit_S(s)
                block.scalar(s_qxS)
            if 1 <= c <= TC:
                def s_evacx(s, cc=c - 1):
                    s.wait_ge(st_.h, marks_trans[cc])
                    b0 = pb[4 + 2 * (cc % 2)]
                    b1 = pb[5 + 2 * (cc % 2)]
                    s.activation(xq4Tc[cc % 2][:, 0:512], b0[:], AF.Copy)
                    se_x.inc(s.activation(xq4Tc[cc % 2][:, 512:1024], b1[:], AF.Copy))
                block.scalar(s_evacx)
            if 2 <= c < TC + 2:
                def s_silu(s, cc=c - 2):
                    for n, w in enumerate(NT):
                        s.wait_ge(st_.h, marks_gu[(cc, n)])
                        pg, pu = pb[(n % 2) * 2], pb[(n % 2) * 2 + 1]
                        prev = cc * 6 + n - 1
                        if prev >= 0:
                            s.wait_ge(sgh.h, prev)
                        ss.inc(s.activation(sgb[n % 2][:, 0:w], pg[:, 0:w], AF.Silu))
                        ss.inc(s.activation(upe[n % 2][:, 0:w], pu[:, 0:w], AF.Copy))
                block.scalar(s_silu)
            if 3 <= c < TC + 3:
                def s_qhS(s, cc=c - 3):
                    for q in range(2):
                        qh.emit_S(s, 1408)
                block.scalar(s_qhS)

        def emit_qh_A(cc, q):
            def vA(v):
                if q == 0:
                    v.wait_ge(sgh.h, (cc + 1) * 6)
                qh.emit_A(v, hb[cc % 2][:, q * 1408:(q + 1) * 1408], 1408)
            block.vector(vA)

        def emit_qh_B(j):
            def vB(v):
                if j >= 2:
                    v.wait_ge(s_hst.h, (j - 1) * 16)
                qh.emit_B(v, hqp[j % 2][:]
                          .rearrange("p (b w) -> p b w", w=16), 1408)
            block.vector(vB)

        for c in range(TC + 4):
            if c < TC:
                def v_norm_qxA(v, c=c):
                    v.wait_ge(sg.h, lp2 + (c + 1) * 16)
                    v.drain()
                    v.tensor_scalar(xnf[:], xcb[c % 2][:],
                                    stat[:, 32 + c:33 + c], stat[:, 48 + c:49 + c],
                                    ALU.subtract, ALU.mult)
                    qx.emit_A(v, xnf[:])
                block.vector(v_norm_qxA)
            if 1 <= c <= TC:
                def v_qxB(v, cc=c - 1):
                    if cc >= 2:
                        v.wait_ge(st_.h, marks_trans[cc - 2])
                    qx.emit_B(v, xq4c[cc % 2][:].rearrange("p (b w) -> p b w", w=16))
                block.vector(v_qxB)
            if 3 <= c < TC + 3:
                cc = c - 3
                for q in range(2):
                    j = cc * 2 + q
                    emit_qh_A(cc, q)
                    if j >= 1:
                        emit_qh_B(j - 1)
        emit_qh_B(NPH - 1)

        def ld2(g, c):
            g.wait_ge(sg.h, lp2 + c * 16)
            if c < 2:
                g.wait_ge(sv.h, v0 + 15 + c)
            else:
                g.wait_ge(qx.sv_q.h, c - 1)
            sg.inc(g.dma_start(
                out=xcb[c % 2][:],
                in_=self.xd.rearrange("(c p) d -> p c d", p=128)[:, c, :]), 16)

        def sth(sync, j):
            cc, q = j // 2, j % 2
            sync.wait_ge(s_hst.h, j * 16)
            sync.wait_ge(qh.sv_qb.h, j + 1)
            s_hst.inc(sync.dma_start(
                out=self.hq_d[cc * 128:(cc + 1) * 128, q * 1408:(q + 1) * 1408],
                in_=hqp[j % 2][:]), 16)

        def g_Vh(g, cc):
            g.drain()
            off = 0
            for n, w in enumerate(NT):
                g.wait_ge(ss.h, s1b + 2 * (cc * 6 + n + 1))
                if n == 0 and cc >= 2:
                    g.wait_ge(qh.sv_q.h, 2 * (cc - 1))
                sgh.inc(g.tensor_tensor(hb[cc % 2][:, off:off + w],
                                        sgb[n % 2][:, 0:w], upe[n % 2][:, 0:w], ALU.mult))
                off += w

        for c in range(TC + 3):
            if c < TC:
                block.gpsimd(lambda g, c=c: ld2(g, c))
            if 3 <= c:
                block.gpsimd(lambda g, cc=c - 3: g_Vh(g, cc))
        for j in range(NPH):
            block.sync(lambda sync, j=j: sth(sync, j))

        self.barrier(extra=[qx.sv_q, qx.ss_q, qx.ss_a, qx.sv_qb,
                            qh.sv_q, qh.ss_q, qh.ss_a, qh.sv_qb, se_x, s_hst, sgh])
        ph1.close()
        nc.leave_named_scope("f41", scope_f41[0] if isinstance(scope_f41, tuple) else scope_f41, False)

        scope_f42 = nc.enter_named_scope("f42", False)
        ph2 = ExitStack()
        sbl2 = lambda nm, shape, dt: ph2.enter_context(nc.sbuf_tensor(self.name(nm), shape, dt))
        hqT = sbl2("hqT", [128, IC * T // 1], BF16) if False else             sbl2("hqT", [128, 22 * T], FP16)
        wdT = sbl2("wdT", [128, 22 * D], FP16)
        obuf = [sbl2("f4o", [128, D], F32) for _ in range(2)]
        xc = [sbl2("f4xc", [128, D], F32) for _ in range(2)]

        ld2 = ld.n

        def f42_w(sync):
            sync.wait_ge(ld.h, ld2)
            for i in range(22):
                ld.inc(sync.dma_start_transpose(
                    hqT[:, i * T:(i + 1) * T], self.hq_d[:, i * 128:(i + 1) * 128]), 16)
            for i in range(22):
                ld.inc(sync.dma_start_transpose(
                    wdT[:, i * D:(i + 1) * D], self.wdq_d[:, i * 128:(i + 1) * 128]), 16)
        block.sync(f42_w)
        n_w2 = ld.n

        t2 = st_.n
        s2 = ss.n
        v2 = sv.n
        ldx2 = ld.n
        stx2 = sst.n

        def f42_xld(sync, c):
            sync.wait_ge(ld.h, ldx2 + c * 16)
            if c >= 2:
                sync.wait_ge(sst.h, stx2 + (c - 1) * 16)
            ld.inc(sync.dma_start(
                out=xc[c % 2][:],
                in_=self.xd.rearrange("(c p) d -> p c d", p=128)[:, c, :]), 16)

        def f42_xst(sync, c):
            sync.wait_ge(sst.h, stx2 + c * 16)
            sync.wait_ge(sv.h, v2 + c + 1)
            sst.inc(sync.dma_start(
                out=self.xd.rearrange("(c p) d -> p c d", p=128)[:, c, :],
                in_=xc[c % 2][:]), 16)

        for c in range(TC):
            if c >= 2:
                block.sync(lambda sync, c=c: f42_xst(sync, c - 2))
            block.sync(lambda sync, c=c: f42_xld(sync, c))
        block.sync(lambda sync: f42_xst(sync, TC - 2))
        block.sync(lambda sync: f42_xst(sync, TC - 1))

        def f42_t(t):
            t.wait_ge(ld.h, n_w2)
            for c in range(TC):
                if c >= 2:
                    t.wait_ge(ss.h, s2 + (c - 1) * 2)
                pa, pbk = pb[(c % 2) * 2], pb[(c % 2) * 2 + 1]
                for i in range(22):
                    i1 = t.matmul(pa[:], hqT[:, i * T + c * 128:i * T + (c + 1) * 128],
                                  wdT[:, i * D:i * D + 512], start=(i == 0), stop=(i == 21))
                st_.inc(i1)
                for i in range(22):
                    i2 = t.matmul(pbk[:], hqT[:, i * T + c * 128:i * T + (c + 1) * 128],
                                  wdT[:, i * D + 512:(i + 1) * D], start=(i == 0), stop=(i == 21))
                st_.inc(i2)
        block.tensor(f42_t)

        def f42_s(s):
            for c in range(TC):
                s.wait_ge(st_.h, t2 + (c + 1) * 2)
                if c >= 2:
                    s.wait_ge(sv.h, v2 + (c - 1))
                pa, pbk = pb[(c % 2) * 2], pb[(c % 2) * 2 + 1]
                s.activation(obuf[c % 2][:, 0:512], pa[:], AF.Copy)
                ss.inc(s.activation(obuf[c % 2][:, 512:1024], pbk[:], AF.Copy))
                ss.inc(s.nop())
        block.scalar(f42_s)

        def f42_v(v):
            for c in range(TC):
                v.wait_ge(ss.h, s2 + (c + 1) * 2)
                v.wait_ge(ld.h, ldx2 + (c + 1) * 16)
                v.drain()
                sv.inc(v.scalar_tensor_tensor(xc[c % 2][:], obuf[c % 2][:], 0.5,
                                              xc[c % 2][:], ALU.mult, ALU.add))
        block.vector(f42_v)

        self.barrier()
        ph2.close()
        nc.leave_named_scope("f42", scope_f42[0] if isinstance(scope_f42, tuple) else scope_f42, False)

    def _emit_attn(self):
        nc, block = self.nc, self.block
        ld, sv, ss, st_, sst, sg = (self.sems[k] for k in ["ld", "v", "s", "t", "st", "g"])
        pb = self.pb

        phh = ExitStack()
        h_tm = phh.enter_context(nc.sbuf_tensor(self.name("h_tm"), [128, TC * D], FP16))
        hT = phh.enter_context(nc.sbuf_tensor(self.name("hT"), [128, 8 * T], FP16))
        ph0 = ExitStack()
        xsb = ph0.enter_context(nc.sbuf_tensor(self.name("atx"), [128, TC * D], F32))
        self._load_x(xsb)
        self._emit_ln(xsb, h_tm)
        n_ln = sv.n

        def a0_stg(sync):
            sync.wait_ge(sv.h, n_ln)
            sst.inc(sync.dma_start(
                out=self.stg_td16.rearrange("(c p) d -> p c d", p=128),
                in_=h_tm[:].rearrange("p (c d) -> p c d", c=TC)), 16)
            sync.wait_ge(sst.h, sst.n)
            for k in range(8):
                sst.inc(sync.dma_start_transpose(
                    hT[:, k * T:(k + 1) * T], self.stg_td16[:, k * 128:(k + 1) * 128]), 16)
        block.sync(a0_stg)
        n_hT = sst.n
        self.barrier()
        ph0.close()

        ph1 = ExitStack()
        sbl = lambda nm, shape, dt: ph1.enter_context(nc.sbuf_tensor(self.name(nm), shape, dt))
        Gb = sbl("Gb", [128, 8 * D], FP16)
        qTb = sbl("qTb", [128, 8 * D], FP16)
        qfm = sbl("qfm", [128, 8 * T], FP16)
        obuf = [sbl("ato", [128, D], F32) for _ in range(2)]
        xc = [sbl("atxc", [128, D], F32) for _ in range(2)]

        g1 = sg.n

        def a_qw(g):
            sg.inc(g.dma_start(
                out=qTb[:].rearrange("p (k d) -> p k d", k=8),
                in_=self.qT_in.rearrange("(k p) d -> p k d", p=128)), 16)
        block.gpsimd(a_qw)

        t1 = st_.n
        s1 = ss.n

        def a1_t(t):
            for d1 in range(8):
                for n2 in range(2):
                    bk = pb[(d1 * 2 + n2) % 8]
                    if d1 * 2 + n2 >= 8:
                        t.wait_ge(ss.h, s1 + (d1 * 2 + n2) - 8 + 1)
                    for tk in range(TC):
                        i = t.matmul(bk[:], h_tm[:, tk * D + d1 * 128:tk * D + (d1 + 1) * 128],
                                     h_tm[:, tk * D + n2 * 512:tk * D + (n2 + 1) * 512],
                                     start=(tk == 0), stop=(tk == TC - 1))
                    st_.inc(i)
            t.wait_ge(sg.h, g1 + 16)
            t.wait_ge(sst.h, n_hT)
            t.wait_ge(ss.h, s1 + 16)
            for e in range(8):
                for tt in range(4):
                    bk = pb[(e % 2) * 4 + tt]
                    if e >= 2:
                        t.wait_ge(ss.h, s1 + 16 + (e - 1) * 4)
                    for k in range(8):
                        i = t.matmul(bk[:], qTb[:, k * D + e * 128:k * D + (e + 1) * 128],
                                     hT[:, k * T + tt * 512:k * T + (tt + 1) * 512],
                                     start=(k == 0), stop=(k == 7))
                    st_.inc(i)
        block.tensor(a1_t)

        def a1_s(s):
            for d1 in range(8):
                for n2 in range(2):
                    bk = pb[(d1 * 2 + n2) % 8]
                    s.wait_ge(st_.h, t1 + d1 * 2 + n2 + 1)
                    ss.inc(s.activation(Gb[:, d1 * D + n2 * 512:d1 * D + (n2 + 1) * 512],
                                        bk[:], AF.Copy))
            for e in range(8):
                for tt in range(4):
                    bk = pb[(e % 2) * 4 + tt]
                    s.wait_ge(st_.h, t1 + 16 + e * 4 + tt + 1)
                    ss.inc(s.activation(qfm[:, e * T + tt * 512:e * T + (tt + 1) * 512],
                                        bk[:], AF.Copy))
        block.scalar(a1_s)
        n_g_q = ss.n
        self.barrier()

        t3 = st_.n
        s3 = ss.n
        v3 = sv.n
        ldx3 = ld.n
        stx3 = sst.n

        def a3_xld(sync, c):
            sync.wait_ge(ld.h, ldx3 + c * 16)
            if c >= 2:
                sync.wait_ge(sst.h, stx3 + (c - 1) * 16)
            ld.inc(sync.dma_start(
                out=xc[c % 2][:],
                in_=self.xd.rearrange("(c p) d -> p c d", p=128)[:, c, :]), 16)

        def a3_xst(sync, c):
            sync.wait_ge(sst.h, stx3 + c * 16)
            sync.wait_ge(sv.h, v3 + c + 1)
            sst.inc(sync.dma_start(
                out=self.xd.rearrange("(c p) d -> p c d", p=128)[:, c, :],
                in_=xc[c % 2][:]), 16)

        for c in range(TC):
            if c >= 2:
                block.sync(lambda sync, c=c: a3_xst(sync, c - 2))
            block.sync(lambda sync, c=c: a3_xld(sync, c))
        block.sync(lambda sync: a3_xst(sync, TC - 2))
        block.sync(lambda sync: a3_xst(sync, TC - 1))

        def a3_t(t):
            for c in range(TC):
                if c >= 2:
                    t.wait_ge(ss.h, s3 + (c - 1) * 2)
                pa, pbk = pb[(c % 2) * 2], pb[(c % 2) * 2 + 1]
                for e in range(8):
                    i1 = t.matmul(pa[:], qfm[:, e * T + c * 128:e * T + (c + 1) * 128],
                                  Gb[:, e * D:e * D + 512], start=(e == 0), stop=(e == 7))
                st_.inc(i1)
                for e in range(8):
                    i2 = t.matmul(pbk[:], qfm[:, e * T + c * 128:e * T + (c + 1) * 128],
                                  Gb[:, e * D + 512:(e + 1) * D], start=(e == 0), stop=(e == 7))
                st_.inc(i2)
        block.tensor(a3_t)

        def a3_s(s):
            for c in range(TC):
                s.wait_ge(st_.h, t3 + (c + 1) * 2)
                if c >= 2:
                    s.wait_ge(sv.h, v3 + (c - 1))
                pa, pbk = pb[(c % 2) * 2], pb[(c % 2) * 2 + 1]
                s.activation(obuf[c % 2][:, 0:512], pa[:], AF.Copy)
                ss.inc(s.activation(obuf[c % 2][:, 512:1024], pbk[:], AF.Copy))
                ss.inc(s.nop())
        block.scalar(a3_s)

        def a3_v(v):
            for c in range(TC):
                v.wait_ge(ss.h, s3 + (c + 1) * 2)
                v.wait_ge(ld.h, ldx3 + (c + 1) * 16)
                v.drain()
                sv.inc(v.tensor_tensor(xc[c % 2][:], xc[c % 2][:], obuf[c % 2][:], ALU.add))
        block.vector(a3_v)

        self.barrier()
        ph1.close()
        phh.close()

    def _emit_fin(self):
        nc, block = self.nc, self.block
        ld, sfin, sst = self.sems["ld"], self.sems["fin"], self.sems["st"]
        ph = ExitStack()
        fbuf = ph.enter_context(nc.sbuf_tensor("fbuf", [128, TC * D], F32))

        def fin_sync(sync):
            ld.inc(sync.dma_start(
                out=fbuf[:].rearrange("p (c d) -> p c d", c=TC),
                in_=self.xd.rearrange("(c p) d -> p c d", p=128)), 16)
            sync.wait_ge(ld.h, ld.n)
            sfin.inc(sync.dma_start(
                out=self.out.rearrange("(c p) d -> p c d", p=128),
                in_=fbuf[:].rearrange("p (c d) -> p c d", c=TC)), 16)
            sync.wait_ge(sfin.h, sfin.n)
        block.sync(fin_sync)
        ph.close()


def prep_core_inputs(xb, W):
    f = np.ascontiguousarray
    cvec = lambda a, nch: f(np.asarray(a, np.float32).reshape(nch, 128).T)
    consts = np.zeros((128, 8), np.float32)
    consts[:, C_EPS] = 1e-5
    consts[:, C_I1024] = 1.0 / 1024
    consts[:, C_HALF] = 0.5
    consts[:, C_QUART] = 0.25
    consts[:, C_ONE] = 1.0
    return dict(
        x=f(np.asarray(xb, np.float32)),
        winT=f(np.asarray(W["L0_attn_in_proj_w"], np.float32).T),
        inb=cvec(W["L0_attn_in_proj_b"], 32),
        convw=f(np.asarray(W["L0_attn_conv_w"], np.float32).reshape(16, 128, 4)
                .transpose(1, 0, 2).reshape(128, 64)),
        convb=cvec(W["L0_attn_conv_b"], 16),
        dtwT=f(np.asarray(W["L0_attn_dt_w"], np.float32).T),
        bwT=f(np.asarray(W["L0_attn_b_w"], np.float32).T),
        cwT=f(np.asarray(W["L0_attn_c_w"], np.float32).T),
        dtb=cvec(W["L0_attn_dt_b"], 16),
        alog=cvec(W["L0_attn_A_log"], 16),
        dpar=cvec(W["L0_attn_D_param"], 16),
        outwT=f(np.asarray(W["L0_attn_out_w"], np.float32).T),
        outb_rep=f(np.broadcast_to(np.asarray(W["L0_attn_out_b"], np.float32), (128, D))),
        kern=f(np.asarray(W["L0_ffn_kernel"])),
        sin_c=f(np.broadcast_to(np.asarray(W["L0_ffn_input_scale"], np.float32).reshape(1, 1),
                                (128, 1))),
        skl_c=f(np.broadcast_to(np.asarray(W["L0_ffn_kernel_scale"], np.float32).reshape(1, 1),
                                (128, 1))),
        fbias_rep=f(np.broadcast_to(np.asarray(W["L0_ffn_bias"]), (128, D))),
        qT=f(np.asarray(W["L1_attn_queries"], np.float32).T),
        wg=f(np.asarray(W["L1_ffn_w_gate"], np.float32)),
        wu=f(np.asarray(W["L1_ffn_w_up"], np.float32)),
        wd=f(np.asarray(W["L1_ffn_w_down"], np.float32)),
        consts=consts,
        ident=np.eye(128, dtype=np.float16),
    )


def run(inputs, n_stages=6, n_cores=8):
    nc = Builder(n_stages).build()
    x = np.asarray(inputs["L0_attn_x"])
    in_maps = [prep_core_inputs(x[b], inputs) for b in range(n_cores)]
    res = run_bass_kernel_spmd(nc, in_maps, core_ids=list(range(n_cores)))
    return np.stack([r["out"] for r in res.results], axis=0)






def kernel(**inputs):
    nc = Builder(6).build()
    x = np.asarray(inputs["L0_attn_x"])
    in_maps = [prep_core_inputs(x[b], inputs) for b in range(8)]
    res = run_bass_kernel_spmd(nc, in_maps, core_ids=list(range(8)))
    return np.stack([np.asarray(r["out"]) for r in res.results], axis=0)
